# version 02
# baseline (speedup 1.0000x reference)
import numpy as np
import ml_dtypes
import concourse.bass as bass
import concourse.mybir as mybir
from concourse.bass_utils import run_bass_kernel_spmd
from contextlib import ExitStack

F32 = mybir.dt.float32
BF16 = mybir.dt.bfloat16
AF = mybir.ActivationFunctionType
ALU = mybir.AluOpType
BF = ml_dtypes.bfloat16

NCORE = 8
NB, SEQ, DM = 2, 8192, 1024
SL = SEQ // NCORE
TL = NB * SL
EPS_LN = 1e-5


class Prog:
    COMPUTE = ("tensor", "vector", "scalar", "gpsimd")
    ALL = COMPUTE + ("sync",)
    NDMA = 16


    def __init__(self, nc, es):
        self.nc = nc
        self.es = es
        self.steps = []
        self.bufs = {}
        self.last_dma_on_sem = {}

    def _dma_sem_id(self, key):
        return hash(key) % self.NDMA

    def step(self, eng, emit, reads=(), writes=(), dma=False, name=""):
        raw = set()
        war = set()
        for b in reads:
            st = self.bufs.get(b)
            if st and st["w"] is not None:
                raw.add(st["w"])
        for b in writes:
            st = self.bufs.get(b)
            if st:
                if st["w"] is not None:
                    raw.add(st["w"])
                war.update(st["r"])
        deps = raw | war
        sid = len(self.steps)
        sem_id = None
        if dma:
            key = writes[0] if writes else name
            sem_id = self._dma_sem_id(key)
            prev = self.last_dma_on_sem.get(sem_id)
            if prev is not None:
                deps.add(prev)
            self.last_dma_on_sem[sem_id] = sid
        self.steps.append(dict(eng=eng, emit=emit, deps=deps, dma=dma,
                               sem_id=sem_id, name=name))
        for b in reads:
            self.bufs.setdefault(b, {"w": None, "r": []})["r"].append(sid)
        for b in writes:
            self.bufs[b] = {"w": sid, "r": []}
        return sid

    def load(self, dst_ap, src_ap, reads=(), writes=(), q="sync", name="ld"):
        return self.step(q, lambda e: e.dma_start(out=dst_ap, in_=src_ap),
                         reads=reads, writes=writes, dma=True, name=name)

    def finalize(self, block):
        nc = self.nc
        counts = {}
        for st in self.steps:
            if st["emit"] is None:
                sem = None
            elif st["dma"]:
                sem = f"d{st['sem_id']}"
            else:
                sem = f"e_{st['eng']}"
            if sem is not None:
                counts[sem] = counts.get(sem, 0) + (16 if st["dma"] else 1)
            st["sem"] = sem
            st["val"] = counts.get(sem, 0)
        sem_es = ExitStack()
        sems = {s: sem_es.enter_context(nc.semaphore(s)) for s in counts}
        by_eng = {e: [s for s in self.steps if s["eng"] == e] for e in self.ALL}
        progself = self

        def runner(eng_name):
            def run(eng):
                waited = {}
                for st in by_eng[eng_name]:
                    for d in sorted(st["deps"]):
                        ds = progself.steps[d]
                        if ds["sem"] is None:
                            raise RuntimeError(f"dep on barrier step {ds['name']}")
                        v = ds["val"]
                        if waited.get(ds["sem"], 0) >= v:
                            continue
                        eng.wait_ge(sems[ds["sem"]], v)
                        waited[ds["sem"]] = v
                    if st["emit"] is None:
                        continue
                    insts = st["emit"](eng)
                    if not isinstance(insts, (list, tuple)):
                        insts = [insts]
                    if st["dma"]:
                        if len(insts) != 1:
                            raise RuntimeError("dma steps must emit exactly 1 dma")
                        insts[0].then_inc(sems[st["sem"]], 16)
                    else:
                        insts[-1].then_inc(sems[st["sem"]], 1)
            return run

        for e in self.ALL:
            if not by_eng[e]:
                continue
            getattr(block, e)(runner(e))
        return sem_es


def mkprog(build_fn, in_specs, out_specs):
    nc = bass.Bass("TRN2", target_bir_lowering=False, debug=False)
    ins = {k: nc.dram_tensor(k, list(v[0]), v[1], kind="ExternalInput").ap()
           for k, v in in_specs.items()}
    outs = {k: nc.dram_tensor(k, list(v[0]), v[1], kind="ExternalOutput").ap()
            for k, v in out_specs.items()}
    es = ExitStack()
    with es:
        with nc.Block() as block:
            P = Prog(nc, es)
            build_fn(nc, P, ins, outs, es)
            sem_es = P.finalize(block)
            es.enter_context(sem_es)
    return nc


def sbuf(nc, es, name, shape, dt=F32):
    return es.enter_context(nc.sbuf_tensor(name, list(shape), dt))


def psum(nc, es, name, shape, dt=F32):
    return es.enter_context(nc.psum_tensor(name, list(shape), dt))


def build_A(nt):
    def bld(nc, P, ins, outs, es):
        blocks = ins["blocks"]
        obl = outs["tiles"]
        idt = sbuf(nc, es, "idt", [128, 128], BF16)
        P.load(idt[:], ins["idty"], writes=[("idt",)])
        nslot = 4
        raw = [sbuf(nc, es, f"raw{i}", [128, 128], F32) for i in range(nslot)]
        cst = [sbuf(nc, es, f"cst{i}", [128, 128], BF16) for i in range(nslot)]
        out = [sbuf(nc, es, f"out{i}", [128, 128], BF16) for i in range(nslot)]
        pst = [psum(nc, es, f"pst{i}", [128, 128], BF16) for i in range(nslot)]
        for i in range(nt):
            s = i % nslot
            P.load(raw[s][:], blocks[i], writes=[("raw", s)], name=f"ldw{i}")
            P.step("scalar", lambda e, s=s: e.copy(out=cst[s][:], in_=raw[s][:]),
                   reads=[("raw", s)], writes=[("cst", s)], name=f"cast{i}")
            P.step("tensor", lambda e, s=s: e.transpose(pst[s][:], cst[s][:], idt[:]),
                   reads=[("cst", s), ("idt",)], writes=[("pst", s)], name=f"tr{i}")
            P.step("vector", lambda e, s=s: e.tensor_copy(out=out[s][:], in_=pst[s][:]),
                   reads=[("pst", s)], writes=[("out", s)], name=f"ev{i}")
            P.step("gpsimd", lambda e, s=s, i=i: e.dma_start(out=obl[i], in_=out[s][:]),
                   reads=[("out", s)], writes=[("st", i)], dma=True, name=f"st{i}")
        last = [("st", i) for i in range(nt)]
        P.step("sync", None, reads=last, name="fin")
    return bld


def _toks(a):
    out = []
    for c in range(NCORE):
        out.append(np.ascontiguousarray(
            np.concatenate([a[0, c * SL:(c + 1) * SL], a[1, c * SL:(c + 1) * SL]], 0)))
    return out


def _untoks(slices, D=DM, dtype=np.float32):
    a = np.empty((NB, SEQ, D), dtype)
    for c in range(NCORE):
        a[0, c * SL:(c + 1) * SL] = slices[c][:SL]
        a[1, c * SL:(c + 1) * SL] = slices[c][SL:]
    return a


def _tileblocks(w):
    M, K = w.shape
    assert M % 128 == 0 and K % 128 == 0
    b = w.reshape(M // 128, 128, K // 128, 128).transpose(0, 2, 1, 3)
    return np.ascontiguousarray(b.reshape(-1, 128, 128).astype(np.float32)), (M // 128, K // 128)


def _assemble_T(tiles, mo, ko):
    t = tiles.reshape(mo, ko, 128, 128)
    t = t.transpose(0, 2, 1, 3)
    return np.ascontiguousarray(t.reshape(mo, 128, ko * 128))


def _colscale(s):
    M = s.shape[0]
    return np.ascontiguousarray(s.reshape(M // 128, 128).T.astype(np.float32))


IDTY = np.eye(128, dtype=np.float32)
IDTY_BF = np.eye(128, dtype=BF)
ONES_COL = np.ones((128, 1), np.float32)
ONES_ROW = np.ones((1, 128), np.float32)


def run_A(inputs):
    specs = [
        ("w1t", inputs["L0_ffn_w1_w"]),
        ("w2t", inputs["L0_ffn_w2_w"]),
        ("h1t", inputs["L0_ffn_h1_w"]),
        ("h2t", inputs["L0_ffn_h2_w"]),
        ("w3t", inputs["L0_ffn_w3_w"]),
        ("wint", inputs["L1_attn_W_in"]),
        ("woutt", inputs["L1_attn_W_out"]),
    ]
    blocks = []
    meta = []
    for name, w in specs:
        b, (mo, ko) = _tileblocks(w)
        meta.append((name, mo, ko, len(b)))
        blocks.append(b)
    allb = np.concatenate(blocks, 0)
    ntot = len(allb)
    per = (ntot + NCORE - 1) // NCORE
    pad = per * NCORE - ntot
    if pad:
        allb = np.concatenate([allb, np.zeros((pad, 128, 128), np.float32)], 0)
    nc = mkprog(build_A(per),
                {"blocks": ((per, 128, 128), F32), "idty": ((128, 128), BF16)},
                {"tiles": ((per, 128, 128), BF16)})
    in_maps = [{"blocks": allb[c * per:(c + 1) * per], "idty": IDTY_BF}
               for c in range(NCORE)]
    res = run_bass_kernel_spmd(nc, in_maps, core_ids=list(range(NCORE)))
    tiles = np.concatenate([res.results[c]["tiles"] for c in range(NCORE)], 0)[:ntot]
    out = {}
    off = 0
    for name, mo, ko, n in meta:
        out[name] = _assemble_T(tiles[off:off + n], mo, ko)
        off += n
    return out



def build_B(variant="B"):
    TKC = 512
    NCH = TL // TKC

    def bld(nc, P, ins, outs, es):
        sc = {}
        sc["ones_col"] = sbuf(nc, es, "s_ones_col", [128, 1], F32)
        sc["ones_row"] = sbuf(nc, es, "s_ones_row", [1, 128], F32)
        idtb = sbuf(nc, es, "idtb", [128, 128], BF16)
        P.load(sc["ones_col"][:], ins["ones_col"], writes=[("ones_col",)])
        P.load(sc["ones_row"][:], ins["ones_row"], writes=[("ones_row",)])
        P.load(idtb[:], ins["idty"], writes=[("idtb",)])
        scl_list = [("w1s", [128, 32]), ("w2s", [128, 32]), ("h1s", [128, 32]),
                    ("h2s", [128, 32]), ("w3s", [128, 8])]
        scl_list += [("bdt", [128, 8])] if variant == "B" else [("gam", [128, 8])]
        for nm, shape in scl_list:
            sc[nm] = sbuf(nc, es, "s_" + nm, shape, F32)
            P.load(sc[nm][:], ins[nm], writes=[(nm,)])
        pb = [psum(nc, es, f"pb{i}", [128, 512], F32) for i in range(7)]
        XR = [sbuf(nc, es, f"XR{i}", [128, TL], F32) for i in range(8)]
        xn1 = [sbuf(nc, es, f"xn1_{i}", [128, TL], BF16) for i in range(8)]
        g = [sbuf(nc, es, f"g{i}", [128, TKC], BF16) for i in range(32)]
        g2 = [sbuf(nc, es, f"g2_{i}", [128, TKC], BF16) for i in range(32)]
        wsl = [sbuf(nc, es, f"wsl{i}", [128, 4096], BF16) for i in range(2)]
        if variant == "B":
            for i in range(8):
                P.load(XR[i][:], ins["x_T"].rearrange("(a p) t -> a p t", p=128)[i],
                       writes=[("XR", i)], name=f"ldx{i}")
        else:
            yt = [sbuf(nc, es, f"yt{i}", [128, TL], F32) for i in range(2)]
            for i in range(8):
                P.load(XR[i][:], ins["x1_T"].rearrange("(a p) t -> a p t", p=128)[i],
                       writes=[("XR", i)], name=f"ldx{i}")
                P.load(yt[i % 2][:], ins["y_T"].rearrange("(a p) t -> a p t", p=128)[i],
                       writes=[("yt", i % 2)], name=f"ldy{i}")
                P.step("vector", lambda e, i=i: e.tensor_add(XR[i][:], XR[i][:], yt[i % 2][:]),
                       reads=[("yt", i % 2), ("XR", i)], writes=[("XR", i)], name=f"addy{i}")
        sc["sq"] = sbuf(nc, es, "sq", [128, 512], F32)
        sc["ceps"] = sbuf(nc, es, "ceps", [128, 1], F32)
        P.step("gpsimd", lambda e: e.memset(sc["ceps"][:], EPS_LN), writes=[("ceps",)], name="ceps")
        for nm in ("mrow", "wrow", "rrow", "mbrow"):
            sc[nm] = sbuf(nc, es, nm, [1, 512], F32)
        sc["ap1"] = sbuf(nc, es, "ap1", [128, 512], F32)
        sc["sa"] = sbuf(nc, es, "sa", [128, 512], BF16)

        def ln_pass(pref, dst_put):
            for ch in range(NCH):
                cs = slice(ch * TKC, (ch + 1) * TKC)
                for dt in range(8):
                    P.step("scalar", lambda e, dt=dt, cs=cs: e.square(out=sc["sq"][:], in_=XR[dt][:, cs]),
                           reads=[("XR", dt)], writes=[("sq",)], name=f"{pref}sq{ch}_{dt}")
                    P.step("tensor", lambda e, dt=dt, cs=cs: e.matmul(
                              pb[0][0:1, :], sc["ones_col"][:], XR[dt][:, cs],
                              start=(dt == 0), stop=(dt == 7)),
                           reads=[("XR", dt), ("ones_col",)], writes=[("pbank", 0)],
                           name=f"{pref}sx{ch}_{dt}")
                    P.step("tensor", lambda e, dt=dt: e.matmul(
                              pb[1][0:1, :], sc["ones_col"][:], sc["sq"][:],
                              start=(dt == 0), stop=(dt == 7)),
                           reads=[("sq",), ("ones_col",)], writes=[("pbank", 1)],
                           name=f"{pref}sq2{ch}_{dt}")
                P.step("vector", lambda e: e.tensor_scalar_mul(sc["mrow"][:], pb[0][0:1, :], 1.0 / 1024),
                       reads=[("pbank", 0)], writes=[("mrow",)], name=f"{pref}m{ch}")
                P.step("vector", lambda e: e.tensor_scalar_mul(sc["wrow"][:], pb[1][0:1, :], 1.0 / 1024),
                       reads=[("pbank", 1)], writes=[("wrow",)], name=f"{pref}e{ch}")
                P.step("vector", lambda e: e.tensor_mul(sc["mbrow"][:], sc["mrow"][:], sc["mrow"][:]),
                       reads=[("mrow",)], writes=[("mbrow",)], name=f"{pref}m2{ch}")
                P.step("vector", lambda e: e.tensor_sub(sc["wrow"][:], sc["wrow"][:], sc["mbrow"][:]),
                       reads=[("mbrow",), ("wrow",)], writes=[("wrow",)], name=f"{pref}v{ch}")
                P.step("scalar", lambda e: e.activation(out=sc["wrow"][:], in_=sc["wrow"][:],
                                                        func=AF.Ln, bias=sc["ceps"][0:1, :]),
                       reads=[("wrow",), ("ceps",)], writes=[("wrow",)], name=f"{pref}l{ch}")
                P.step("scalar", lambda e: e.activation(out=sc["rrow"][:], in_=sc["wrow"][:],
                                                        func=AF.Exp, scale=-0.5),
                       reads=[("wrow",)], writes=[("rrow",)], name=f"{pref}r{ch}")
                P.step("vector", lambda e: e.scalar_tensor_tensor(
                          out=sc["mbrow"][:], in0=sc["mrow"][:], scalar=-1.0, in1=sc["rrow"][:],
                          op0=ALU.mult, op1=ALU.mult),
                       reads=[("mrow",), ("rrow",)], writes=[("mbrow",)], name=f"{pref}mb{ch}")
                P.step("tensor", lambda e: e.matmul(pb[2][:], sc["ones_row"][:], sc["rrow"][:],
                                                    start=True, stop=True),
                       reads=[("rrow",), ("ones_row",)], writes=[("pbank", 2)], name=f"{pref}bcr{ch}")
                P.step("tensor", lambda e: e.matmul(pb[3][:], sc["ones_row"][:], sc["mbrow"][:],
                                                    start=True, stop=True),
                       reads=[("mbrow",), ("ones_row",)], writes=[("pbank", 3)], name=f"{pref}bcm{ch}")
                for dt in range(8):
                    P.step("vector", lambda e, dt=dt, cs=cs: e.tensor_mul(
                              sc["ap1"][:], XR[dt][:, cs], pb[2][:]),
                           reads=[("XR", dt), ("pbank", 2)], writes=[("ap1",)],
                           name=f"{pref}ap1_{ch}_{dt}")
                    dst_put(dt, ch)

        def put_xn1(dt, ch):
            cs = slice(ch * TKC, (ch + 1) * TKC)
            P.step("vector", lambda e, dt=dt, cs=cs: e.tensor_add(
                      xn1[dt][:, cs], sc["ap1"][:], pb[3][:]),
                   reads=[("ap1",), ("pbank", 3)], writes=[("xn1", dt, cs.start)],
                   name=f"xn1_{ch}_{dt}")
        ln_pass("ln1", put_xn1)

        def ffn_layer(pref, ch, srcs, src_keys, wname, nko, mos, evac):
            cs = slice(ch * TKC, (ch + 1) * TKC)
            for mi, mo in enumerate(mos):
                slot = mi % len(wsl)
                bank = mi % 2
                P.load(wsl[slot][:, :nko * 128], ins[wname].rearrange("m p f -> m p f")[mo],
                       writes=[("wsl", slot)], q="sync", name=f"{pref}w{ch}_{mo}")
                def mmgroup(e, slot=slot, bank=bank, cs=cs, nko=nko, srcs=srcs):
                    insts = []
                    for ko in range(nko):
                        insts.append(e.matmul(
                            pb[bank][:], wsl[slot][:, ko * 128:(ko + 1) * 128],
                            srcs[ko][:, cs], start=(ko == 0), stop=(ko == nko - 1)))
                    return insts[-1]
                P.step("tensor", mmgroup,
                       reads=[("wsl", slot)] + [src_keys(ko, cs.start) for ko in range(nko)],
                       writes=[("pbank", bank)], name=f"{pref}mm{ch}_{mo}")
                evac(mo, ("pbank", bank), pb[bank])

        for ch in range(NCH):
            cs = slice(ch * TKC, (ch + 1) * TKC)
            for mi in range(32):
                slot = 0
                slot2 = 1
                bank_a, bank_b = (0, 1) if mi % 2 == 0 else (2, 3)
                P.load(wsl[slot][:, :1024], ins["w1t"][mi], writes=[("wsl", slot)],
                       name=f"fw1_{ch}_{mi}")
                def mma(e, slot=slot, bank=bank_a, cs=cs):
                    for ko in range(8):
                        i = e.matmul(pb[bank][:], wsl[slot][:, ko * 128:(ko + 1) * 128],
                                     xn1[ko][:, cs], start=(ko == 0), stop=(ko == 7))
                    return i
                P.step("tensor", mma,
                       reads=[("wsl", slot)] + [("xn1", ko, cs.start) for ko in range(8)],
                       writes=[("pbank", bank_a)], name=f"fma_{ch}_{mi}")
                P.step("scalar", lambda e, bank=bank_a, mi=mi: e.activation(
                          out=sc["sa"][:], in_=pb[bank][:], func=AF.Silu,
                          scale=sc["w1s"][:, mi:mi + 1]),
                       reads=[("pbank", bank_a), ("w1s",)], writes=[("sa",)],
                       name=f"fsa_{ch}_{mi}")
                P.load(wsl[slot2][:, :1024], ins["w2t"][mi], writes=[("wsl", slot2)],
                       name=f"fw2_{ch}_{mi}")
                def mmb(e, slot=slot2, bank=bank_b, cs=cs):
                    for ko in range(8):
                        i = e.matmul(pb[bank][:], wsl[slot][:, ko * 128:(ko + 1) * 128],
                                     xn1[ko][:, cs], start=(ko == 0), stop=(ko == 7))
                    return i
                P.step("tensor", mmb,
                       reads=[("wsl", slot2)] + [("xn1", ko, cs.start) for ko in range(8)],
                       writes=[("pbank", bank_b)], name=f"fmb_{ch}_{mi}")
                P.step("vector", lambda e, mi=mi, bank=bank_b: e.scalar_tensor_tensor(
                          out=g[mi][:], in0=pb[bank][:], scalar=sc["w2s"][:, mi:mi + 1],
                          in1=sc["sa"][:], op0=ALU.mult, op1=ALU.mult),
                       reads=[("pbank", bank_b), ("w2s",), ("sa",)], writes=[("g", mi)],
                       name=f"fg_{ch}_{mi}")
            for mi in range(32):
                slot = mi % 2
                bank = 4 + (mi % 2)
                P.load(wsl[slot][:], ins["h1t"][mi], writes=[("wsl", slot)],
                       name=f"fh1_{ch}_{mi}")
                def mmh(e, slot=slot, bank=bank):
                    for ko in range(32):
                        i = e.matmul(pb[bank][:], wsl[slot][:, ko * 128:(ko + 1) * 128],
                                     g[ko][:], start=(ko == 0), stop=(ko == 31))
                    return i
                P.step("tensor", mmh,
                       reads=[("wsl", slot)] + [("g", ko) for ko in range(32)],
                       writes=[("pbank", bank)], name=f"fmh1_{ch}_{mi}")
                P.step("scalar", lambda e, bank=bank, mi=mi: e.activation(
                          out=g2[mi][:], in_=pb[bank][:], func=AF.Gelu,
                          scale=sc["h1s"][:, mi:mi + 1]),
                       reads=[("pbank", bank), ("h1s",)], writes=[("g2", mi)],
                       name=f"fg2_{ch}_{mi}")
            for mi in range(32):
                slot = mi % 2
                bank = 4 + (mi % 2)
                P.load(wsl[slot][:], ins["h2t"][mi], writes=[("wsl", slot)],
                       name=f"fh2_{ch}_{mi}")
                def mmh2(e, slot=slot, bank=bank):
                    for ko in range(32):
                        i = e.matmul(pb[bank][:], wsl[slot][:, ko * 128:(ko + 1) * 128],
                                     g2[ko][:], start=(ko == 0), stop=(ko == 31))
                    return i
                P.step("tensor", mmh2,
                       reads=[("wsl", slot)] + [("g2", ko) for ko in range(32)],
                       writes=[("pbank", bank)], name=f"fmh2_{ch}_{mi}")
                P.step("scalar", lambda e, bank=bank, mi=mi: e.activation(
                          out=g[mi][:], in_=pb[bank][:], func=AF.Gelu,
                          scale=sc["h2s"][:, mi:mi + 1]),
                       reads=[("pbank", bank), ("h2s",)], writes=[("g", mi)],
                       name=f"fg3_{ch}_{mi}")
            for mi in range(8):
                slot = mi % 2
                bank = 6
                P.load(wsl[slot][:], ins["w3t"][mi], writes=[("wsl", slot)],
                       name=f"fw3_{ch}_{mi}")
                def mmo(e, slot=slot, bank=bank):
                    for ko in range(32):
                        i = e.matmul(pb[bank][:], wsl[slot][:, ko * 128:(ko + 1) * 128],
                                     g[ko][:], start=(ko == 0), stop=(ko == 31))
                    return i
                P.step("tensor", mmo,
                       reads=[("wsl", slot)] + [("g", ko) for ko in range(32)],
                       writes=[("pbank", bank)], name=f"fmo_{ch}_{mi}")
                P.step("scalar", lambda e, bank=bank, mi=mi: e.activation(
                          out=sc["ap1"][:], in_=pb[bank][:], func=AF.Copy,
                          scale=sc["w3s"][:, mi:mi + 1]),
                       reads=[("pbank", bank), ("w3s",)], writes=[("ap1",)],
                       name=f"fto_{ch}_{mi}")
                P.step("vector", lambda e, mi=mi, cs=cs: e.scalar_tensor_tensor(
                          out=XR[mi][:, cs], in0=sc["ap1"][:], scalar=0.5,
                          in1=XR[mi][:, cs], op0=ALU.mult, op1=ALU.add),
                       reads=[("ap1",), ("XR", mi)], writes=[("XR", mi)],
                       name=f"fxr_{ch}_{mi}")

        if variant == "D":
            def put_ln4(dt, ch):
                cs = slice(ch * TKC, (ch + 1) * TKC)
                P.step("vector", lambda e, dt=dt, cs=cs: e.tensor_add(
                          xn1[dt][:, cs], sc["ap1"][:], pb[3][:]),
                       reads=[("ap1",), ("pbank", 3)], writes=[("xn1", dt, cs.start)],
                       name=f"ln4_{ch}_{dt}")
            ln_pass("ln4", put_ln4)
            r32 = sbuf(nc, es, "r32", [1, 128], F32)
            P.step("gpsimd", lambda e: e.memset(r32[:], float(np.sqrt(DM))),
                   writes=[("r32",)], name="c32")
            for ch in range(NCH):
                cs = slice(ch * TKC, (ch + 1) * TKC)
                for dt in range(8):
                    P.step("scalar", lambda e, dt=dt, cs=cs: e.square(
                              out=sc["sq"][:], in_=xn1[dt][:, cs]),
                           reads=[("xn1", dt, cs.start)], writes=[("sq",)],
                           name=f"qsq{ch}_{dt}")
                    P.step("tensor", lambda e, dt=dt: e.matmul(
                              pb[0][0:1, :], sc["ones_col"][:], sc["sq"][:],
                              start=(dt == 0), stop=(dt == 7)),
                           reads=[("sq",), ("ones_col",)], writes=[("pbank", 0)],
                           name=f"qs{ch}_{dt}")
                P.step("scalar", lambda e: e.activation(out=sc["wrow"][:], in_=pb[0][0:1, :],
                                                        func=AF.Ln),
                       reads=[("pbank", 0)], writes=[("wrow",)], name=f"qln{ch}")
                P.step("scalar", lambda e: e.activation(out=sc["rrow"][:], in_=sc["wrow"][:],
                                                        func=AF.Exp, scale=-0.5),
                       reads=[("wrow",)], writes=[("rrow",)], name=f"qr{ch}")
                P.step("tensor", lambda e: e.matmul(pb[2][:], r32[:], sc["rrow"][:],
                                                    start=True, stop=True),
                       reads=[("rrow",), ("r32",)], writes=[("pbank", 2)], name=f"qbc{ch}")
                for dt in range(8):
                    P.step("vector", lambda e, dt=dt, cs=cs: e.scalar_tensor_tensor(
                              out=sc["sa"][:], in0=xn1[dt][:, cs],
                              scalar=sc["gam"][:, dt:dt + 1], in1=pb[2][:],
                              op0=ALU.mult, op1=ALU.mult),
                           reads=[("xn1", dt, cs.start), ("gam",), ("pbank", 2)],
                           writes=[("sa",)], name=f"qm{ch}_{dt}")
                    P.step("gpsimd", lambda e, dt=dt, ch=ch: e.dma_start(
                              out=outs["xq_T"][dt * 128:(dt + 1) * 128,
                                               ch * TKC:(ch + 1) * TKC],
                              in_=sc["sa"][:]),
                           reads=[("sa",)], writes=[("st_xq", dt, ch)], dma=True,
                           name=f"stxq{ch}_{dt}")
            for i in range(8):
                P.step("gpsimd", lambda e, i=i: e.dma_start(
                          out=outs["x3"][i * 128:(i + 1) * 128, :], in_=XR[i][:]),
                       reads=[("XR", i)], writes=[("st_x1", i)], dma=True, name=f"stx3_{i}")
            fin_reads = ([("st_x1", i) for i in range(8)] +
                         [("st_xq", dt, ch) for dt in range(8) for ch in range(NCH)])
            P.step("sync", None, reads=fin_reads, name="fin")
            return

        def put_xn2(dt, ch):
            cs = slice(ch * TKC, (ch + 1) * TKC)
            P.step("vector", lambda e, dt=dt: e.tensor_add(
                      sc["sq"][:], sc["ap1"][:], pb[3][:]),
                   reads=[("ap1",), ("pbank", 3)], writes=[("sq",)], name=f"xn2f_{ch}_{dt}")
            P.step("scalar", lambda e, dt=dt, cs=cs: e.copy(out=xn1[dt][:, cs], in_=sc["sq"][:]),
                   reads=[("sq",)], writes=[("xn1", dt, cs.start)], name=f"xn2b_{ch}_{dt}")
            P.step("gpsimd", lambda e, dt=dt, ch=ch: e.dma_start(
                      out=outs["xn2"][dt * 128:(dt + 1) * 128, ch * TKC:(ch + 1) * TKC],
                      in_=sc["sq"][:]),
                   reads=[("sq",)], writes=[("st_xn2", dt, ch)], dma=True,
                   name=f"stxn2_{ch}_{dt}")
        ln_pass("ln2", put_xn2)

        wxpb = sbuf(nc, es, "wxpb", [96, 1024], BF16)
        P.load(wxpb[:], ins["wxp"], writes=[("wxpb",)], q="gpsimd", name="ldwxp")
        xpjT = sbuf(nc, es, "xpjT", [128, 8 * 96], BF16)
        pT = psum(nc, es, "pT", [128, 128], BF16)
        for ko in range(8):
            P.step("tensor", lambda e, ko=ko: e.transpose(
                      pT[:, 0:96], wxpb[:, ko * 128:(ko + 1) * 128], idtb[0:96, 0:96]),
                   reads=[("wxpb",), ("idtb",)], writes=[("pT",)], name=f"trxp{ko}")
            P.step("scalar", lambda e, ko=ko: e.copy(out=xpjT[:, ko * 96:(ko + 1) * 96],
                                                     in_=pT[:, 0:96]),
                   reads=[("pT",)], writes=[("xpjT", ko)], name=f"evxp{ko}")
        xpT = sbuf(nc, es, "xpT", [96, 512], F32)
        xpdt = sbuf(nc, es, "xpdt", [64, 512], BF16)
        zrow = sbuf(nc, es, "zrow", [16, 512], BF16)
        P.step("gpsimd", lambda e: e.memset(zrow[:], 0.0), writes=[("zrow",)], name="z0")
        wdtb = sbuf(nc, es, "wdtb", [128, 8 * 64], BF16)
        P.load(wdtb[:].rearrange("p (a r) -> p a r", a=8),
               ins["wdt"].rearrange("(a p) r -> p a r", p=128),
               writes=[("wdtb",)], q="gpsimd", name="ldwdt")
        wdtT = sbuf(nc, es, "wdtT", [64, 1024], BF16)
        for dt in range(8):
            P.step("tensor", lambda e, dt=dt: e.transpose(
                      pT[0:64, 0:128], wdtb[:, dt * 64:(dt + 1) * 64], idtb[:]),
                   reads=[("wdtb",), ("idtb",)], writes=[("pT",)], name=f"trdt{dt}")
            P.step("scalar", lambda e, dt=dt: e.copy(out=wdtT[0:64, dt * 128:(dt + 1) * 128],
                                                     in_=pT[0:64, 0:128]),
                   reads=[("pT",)], writes=[("wdtT", dt)], name=f"evdt{dt}")
        r1 = sc["ap1"]
        r2 = sc["sq"]
        for ch in range(NCH):
            cs = slice(ch * TKC, (ch + 1) * TKC)
            def mmxp(e, cs=cs):
                for ko in range(8):
                    i = e.matmul(pb[0][0:96, :], xpjT[:, ko * 96:(ko + 1) * 96],
                                 xn1[ko][:, cs], start=(ko == 0), stop=(ko == 7))
                return i
            P.step("tensor", mmxp,
                   reads=[("xpjT", ko) for ko in range(8)] +
                         [("xn1", ko, cs.start) for ko in range(8)],
                   writes=[("pbank", 0)], name=f"mmxp{ch}")
            P.step("scalar", lambda e: e.copy(out=xpT[:], in_=pb[0][0:96, :]),
                   reads=[("pbank", 0)], writes=[("xpT",)], name=f"evxpT{ch}")
            P.step("scalar", lambda e: e.copy(out=xpdt[:], in_=xpT[0:64, :]),
                   reads=[("xpT",)], writes=[("xpdt",)], name=f"cxpdt{ch}")
            P.step("gpsimd", lambda e, cs=cs: e.dma_start(
                      out=outs["bc0"][0:16, cs], in_=xpT[64:80, :]),
                   reads=[("xpT",)], writes=[("st_b", cs.start)], dma=True, name=f"stB{ch}")
            P.step("gpsimd", lambda e, cs=cs: e.dma_start(
                      out=outs["bc0"][32:48, cs], in_=xpT[80:96, :]),
                   reads=[("xpT",)], writes=[("st_c", cs.start)], dma=True, name=f"stC{ch}")
            P.step("gpsimd", lambda e, cs=cs: e.dma_start(
                      out=outs["bc0"][16:32, cs], in_=zrow[:]),
                   reads=[("zrow",)], writes=[("st_z", cs.start)], dma=True, name=f"stZ{ch}")
            for mo in range(8):
                P.step("tensor", lambda e, mo=mo: e.matmul(
                          pb[1][:], wdtT[:, mo * 128:(mo + 1) * 128], xpdt[:],
                          start=True, stop=True),
                       reads=[("wdtT", mo), ("xpdt",)], writes=[("pbank", 1)],
                       name=f"mmdt{mo}_{ch}")
                P.step("scalar", lambda e, mo=mo: e.activation(
                          out=r1[:], in_=pb[1][:], func=AF.Relu, bias=sc["bdt"][:, mo:mo + 1]),
                       reads=[("pbank", 1), ("bdt",)], writes=[("ap1",)], name=f"d1_{mo}_{ch}")
                P.step("scalar", lambda e, mo=mo: e.activation(
                          out=r2[:], in_=pb[1][:], func=AF.Abs, bias=sc["bdt"][:, mo:mo + 1]),
                       reads=[("pbank", 1), ("bdt",)], writes=[("sq",)], name=f"d2_{mo}_{ch}")
                P.step("scalar", lambda e: e.activation(out=r2[:], in_=r2[:], func=AF.Exp, scale=-1.0),
                       reads=[("sq",)], writes=[("sq",)], name=f"d3_{mo}_{ch}")
                P.step("scalar", lambda e: e.activation(out=r2[:], in_=r2[:], func=AF.Ln, bias=1.0),
                       reads=[("sq",)], writes=[("sq",)], name=f"d4_{mo}_{ch}")
                P.step("vector", lambda e: e.tensor_add(r2[:], r2[:], r1[:]),
                       reads=[("sq",), ("ap1",)], writes=[("sq",)], name=f"d5_{mo}_{ch}")
                P.step("gpsimd", lambda e, mo=mo, ch=ch: e.dma_start(
                          out=outs["delta0"][mo * 128:(mo + 1) * 128, ch * TKC:(ch + 1) * TKC],
                          in_=r2[:]),
                       reads=[("sq",)], writes=[("st_d", mo, ch)], dma=True,
                       name=f"std{mo}_{ch}")


        for i in range(8):
            P.step("gpsimd", lambda e, i=i: e.dma_start(
                      out=outs["x1"][i * 128:(i + 1) * 128, :], in_=XR[i][:]),
                   reads=[("XR", i)], writes=[("st_x1", i)], dma=True, name=f"stx1_{i}")

        fin_reads = ([("st_x1", i) for i in range(8)] + [("st_b", k * 512) for k in range(4)] + [("st_c", k * 512) for k in range(4)] + [("st_z", k * 512) for k in range(4)] +
                     [("st_xn2", dt, ch) for dt in range(8) for ch in range(NCH)] +
                     [("st_d", mo, ch) for mo in range(8) for ch in range(NCH)])
        P.step("sync", None, reads=fin_reads, name="fin")
    return bld


def run_B(inputs, prep):
    in_specs = {
        "x_T": ((DM, TL), F32),
        "w1t": ((32, 128, 1024), BF16), "w2t": ((32, 128, 1024), BF16),
        "h1t": ((32, 128, 4096), BF16), "h2t": ((32, 128, 4096), BF16),
        "w3t": ((8, 128, 4096), BF16),
        "w1s": ((128, 32), F32), "w2s": ((128, 32), F32),
        "h1s": ((128, 32), F32), "h2s": ((128, 32), F32), "w3s": ((128, 8), F32),
        "wxp": ((96, 1024), F32), "wdt": ((1024, 64), F32), "bdt": ((128, 8), F32),
        "idty": ((128, 128), BF16),
        "ones_col": ((128, 1), F32), "ones_row": ((1, 128), F32),
    }
    out_specs = {
        "x1": ((DM, TL), F32), "xn2": ((DM, TL), F32),
        "delta0": ((DM, TL), F32), "bc0": ((48, TL), F32),
    }
    nc = mkprog(build_B("B"), in_specs, out_specs)
    xs = _toks(inputs["L0_attn_x"])
    shared = {
        "w1t": prep["w1t"], "w2t": prep["w2t"], "h1t": prep["h1t"],
        "h2t": prep["h2t"], "w3t": prep["w3t"],
        "w1s": _colscale(inputs["L0_ffn_w1_s"]), "w2s": _colscale(inputs["L0_ffn_w2_s"]),
        "h1s": _colscale(inputs["L0_ffn_h1_s"]), "h2s": _colscale(inputs["L0_ffn_h2_s"]),
        "w3s": _colscale(inputs["L0_ffn_w3_s"]),
        "wxp": inputs["L0_attn_W_xproj"].astype(np.float32),
        "wdt": inputs["L0_attn_W_dt"].astype(np.float32),
        "bdt": _colscale(inputs["L0_attn_b_dt"]),
        "idty": IDTY_BF, "ones_col": ONES_COL, "ones_row": ONES_ROW,
    }
    in_maps = [dict(shared, x_T=np.ascontiguousarray(xs[c].T)) for c in range(NCORE)]
    res = run_bass_kernel_spmd(nc, in_maps, core_ids=list(range(NCORE)))
    return [res.results[c] for c in range(NCORE)]



def run_D(inputs, prep, b_outs, y_tok):
    in_specs = {
        "x1_T": ((DM, TL), F32), "y_T": ((DM, TL), F32),
        "w1t": ((32, 128, 1024), BF16), "w2t": ((32, 128, 1024), BF16),
        "h1t": ((32, 128, 4096), BF16), "h2t": ((32, 128, 4096), BF16),
        "w3t": ((8, 128, 4096), BF16),
        "w1s": ((128, 32), F32), "w2s": ((128, 32), F32),
        "h1s": ((128, 32), F32), "h2s": ((128, 32), F32), "w3s": ((128, 8), F32),
        "gam": ((128, 8), F32),
        "idty": ((128, 128), BF16),
        "ones_col": ((128, 1), F32), "ones_row": ((1, 128), F32),
    }
    out_specs = {"x3": ((DM, TL), F32), "xq_T": ((DM, TL), BF16)}
    nc = mkprog(build_B("D"), in_specs, out_specs)
    shared = {
        "w1t": prep["w1t"], "w2t": prep["w2t"], "h1t": prep["h1t"],
        "h2t": prep["h2t"], "w3t": prep["w3t"],
        "w1s": _colscale(inputs["L0_ffn_w1_s"]), "w2s": _colscale(inputs["L0_ffn_w2_s"]),
        "h1s": _colscale(inputs["L0_ffn_h1_s"]), "h2s": _colscale(inputs["L0_ffn_h2_s"]),
        "w3s": _colscale(inputs["L0_ffn_w3_s"]),
        "gam": _colscale(inputs["L1_ffn_norm_gamma"]),
        "idty": IDTY_BF, "ones_col": ONES_COL, "ones_row": ONES_ROW,
    }
    in_maps = [dict(shared, x1_T=b_outs[c]["x1"], y_T=y_tok[c]) for c in range(NCORE)]
    res = run_bass_kernel_spmd(nc, in_maps, core_ids=list(range(NCORE)))
    return [res.results[c] for c in range(NCORE)]


def build_C():
    TC = 512
    NOC = (NB * SEQ) // TC

    def bld(nc, P, ins, outs, es):
        sel = sbuf(nc, es, "s_sel", [48, 16 * 128], F32)
        P.load(sel[:], ins["sel16"], writes=[("sel",)])
        alog = sbuf(nc, es, "s_alog", [128, 16], F32)
        P.load(alog[:], ins["alog"], writes=[("alog",)])
        dsl = sbuf(nc, es, "s_dsl", [128, 1], F32)
        P.load(dsl[:], ins["dsl"], writes=[("dsl",)])
        A = sbuf(nc, es, "A", [128, 16], F32)
        P.step("scalar", lambda e: e.activation(out=A[:], in_=alog[:], func=AF.Exp),
               reads=[("alog",)], writes=[("A",)], name="expA")
        P.step("vector", lambda e: e.tensor_scalar_mul(A[:], A[:], -1.0),
               reads=[("A",)], writes=[("A",)], name="negA")
        carry = sbuf(nc, es, "carry", [128, 16], F32)
        zcol = sbuf(nc, es, "zcol", [128, 1], F32)
        P.step("gpsimd", lambda e: e.memset(zcol[:], 0.0), writes=[("zcol",)], name="z")
        uc = [sbuf(nc, es, f"uc{i}", [128, TC], F32) for i in range(2)]
        dc = [sbuf(nc, es, f"dc{i}", [128, TC], F32) for i in range(2)]
        bcc = [sbuf(nc, es, f"bcc{i}", [48, TC], F32) for i in range(2)]
        wc = sbuf(nc, es, "wc", [128, TC], F32)
        dac = sbuf(nc, es, "dac", [128, TC], F32)
        bbc = sbuf(nc, es, "bbc", [128, TC], F32)
        hc = sbuf(nc, es, "hc", [128, TC], F32)
        ymc = sbuf(nc, es, "ymc", [128, TC], F32)
        yac = sbuf(nc, es, "yac", [128, TC], F32)
        pb = [psum(nc, es, f"pb{i}", [128, TC], F32) for i in range(2)]

        for oc in range(NOC):
            s = oc % 2
            cs = slice(oc * TC, (oc + 1) * TC)
            P.load(uc[s][:], ins["u0"][:, cs], writes=[("uc", s)], name=f"ldu{oc}")
            P.load(dc[s][:], ins["d0"][:, cs], writes=[("dc", s)], name=f"ldd{oc}")
            P.load(bcc[s][:], ins["bc"][:, cs], writes=[("bcc", s)], name=f"ldbc{oc}")
            P.step("vector", lambda e, s=s: e.tensor_mul(wc[:], dc[s][:], uc[s][:]),
                   reads=[("dc", s), ("uc", s)], writes=[("wc",)], name=f"w{oc}")
            for n in range(16):
                P.step("tensor", lambda e, s=s, n=n: e.matmul(
                          pb[0][:], sel[0:16, n * 128:(n + 1) * 128], bcc[s][0:16, :],
                          start=True, stop=True),
                       reads=[("sel",), ("bcc", s)], writes=[("pbank", 0)],
                       name=f"bb{oc}_{n}")
                P.step("tensor", lambda e, s=s, n=n: e.matmul(
                          pb[1][:], sel[32:48, n * 128:(n + 1) * 128], bcc[s][32:48, :],
                          start=True, stop=True),
                       reads=[("sel",), ("bcc", s)], writes=[("pbank", 1)],
                       name=f"cb{oc}_{n}")
                P.step("scalar", lambda e, s=s, n=n: e.activation(
                          out=dac[:], in_=dc[s][:], func=AF.Exp, scale=A[:, n:n + 1]),
                       reads=[("dc", s), ("A",)], writes=[("dac",)], name=f"da{oc}_{n}")
                P.step("vector", lambda e: e.scalar_tensor_tensor(
                          out=bbc[:], in0=wc[:], scalar=1.0, in1=pb[0][:],
                          op0=ALU.mult, op1=ALU.mult),
                       reads=[("wc",), ("pbank", 0)], writes=[("bbc",)], name=f"bB{oc}_{n}")
                init = zcol[:] if oc % (NOC // 2) == 0 else carry[:, n:n + 1]
                ikey = ("zcol",) if oc % (NOC // 2) == 0 else ("carry", n)
                P.step("vector", lambda e, init=init: e.tensor_tensor_scan(
                          out=hc[:], data0=dac[:], data1=bbc[:], initial=init,
                          op0=ALU.mult, op1=ALU.add),
                       reads=[("dac",), ("bbc",), ikey], writes=[("hc",)],
                       name=f"h{oc}_{n}")
                P.step("vector", lambda e, n=n: e.tensor_copy(
                          out=carry[:, n:n + 1], in_=hc[:, TC - 1:TC]),
                       reads=[("hc",)], writes=[("carry", n)], name=f"cy{oc}_{n}")
                if n == 0:
                    P.step("vector", lambda e: e.scalar_tensor_tensor(
                              out=yac[:], in0=hc[:], scalar=1.0, in1=pb[1][:],
                              op0=ALU.mult, op1=ALU.mult),
                           reads=[("hc",), ("pbank", 1)], writes=[("yac",)],
                           name=f"y{oc}_{n}")
                else:
                    P.step("vector", lambda e: e.scalar_tensor_tensor(
                              out=ymc[:], in0=hc[:], scalar=1.0, in1=pb[1][:],
                              op0=ALU.mult, op1=ALU.mult),
                           reads=[("hc",), ("pbank", 1)], writes=[("ymc",)],
                           name=f"ym{oc}_{n}")
                    P.step("vector", lambda e: e.tensor_add(yac[:], yac[:], ymc[:]),
                           reads=[("ymc",), ("yac",)], writes=[("yac",)],
                           name=f"y{oc}_{n}")
            P.step("vector", lambda e, s=s: e.tensor_scalar_mul(ymc[:], uc[s][:], dsl[:, 0:1]),
                   reads=[("uc", s), ("dsl",)], writes=[("ymc",)], name=f"ud{oc}")
            P.step("vector", lambda e: e.tensor_add(yac[:], yac[:], ymc[:]),
                   reads=[("ymc",), ("yac",)], writes=[("yac",)], name=f"yd{oc}")
            P.step("gpsimd", lambda e, cs=cs: e.dma_start(out=outs["y0"][:, cs], in_=yac[:]),
                   reads=[("yac",)], writes=[("st", oc % 4)], dma=True, name=f"st{oc}")
        P.step("sync", None, reads=[("st", i) for i in range(4)], name="fin")
    return bld


SEL16 = np.zeros((48, 16 * 128), np.float32)
for _n in range(16):
    SEL16[_n, _n * 128:(_n + 1) * 128] = 1.0
    SEL16[32 + _n, _n * 128:(_n + 1) * 128] = 1.0


def run_C(b_outs, inputs):
    TT_ = NB * SEQ
    in_specs = {
        "u0": ((128, TT_), F32), "d0": ((128, TT_), F32), "bc": ((48, TT_), F32),
        "alog": ((128, 16), F32), "dsl": ((128, 1), F32),
        "sel16": ((48, 16 * 128), F32),
    }
    out_specs = {"y0": ((128, TT_), F32)}
    nc = mkprog(build_C(), in_specs, out_specs)
    ufull = np.concatenate([o["xn2"][:, :SL] for o in b_outs] +
                           [o["xn2"][:, SL:] for o in b_outs], axis=1)
    dfull = np.concatenate([o["delta0"][:, :SL] for o in b_outs] +
                           [o["delta0"][:, SL:] for o in b_outs], axis=1)
    bcfull = np.concatenate([o["bc0"][:, :SL] for o in b_outs] +
                            [o["bc0"][:, SL:] for o in b_outs], axis=1)
    alog = inputs["L0_attn_A_log"].astype(np.float32)
    dvec = inputs["L0_attn_D"].astype(np.float32)
    in_maps = []
    for c in range(NCORE):
        rs = slice(c * 128, (c + 1) * 128)
        in_maps.append({
            "u0": np.ascontiguousarray(ufull[rs]),
            "d0": np.ascontiguousarray(dfull[rs]),
            "bc": bcfull, "alog": alog[rs], "dsl": dvec[rs, None],
            "sel16": SEL16,
        })
    res = run_bass_kernel_spmd(nc, in_maps, core_ids=list(range(NCORE)))
    yfull = np.concatenate([res.results[c]["y0"] for c in range(NCORE)], axis=0)
    y_tok = []
    for c in range(NCORE):
        y_tok.append(np.ascontiguousarray(np.concatenate(
            [yfull[:, c * SL:(c + 1) * SL],
             yfull[:, SEQ + c * SL:SEQ + (c + 1) * SL]], axis=1)))
    return y_tok



def build_E():
    TC = 512
    NCHT = (NB * SEQ) // TC
    CPB = NCHT // NB

    def bld(nc, P, ins, outs, es):
        idtb = sbuf(nc, es, "s_idtb", [128, 128], BF16)
        idtf = sbuf(nc, es, "s_idtf", [128, 128], F32)
        P.load(idtb[:], ins["idty"], writes=[("idtb",)])
        P.load(idtf[:], ins["idtf"], writes=[("idtf",)])
        sgam = sbuf(nc, es, "s_sgam", [128, 8], F32)
        P.load(sgam[:], ins["sgam"], writes=[("sgam",)])
        b1s = sbuf(nc, es, "s_b1s", [128, 32], F32)
        P.load(b1s[:], ins["b1s"], writes=[("b1s",)])
        b2s = sbuf(nc, es, "s_b2s", [128, 8], F32)
        P.load(b2s[:], ins["b2s"], writes=[("b2s",)])
        pba = [psum(nc, es, f"pba{i}", [128, 512], F32) for i in range(6)]
        pbt = [psum(nc, es, f"pbt{i}", [128, 128], BF16) for i in range(2)]
        lg = nc.dram_tensor("lg", [512, NB * SEQ], F32, kind="Internal").ap()

        snT = [sbuf(nc, es, f"snT{i}", [128, 512], BF16) for i in range(8)]
        se = sbuf(nc, es, "se", [128, 1024], F32)
        seb = sbuf(nc, es, "seb", [128, 1024], BF16)
        srow = sbuf(nc, es, "srow", [128, 1], F32)
        ssq = sbuf(nc, es, "ssq", [128, 1024], F32)
        for st in range(4):
            P.load(se[:], ins["semb"].rearrange("(a p) d -> a p d", p=128)[st],
                   writes=[("se",)], name=f"ldse{st}")
            P.step("scalar", lambda e: e.square(out=ssq[:], in_=se[:]),
                   reads=[("se",)], writes=[("ssq",)], name=f"ssq{st}")
            P.step("vector", lambda e: e.reduce_sum(srow[:], ssq[:], axis=mybir.AxisListType.X),
                   reads=[("ssq",)], writes=[("srow",)], name=f"snrm{st}")
            P.step("scalar", lambda e: e.activation(out=srow[:], in_=srow[:], func=AF.Ln),
                   reads=[("srow",)], writes=[("srow",)], name=f"snl{st}")
            P.step("scalar", lambda e: e.activation(out=srow[:], in_=srow[:], func=AF.Exp, scale=-0.5),
                   reads=[("srow",)], writes=[("srow",)], name=f"sne{st}")
            P.step("vector", lambda e: e.tensor_scalar_mul(srow[:], srow[:], float(np.sqrt(DM))),
                   reads=[("srow",)], writes=[("srow",)], name=f"sn32{st}")
            P.step("vector", lambda e: e.tensor_scalar_mul(seb[:], se[:], srow[:, 0:1]),
                   reads=[("se",), ("srow",)], writes=[("seb",)], name=f"snn{st}")
            for dt in range(8):
                P.step("tensor", lambda e, dt=dt: e.transpose(
                          pbt[0][:], seb[:, dt * 128:(dt + 1) * 128], idtb[:]),
                       reads=[("seb",), ("idtb",)], writes=[("pbt", 0)], name=f"snt{st}_{dt}")
                P.step("scalar", lambda e, dt=dt, st=st: e.activation(
                          out=snT[dt][:, st * 128:(st + 1) * 128], in_=pbt[0][:],
                          func=AF.Copy, scale=sgam[:, dt:dt + 1]),
                       reads=[("pbt", 0), ("sgam",)], writes=[("snT", dt, st)],
                       name=f"sne{st}_{dt}")

        xqt = [sbuf(nc, es, f"xqt{i}", [128, 512], BF16) for i in range(8)]
        lgb = sbuf(nc, es, "lgb", [128, 512], F32)
        rmax = sbuf(nc, es, "rmax", [128, 8], F32)
        tcol = sbuf(nc, es, "tcol", [128, 1], F32)
        for ch in range(NCHT):
            b = ch // CPB
            cs = slice(ch * TC, (ch + 1) * TC)
            for ko in range(8):
                P.load(xqt[ko][:], ins["xq_T"].rearrange("(a p) t -> a p t", p=128)[ko, :, cs],
                       writes=[("xqt", ko)], name=f"lqt{ch}_{ko}")
            for et in range(4):
                def mml(e, et=et):
                    for ko in range(8):
                        i = e.matmul(pba[et % 2][:], snT[ko][:, et * 128:(et + 1) * 128],
                                     xqt[ko][:], start=(ko == 0), stop=(ko == 7))
                    return i
                P.step("tensor", mml,
                       reads=[("snT", ko, st) for ko in range(8) for st in range(4)] +
                             [("xqt", ko) for ko in range(8)],
                       writes=[("pbank", et % 2)], name=f"lmm{ch}_{et}")
                P.step("scalar", lambda e, et=et: e.copy(out=lgb[:], in_=pba[et % 2][:]),
                       reads=[("pbank", et % 2)], writes=[("lgb",)], name=f"lev{ch}_{et}")
                P.step("vector", lambda e: e.reduce_max(tcol[:], lgb[:], axis=mybir.AxisListType.X),
                       reads=[("lgb",)], writes=[("tcol",)], name=f"lmx{ch}_{et}")
                col = 2 * et + b
                if ch % CPB == 0:
                    P.step("vector", lambda e, col=col: e.tensor_copy(
                              out=rmax[:, col:col + 1], in_=tcol[:]),
                           reads=[("tcol",)], writes=[("rmax", col)], name=f"lrm{ch}_{et}")
                else:
                    P.step("vector", lambda e, col=col: e.tensor_max(
                              rmax[:, col:col + 1], rmax[:, col:col + 1], tcol[:]),
                           reads=[("tcol",), ("rmax", col)], writes=[("rmax", col)],
                           name=f"lrm{ch}_{et}")
                P.step("gpsimd", lambda e, et=et, cs=cs: e.dma_start(
                          out=lg[et * 128:(et + 1) * 128, cs], in_=lgb[:]),
                       reads=[("lgb",)], writes=[("st_lg", et)], dma=True,
                       name=f"lst{ch}_{et}")

        nrmax = sbuf(nc, es, "nrmax", [128, 8], F32)
        rsum = sbuf(nc, es, "rsum", [128, 8], F32)
        lgc = [sbuf(nc, es, f"lgc{i}", [128, 512], F32) for i in range(4)]
        ec = sbuf(nc, es, "ec", [128, 512], BF16)
        dT = [sbuf(nc, es, f"dT{i}", [128, 512], BF16) for i in range(4)]
        xqn = [sbuf(nc, es, f"xqn{i}", [128, 1024], BF16) for i in range(4)]
        slots = [sbuf(nc, es, f"slots{i}", [128, 512], F32) for i in range(8)]
        sT = [sbuf(nc, es, f"sT{i}", [128, 512], BF16) for i in range(8)]
        hT = [sbuf(nc, es, f"hT{i}", [128, 512], BF16) for i in range(32)]
        eon = [sbuf(nc, es, f"eon{i}", [128, 1024], BF16) for i in range(4)]
        wf = [sbuf(nc, es, f"wf{i}", [128, 4096], F32) for i in range(2)]
        wb = [sbuf(nc, es, f"wb{i}", [128, 4096], BF16) for i in range(2)]
        rrow = sbuf(nc, es, "rrow", [1, 512], F32)
        onesr = sbuf(nc, es, "onesr", [1, 128], F32)
        P.load(onesr[:], ins["ones_row"], writes=[("ones_row",)])
        lgn = [sbuf(nc, es, f"lgn{i}", [128, 512], F32) for i in range(4)]
        cnT = [sbuf(nc, es, f"cnT{i}", [128, 512], BF16) for i in range(4)]
        cn = sbuf(nc, es, "cn", [128, 512], BF16)
        ncb = sbuf(nc, es, "ncb", [128, 512], BF16)
        d2sb = sbuf(nc, es, "d2sb", [128, 128], F32)
        mx2sb = sbuf(nc, es, "mx2sb", [128, 128], F32)

        for b in range(NB):
            P.step("vector", lambda e: e.tensor_scalar_mul(nrmax[:], rmax[:], -1.0),
                   reads=[("rmax", c) for c in range(8)], writes=[("nrmax",)],
                   name=f"nrm{b}")
            for ci in range(CPB):
                ch = b * CPB + ci
                cs = slice(ch * TC, (ch + 1) * TC)
                for ts in range(4):
                    P.load(xqn[ts][:], ins["xq_nat"].rearrange("(c p) d -> c p d", p=128)[4 * ch + ts],
                           writes=[("xqn", ts)], name=f"dqn{ch}_{ts}")
                for et in range(4):
                    P.load(lgc[et][:], lg[et * 128:(et + 1) * 128, cs],
                           reads=[("st_lg", et)], writes=[("lgc", et)], name=f"dlg{ch}_{et}")
                    col = 2 * et + b
                    P.step("scalar", lambda e, et=et, col=col: e.activation(
                              out=ec[:], in_=lgc[et][:], func=AF.Exp,
                              bias=nrmax[:, col:col + 1]),
                           reads=[("lgc", et), ("nrmax",)], writes=[("ec",)],
                           name=f"dex{ch}_{et}")
                    P.step("vector", lambda e: e.reduce_sum(tcol[:], ec[:], axis=mybir.AxisListType.X),
                           reads=[("ec",)], writes=[("tcol",)], name=f"dsm{ch}_{et}")
                    if ci == 0:
                        P.step("vector", lambda e, col=col: e.tensor_copy(
                                  out=rsum[:, col:col + 1], in_=tcol[:]),
                               reads=[("tcol",)], writes=[("rsum", col)], name=f"drs{ch}_{et}")
                    else:
                        P.step("vector", lambda e, col=col: e.tensor_add(
                                  rsum[:, col:col + 1], rsum[:, col:col + 1], tcol[:]),
                               reads=[("tcol",), ("rsum", col)], writes=[("rsum", col)],
                               name=f"drs{ch}_{et}")
                    for ts in range(4):
                        P.step("tensor", lambda e, ts=ts: e.transpose(
                                  pbt[ts % 2][:], ec[:, ts * 128:(ts + 1) * 128], idtb[:]),
                               reads=[("ec",), ("idtb",)], writes=[("pbt", ts % 2)],
                               name=f"dtr{ch}_{et}_{ts}")
                        P.step("scalar", lambda e, ts=ts, et=et: e.copy(
                                  out=dT[ts][:, et * 128:(et + 1) * 128], in_=pbt[ts % 2][:]),
                               reads=[("pbt", ts % 2)], writes=[("dT", ts, et)],
                               name=f"dev{ch}_{et}_{ts}")
                for dt in range(8):
                    bank = 2 + (dt % 2)
                    def mms(e, dt=dt, bank=bank):
                        for ts in range(4):
                            i = e.matmul(pba[bank][:], xqn[ts][:, dt * 128:(dt + 1) * 128],
                                         dT[ts][:], start=(ts == 0), stop=(ts == 3))
                        return i
                    P.step("tensor", mms,
                           reads=[("xqn", ts) for ts in range(4)] +
                                 [("dT", ts, et) for ts in range(4) for et in range(4)],
                           writes=[("pbank", bank)], name=f"dss{ch}_{dt}")
                    if ci == 0:
                        P.step("scalar", lambda e, dt=dt, bank=bank: e.copy(
                                  out=slots[dt][:], in_=pba[bank][:]),
                               reads=[("pbank", bank)], writes=[("slots", dt)],
                               name=f"dsa{ch}_{dt}")
                    else:
                        P.step("vector", lambda e, dt=dt, bank=bank: e.tensor_add(
                                  slots[dt][:], slots[dt][:], pba[bank][:]),
                               reads=[("pbank", bank), ("slots", dt)], writes=[("slots", dt)],
                               name=f"dsa{ch}_{dt}")
            P.step("vector", lambda e: e.reciprocal(nrmax[:], rsum[:]),
                   reads=[("rsum", c) for c in range(8)], writes=[("nrmax",)],
                   name=f"rcp{b}")
            for et in range(4):
                col = 2 * et + b
                P.step("tensor", lambda e, et=et, col=col: e.transpose(
                          pba[4][0:1, 0:128], nrmax[:, col:col + 1], idtf[:]),
                       reads=[("nrmax",), ("idtf",)], writes=[("pbank", 4)],
                       name=f"rtr{b}_{et}")
                P.step("scalar", lambda e, et=et: e.copy(
                          out=rrow[:, et * 128:(et + 1) * 128], in_=pba[4][0:1, 0:128]),
                       reads=[("pbank", 4)], writes=[("rrow", et)], name=f"rev{b}_{et}")
            P.step("tensor", lambda e: e.matmul(pba[5][:], onesr[:], rrow[:],
                                                start=True, stop=True),
                   reads=[("rrow", et) for et in range(4)] + [("ones_row",)],
                   writes=[("pbank", 5)], name=f"rbc{b}")
            for dt in range(8):
                P.step("vector", lambda e, dt=dt: e.tensor_mul(slots[dt][:], slots[dt][:], pba[5][:]),
                       reads=[("pbank", 5), ("slots", dt)], writes=[("slots", dt)],
                       name=f"rnm{b}_{dt}")
                P.step("scalar", lambda e, dt=dt: e.copy(out=sT[dt][:], in_=slots[dt][:]),
                       reads=[("slots", dt)], writes=[("sT", dt)], name=f"rcast{b}_{dt}")
            for hm in range(32):
                s = hm % 2
                P.load(wf[s][:, 0:1024], ins["w1e"][hm], writes=[("wf", s)], name=f"f1l{b}_{hm}")
                P.step("scalar", lambda e, s=s: e.copy(out=wb[s][:, 0:1024], in_=wf[s][:, 0:1024]),
                       reads=[("wf", s)], writes=[("wb", s)], name=f"f1c{b}_{hm}")
                def mmh(e, s=s, hm=hm):
                    for dt in range(8):
                        i = e.matmul(pba[4][:], wb[s][:, dt * 128:(dt + 1) * 128],
                                     sT[dt][:], start=(dt == 0), stop=(dt == 7))
                    return i
                P.step("tensor", mmh,
                       reads=[("wb", s)] + [("sT", dt) for dt in range(8)],
                       writes=[("pbank", 4)], name=f"f1m{b}_{hm}")
                P.step("scalar", lambda e, hm=hm: e.activation(
                          out=hT[hm][:], in_=pba[4][:], func=AF.Gelu,
                          bias=b1s[:, hm:hm + 1]),
                       reads=[("pbank", 4), ("b1s",)], writes=[("hT", hm)],
                       name=f"f1e{b}_{hm}")
            for dm in range(8):
                s = dm % 2
                P.load(wf[s][:], ins["w2e"][dm], writes=[("wf", s)], name=f"f2l{b}_{dm}")
                P.step("scalar", lambda e, s=s: e.copy(out=wb[s][:], in_=wf[s][:]),
                       reads=[("wf", s)], writes=[("wb", s)], name=f"f2c{b}_{dm}")
                def mme(e, s=s):
                    for hm in range(32):
                        i = e.matmul(pba[5][:], wb[s][:, hm * 128:(hm + 1) * 128],
                                     hT[hm][:], start=(hm == 0), stop=(hm == 31))
                    return i
                P.step("tensor", mme,
                       reads=[("wb", s)] + [("hT", hm) for hm in range(32)],
                       writes=[("pbank", 5)], name=f"f2m{b}_{dm}")
                P.step("vector", lambda e, dm=dm: e.tensor_scalar_add(
                          ncb[:], pba[5][:], b2s[:, dm:dm + 1]),
                       reads=[("pbank", 5), ("b2s",)], writes=[("ncb",)], name=f"f2e{b}_{dm}")
                for ts in range(4):
                    P.step("tensor", lambda e, ts=ts: e.transpose(
                              pbt[ts % 2][:], ncb[:, ts * 128:(ts + 1) * 128], idtb[:]),
                           reads=[("ncb",), ("idtb",)], writes=[("pbt", ts % 2)],
                           name=f"f2t{b}_{dm}_{ts}")
                    P.step("scalar", lambda e, ts=ts, dm=dm: e.copy(
                              out=eon[ts][:, dm * 128:(dm + 1) * 128], in_=pbt[ts % 2][:]),
                           reads=[("pbt", ts % 2)], writes=[("eon", ts, dm)],
                           name=f"f2v{b}_{dm}_{ts}")
            for ci in range(CPB):
                ch = b * CPB + ci
                cs = slice(ch * TC, (ch + 1) * TC)
                for et in range(4):
                    P.load(lgc[et][:], lg[et * 128:(et + 1) * 128, cs],
                           reads=[("st_lg", et)], writes=[("lgc", et)], name=f"clg{ch}_{et}")
                for ts in range(4):
                    for et in range(4):
                        P.step("tensor", lambda e, ts=ts, et=et: e.transpose(
                                  pba[4][:, 0:128], lgc[et][:, ts * 128:(ts + 1) * 128], idtf[:]),
                               reads=[("lgc", et), ("idtf",)], writes=[("pbank", 4)],
                               name=f"ctr{ch}_{ts}_{et}")
                        P.step("scalar", lambda e, ts=ts, et=et: e.copy(
                                  out=lgn[ts][:, et * 128:(et + 1) * 128], in_=pba[4][:, 0:128]),
                               reads=[("pbank", 4)], writes=[("lgn", ts, et)],
                               name=f"cev{ch}_{ts}_{et}")
                for ts in range(4):
                    tg = 4 * ch + ts
                    P.step("vector", lambda e, ts=ts: e.reduce_max(
                              tcol[:], lgn[ts][:], axis=mybir.AxisListType.X),
                           reads=[("lgn", ts, et) for et in range(4)], writes=[("tcol",)],
                           name=f"cmx{ch}_{ts}")
                    P.step("vector", lambda e, tg=tg: e.tensor_copy(
                              out=mx2sb[:, tg % 128:tg % 128 + 1], in_=tcol[:]),
                           reads=[("tcol",)], writes=[("mx2", tg)], name=f"cmc{ch}_{ts}")
                    P.step("vector", lambda e: e.tensor_scalar_mul(tcol[:], tcol[:], -1.0),
                           reads=[("tcol",)], writes=[("tcol",)], name=f"cmn{ch}_{ts}")
                    P.step("scalar", lambda e, ts=ts: e.activation(
                              out=cn[:], in_=lgn[ts][:], func=AF.Exp, bias=tcol[:, 0:1]),
                           reads=[("lgn", ts, et) for et in range(4)] + [("tcol",)],
                           writes=[("cn",)], name=f"cex{ch}_{ts}")
                    P.step("vector", lambda e: e.reduce_sum(tcol[:], cn[:], axis=mybir.AxisListType.X),
                           reads=[("cn",)], writes=[("tcol",)], name=f"csm{ch}_{ts}")
                    P.step("vector", lambda e, tg=tg: e.tensor_copy(
                              out=d2sb[:, tg % 128:tg % 128 + 1], in_=tcol[:]),
                           reads=[("tcol",)], writes=[("d2", tg)], name=f"cdc{ch}_{ts}")
                    for et in range(4):
                        P.step("tensor", lambda e, et=et: e.transpose(
                                  pbt[et % 2][:], cn[:, et * 128:(et + 1) * 128], idtb[:]),
                               reads=[("cn",), ("idtb",)], writes=[("pbt", et % 2)],
                               name=f"cct{ch}_{ts}_{et}")
                        P.step("scalar", lambda e, ts=ts, et=et: e.copy(
                                  out=cnT[et][:, ts * 128:(ts + 1) * 128], in_=pbt[et % 2][:]),
                               reads=[("pbt", et % 2)], writes=[("cnT", et, ts)],
                               name=f"ccv{ch}_{ts}_{et}")
                for dm in range(8):
                    bank = 2 + (dm % 2)
                    def mmn(e, dm=dm, bank=bank):
                        for et in range(4):
                            i = e.matmul(pba[bank][:], eon[et][:, dm * 128:(dm + 1) * 128],
                                         cnT[et][:], start=(et == 0), stop=(et == 3))
                        return i
                    P.step("tensor", mmn,
                           reads=[("cnT", et, ts) for et in range(4) for ts in range(4)] +
                                 [("eon", et, dmx) for et in range(4) for dmx in range(8)],
                           writes=[("pbank", bank)], name=f"cnm{ch}_{dm}")
                    P.step("scalar", lambda e, bank=bank: e.copy(out=ncb[:], in_=pba[bank][:]),
                           reads=[("pbank", bank)], writes=[("ncb",)], name=f"cne{ch}_{dm}")
                    P.step("gpsimd", lambda e, dm=dm, cs=cs: e.dma_start(
                              out=outs["nT"][dm * 128:(dm + 1) * 128, cs], in_=ncb[:]),
                           reads=[("ncb",)], writes=[("st_n", dm)], dma=True,
                           name=f"cns{ch}_{dm}")
            P.step("gpsimd", lambda e, b=b: e.dma_start(
                      out=outs["d2"][:, b * 64:(b + 1) * 64], in_=d2sb[:, b * 64:(b + 1) * 64]),
                   reads=[("d2", tg) for tg in range(b * 64, (b + 1) * 64)],
                   writes=[("st_d2", b)], dma=True, name=f"std2{b}")
            P.step("gpsimd", lambda e, b=b: e.dma_start(
                      out=outs["mx2"][:, b * 64:(b + 1) * 64], in_=mx2sb[:, b * 64:(b + 1) * 64]),
                   reads=[("mx2", tg) for tg in range(b * 64, (b + 1) * 64)],
                   writes=[("st_m2", b)], dma=True, name=f"stm2{b}")
        P.step("sync", None,
               reads=[("st_n", dm) for dm in range(8)] +
                     [("st_d2", b) for b in range(NB)] + [("st_m2", b) for b in range(NB)],
               name="fin")
    return bld


def run_E(inputs, xq_T_full, layer_prefix="L1_ffn"):
    TT_ = NB * SEQ
    in_specs = {
        "xq_T": ((DM, TT_), BF16), "xq_nat": ((TT_, DM), BF16),
        "semb": ((512, DM), F32), "sgam": ((128, 8), F32),
        "w1e": ((32, 128, 1024), F32), "w2e": ((8, 128, 4096), F32),
        "b1s": ((128, 32), F32), "b2s": ((128, 8), F32),
        "idty": ((128, 128), BF16), "idtf": ((128, 128), F32),
        "ones_row": ((1, 128), F32),
    }
    out_specs = {"nT": ((DM, TT_), BF16), "d2": ((128, 128), F32),
                 "mx2": ((128, 128), F32)}
    nc = mkprog(build_E(), in_specs, out_specs)
    xq_nat = np.ascontiguousarray(xq_T_full.T)
    w1 = inputs[layer_prefix + "_w1"].astype(np.float32)
    w2 = inputs[layer_prefix + "_w2"].astype(np.float32)
    b1 = inputs[layer_prefix + "_b1"].astype(np.float32)
    b2 = inputs[layer_prefix + "_b2"].astype(np.float32)
    semb = inputs[layer_prefix + "_slot_embeds"].astype(np.float32)
    sgam = _colscale(inputs[layer_prefix + "_slot_norm_gamma"])
    in_maps = []
    for c in range(NCORE):
        in_maps.append({
            "xq_T": xq_T_full, "xq_nat": xq_nat,
            "semb": np.ascontiguousarray(semb[c]), "sgam": sgam,
            "w1e": np.ascontiguousarray(
                w1[c].reshape(8, 128, 32, 128).transpose(2, 1, 0, 3).reshape(32, 128, 1024)),
            "w2e": np.ascontiguousarray(
                w2[c].reshape(32, 128, 8, 128).transpose(2, 1, 0, 3).reshape(8, 128, 4096)),
            "b1s": _colscale(b1[c]), "b2s": _colscale(b2[c]),
            "idty": IDTY_BF, "idtf": IDTY, "ones_row": ONES_ROW,
        })
    res = run_bass_kernel_spmd(nc, in_maps, core_ids=list(range(NCORE)))
    return [res.results[c] for c in range(NCORE)]


def assemble_xq_full(d_outs, key="xq_T"):
    return np.ascontiguousarray(np.concatenate(
        [o[key][:, :SL] for o in d_outs] + [o[key][:, SL:] for o in d_outs], axis=1))



TP = 2176
NTS = TP // 128


def _own_segments(p0, p1):
    segs = []
    for lo, hi, off in ((3, 1027, -3), (1030, 2054, -6)):
        a, b = max(p0, lo), min(p1, hi)
        if a < b:
            segs.append((a, b - a, a + off))
    return segs


def build_FG():
    def bld(nc, P, ins, outs, es):
        idtb = sbuf(nc, es, "s_idtb", [128, 128], BF16)
        idtf = sbuf(nc, es, "s_idtf", [128, 128], F32)
        P.load(idtb[:], ins["idty"], writes=[("idtb",)])
        P.load(idtf[:], ins["idtf"], writes=[("idtf",)])
        cvw = sbuf(nc, es, "s_cvw", [128, 16 * 4], F32)
        P.load(cvw[:].rearrange("p (a k) -> p a k", a=16),
               ins["cvw"].rearrange("(a p) o k -> p a (o k)", p=128), writes=[("cvw",)])
        cvb = sbuf(nc, es, "s_cvb", [128, 16], F32)
        P.load(cvb[:], ins["cvb"], writes=[("cvb",)])
        bdt1 = sbuf(nc, es, "s_bdt1", [128, 16], F32)
        P.load(bdt1[:], ins["bdt1"], writes=[("bdt1",)])
        pba = [psum(nc, es, f"pba{i}", [128, 512], F32) for i in range(6)]
        pbt = [psum(nc, es, f"pbt{i}", [128, 128], BF16) for i in range(2)]

        m2s = sbuf(nc, es, "s_m2s", [128, NTS * 8], F32)
        P.load(m2s[:], ins["m2s"], writes=[("m2s",)])
        d2s = sbuf(nc, es, "s_d2s", [128, NTS * 8], F32)
        P.load(d2s[:], ins["d2s"], writes=[("d2s",)])
        ceps = sbuf(nc, es, "s_ceps", [128, 1], F32)
        P.step("gpsimd", lambda e: e.memset(ceps[:], EPS_LN), writes=[("ceps",)], name="ceps")
        ceps2 = sbuf(nc, es, "s_ceps2", [128, 1], F32)
        P.step("gpsimd", lambda e: e.memset(ceps2[:], float(np.finfo(np.float32).eps)),
               writes=[("ceps2",)], name="ceps2")
        nw = sbuf(nc, es, "s_nw", [128, 8], F32)
        P.load(nw[:], ins["nw"], writes=[("nw",)])
        sd8 = sbuf(nc, es, "sd8", [128, 8], F32)
        m2c = sbuf(nc, es, "m2c", [128, 1], F32)
        x3n = sbuf(nc, es, "s_x3n", [128, 1024], F32)
        ntile = sbuf(nc, es, "ntile", [128, 1024], BF16)
        acc = sbuf(nc, es, "acc", [128, 1024], F32)
        srow8 = sbuf(nc, es, "srow8", [128, 8], F32)
        mcol = sbuf(nc, es, "mcol", [128, 1], F32)
        dcol = sbuf(nc, es, "dcol", [128, 1], F32)
        sqn = sbuf(nc, es, "sqn", [128, 1024], F32)
        xn5n = sbuf(nc, es, "xn5n", [128, 1024], BF16)
        xn5T = [sbuf(nc, es, f"xn5T{i}", [128, TP], BF16) for i in range(8)]
        for ts in range(NTS):
            P.load(x3n[:], ins["x3n"].rearrange("(c p) d -> c p d", p=128)[ts],
                   writes=[("x3n",)], name=f"ldx3{ts}")
            P.step("vector", lambda e, ts=ts: e.reduce_max(
                      mcol[:], m2s[:, ts * 8:(ts + 1) * 8], axis=mybir.AxisListType.X),
                   reads=[("m2s",)], writes=[("mcol",)], name=f"fM{ts}")
            P.step("vector", lambda e: e.tensor_scalar_mul(mcol[:], mcol[:], -1.0),
                   reads=[("mcol",)], writes=[("mcol",)], name=f"fMn{ts}")
            P.step("scalar", lambda e, ts=ts: e.activation(
                      out=srow8[:], in_=m2s[:, ts * 8:(ts + 1) * 8], func=AF.Exp,
                      bias=mcol[:, 0:1]),
                   reads=[("m2s",), ("mcol",)], writes=[("srow8",)], name=f"fS{ts}")
            P.step("vector", lambda e, ts=ts: e.tensor_mul(
                      sd8[:], srow8[:], d2s[:, ts * 8:(ts + 1) * 8]),
                   reads=[("srow8",), ("d2s",)], writes=[("sd8",)], name=f"fSD{ts}")
            P.step("vector", lambda e: e.reduce_sum(dcol[:], sd8[:], axis=mybir.AxisListType.X),
                   reads=[("sd8",)], writes=[("dcol",)], name=f"fden{ts}")
            P.step("vector", lambda e: e.reciprocal(dcol[:], dcol[:]),
                   reads=[("dcol",)], writes=[("dcol",)], name=f"frc{ts}")
            for c in range(NCORE):
                P.load(ntile[:], ins["nTs"].rearrange("e (c p) d -> e c p d", p=128)[c, ts],
                       writes=[("ntile",)], name=f"ldn{ts}_{c}")
                if c == 0:
                    P.step("vector", lambda e, c=c: e.tensor_scalar_mul(
                              acc[:], ntile[:], srow8[:, c:c + 1]),
                           reads=[("ntile",), ("srow8",)], writes=[("acc",)],
                           name=f"fa{ts}_{c}")
                else:
                    P.step("vector", lambda e, c=c: e.scalar_tensor_tensor(
                              out=acc[:], in0=ntile[:], scalar=srow8[:, c:c + 1],
                              in1=acc[:], op0=ALU.mult, op1=ALU.add),
                           reads=[("ntile",), ("srow8",), ("acc",)], writes=[("acc",)],
                           name=f"fa{ts}_{c}")
            P.step("vector", lambda e: e.tensor_scalar(
                      out=acc[:], in0=acc[:], scalar1=dcol[:, 0:1], scalar2=0.5,
                      op0=ALU.mult, op1=ALU.mult),
                   reads=[("acc",), ("dcol",)], writes=[("acc",)], name=f"fmx{ts}")
            P.step("vector", lambda e: e.tensor_add(x3n[:], x3n[:], acc[:]),
                   reads=[("acc",), ("x3n",)], writes=[("x3n",)], name=f"fx4{ts}")
            for si, (ps, ln_, os_) in enumerate(_own_segments(ts * 128, ts * 128 + 128)):
                P.step("gpsimd", lambda e, ps=ps, ln_=ln_, os_=os_, ts=ts: e.dma_start(
                          out=outs["x4"][os_:os_ + ln_, :],
                          in_=x3n[ps - ts * 128:ps - ts * 128 + ln_, :]),
                       reads=[("x3n",)], writes=[("st_x4", ts, si)], dma=True,
                       name=f"sx4{ts}_{si}")
            P.step("vector", lambda e: e.reduce_sum(mcol[:], x3n[:], axis=mybir.AxisListType.X),
                   reads=[("x3n",)], writes=[("mcol",)], name=f"l5s{ts}")
            P.step("vector", lambda e: e.tensor_scalar_mul(mcol[:], mcol[:], 1.0 / 1024),
                   reads=[("mcol",)], writes=[("mcol",)], name=f"l5m{ts}")
            P.step("scalar", lambda e: e.square(out=sqn[:], in_=x3n[:]),
                   reads=[("x3n",)], writes=[("sqn",)], name=f"l5q{ts}")
            P.step("vector", lambda e: e.reduce_sum(dcol[:], sqn[:], axis=mybir.AxisListType.X),
                   reads=[("sqn",)], writes=[("dcol",)], name=f"l5e{ts}")
            P.step("vector", lambda e: e.tensor_scalar_mul(dcol[:], dcol[:], 1.0 / 1024),
                   reads=[("dcol",)], writes=[("dcol",)], name=f"l5e2{ts}")
            P.step("vector", lambda e: e.scalar_tensor_tensor(
                      out=m2c[:], in0=mcol[:], scalar=-1.0, in1=mcol[:],
                      op0=ALU.mult, op1=ALU.mult),
                   reads=[("mcol",)], writes=[("m2c",)], name=f"l5m2{ts}")
            P.step("vector", lambda e: e.tensor_add(dcol[:], dcol[:], m2c[:]),
                   reads=[("m2c",), ("dcol",)], writes=[("dcol",)], name=f"l5v{ts}")
            P.step("scalar", lambda e: e.activation(out=dcol[:], in_=dcol[:], func=AF.Ln,
                                                    bias=ceps[:, 0:1]),
                   reads=[("dcol",), ("ceps",)], writes=[("dcol",)], name=f"l5l{ts}")
            P.step("scalar", lambda e: e.activation(out=dcol[:], in_=dcol[:], func=AF.Exp,
                                                    scale=-0.5),
                   reads=[("dcol",)], writes=[("dcol",)], name=f"l5r{ts}")
            P.step("vector", lambda e: e.tensor_scalar_mul(mcol[:], mcol[:], -1.0),
                   reads=[("mcol",)], writes=[("mcol",)], name=f"l5nm{ts}")
            P.step("vector", lambda e: e.tensor_scalar(
                      out=sqn[:], in0=x3n[:], scalar1=mcol[:, 0:1], scalar2=dcol[:, 0:1],
                      op0=ALU.add, op1=ALU.mult),
                   reads=[("x3n",), ("mcol",), ("dcol",)], writes=[("sqn",)],
                   name=f"l5a{ts}")
            P.step("vector", lambda e: e.tensor_copy(out=xn5n[:], in_=sqn[:]),
                   reads=[("sqn",)], writes=[("xn5n",)], name=f"l5c{ts}")
            for si, (ps, ln_, os_) in enumerate(_own_segments(ts * 128, ts * 128 + 128)):
                P.step("gpsimd", lambda e, ps=ps, ln_=ln_, os_=os_, ts=ts: e.dma_start(
                          out=outs["ln5"][os_:os_ + ln_, :],
                          in_=sqn[ps - ts * 128:ps - ts * 128 + ln_, :]),
                       reads=[("sqn",)], writes=[("st_l5", ts, si)], dma=True,
                       name=f"sl5{ts}_{si}")
            P.step("scalar", lambda e: e.square(out=sqn[:], in_=xn5n[:]),
                   reads=[("xn5n",)], writes=[("sqn",)], name=f"r5q{ts}")
            P.step("vector", lambda e: e.reduce_sum(dcol[:], sqn[:], axis=mybir.AxisListType.X),
                   reads=[("sqn",)], writes=[("dcol",)], name=f"r5s{ts}")
            P.step("vector", lambda e: e.tensor_scalar_mul(dcol[:], dcol[:], 1.0 / 1024),
                   reads=[("dcol",)], writes=[("dcol",)], name=f"r5m{ts}")
            P.step("scalar", lambda e: e.activation(out=dcol[:], in_=dcol[:], func=AF.Ln,
                                                    bias=ceps2[:, 0:1]),
                   reads=[("dcol",), ("ceps2",)], writes=[("dcol",)], name=f"r5l{ts}")
            P.step("scalar", lambda e: e.activation(out=dcol[:], in_=dcol[:], func=AF.Exp,
                                                    scale=-0.5),
                   reads=[("dcol",)], writes=[("dcol",)], name=f"r5r{ts}")
            P.step("vector", lambda e: e.tensor_scalar_mul(xn5n[:], xn5n[:], dcol[:, 0:1]),
                   reads=[("dcol",), ("xn5n",)], writes=[("xn5n",)], name=f"r5a{ts}")
            for dt in range(8):
                P.step("tensor", lambda e, dt=dt: e.transpose(
                          pbt[dt % 2][:], xn5n[:, dt * 128:(dt + 1) * 128], idtb[:]),
                       reads=[("xn5n",), ("idtb",)], writes=[("pbt", dt % 2)],
                       name=f"l5t{ts}_{dt}")
                P.step("scalar", lambda e, dt=dt, ts=ts: e.activation(
                          out=xn5T[dt][:, ts * 128:(ts + 1) * 128], in_=pbt[dt % 2][:],
                          func=AF.Copy, scale=nw[:, dt:dt + 1]),
                       reads=[("pbt", dt % 2), ("nw",)], writes=[("xn5T", dt, ts)],
                       name=f"l5tv{ts}_{dt}")

        xmr = [sbuf(nc, es, f"xmr{i}", [128, TP], BF16) for i in range(2)]
        wsl = [sbuf(nc, es, f"wsl{i}", [128, 1024], BF16) for i in range(2)]
        zb = sbuf(nc, es, "zb", [128, 512], BF16)
        NCH5 = [(0, 512), (512, 512), (1024, 512), (1536, 512), (2048, 128)]
        ubf = [sbuf(nc, es, f"ubf{i}", [128, TL], BF16) for i in range(16)]
        uf = sbuf(nc, es, "uf", [128, TL], F32)
        cacc = sbuf(nc, es, "cacc", [128, TL], F32)
        for mo in range(32):
            s = mo % 2
            P.load(wsl[s][:], ins["wint"][mo], writes=[("wsl", s)], name=f"wi{mo}")
            for ci, (c0, cw) in enumerate(NCH5):
                bank = ci % 2
                def mmw(e, s=s, bank=bank, c0=c0, cw=cw):
                    for ko in range(8):
                        i = e.matmul(pba[bank][:, 0:cw], wsl[s][:, ko * 128:(ko + 1) * 128],
                                     xn5T[ko][:, c0:c0 + cw], start=(ko == 0), stop=(ko == 7))
                    return i
                P.step("tensor", mmw,
                       reads=[("wsl", s)] + [("xn5T", ko, ts) for ko in range(8)
                                             for ts in range(c0 // 128, (c0 + cw) // 128)],
                       writes=[("pbank", bank)], name=f"wm{mo}_{ci}")
                if mo < 16:
                    P.step("scalar", lambda e, mo=mo, bank=bank, c0=c0, cw=cw: e.copy(
                              out=xmr[mo % 2][:, c0:c0 + cw], in_=pba[bank][:, 0:cw]),
                           reads=[("pbank", bank)], writes=[("xmr", mo % 2, c0)],
                           name=f"we{mo}_{ci}")
                else:
                    P.step("scalar", lambda e, bank=bank, cw=cw: e.copy(
                              out=zb[:, 0:cw], in_=pba[bank][:, 0:cw]),
                           reads=[("pbank", bank)], writes=[("zb",)], name=f"ze{mo}_{ci}")
                    for si, (ps, ln_, os_) in enumerate(_own_segments(c0, c0 + cw)):
                        P.step("gpsimd", lambda e, mo=mo, c0=c0, ps=ps, ln_=ln_, os_=os_: e.dma_start(
                                  out=outs["z1"][(mo - 16) * 128:(mo - 15) * 128, os_:os_ + ln_],
                                  in_=zb[:, ps - c0:ps - c0 + ln_]),
                               reads=[("zb",)], writes=[("st_z", mo, c0, si)], dma=True,
                               name=f"zs{mo}_{ci}_{si}")

            if mo < 16:
                kt = mo
                sl = mo % 2
                rk = [("xmr", sl, c0) for c0, _ in NCH5]
                for sb, (pb_, ob_) in enumerate(((0, 0), (1027, 1024))):
                    P.step("vector", lambda e, sl=sl, kt=kt, pb_=pb_, ob_=ob_: e.tensor_scalar_mul(
                              cacc[:, ob_:ob_ + SL], xmr[sl][:, pb_:pb_ + SL],
                              cvw[:, kt * 4:kt * 4 + 1]),
                           reads=rk + [("cvw",)], writes=[("cacc",)], name=f"cv0_{kt}_{sb}")
                    for k in range(1, 4):
                        P.step("vector", lambda e, sl=sl, kt=kt, k=k, pb_=pb_, ob_=ob_: e.scalar_tensor_tensor(
                                  out=cacc[:, ob_:ob_ + SL], in0=xmr[sl][:, pb_ + k:pb_ + k + SL],
                                  scalar=cvw[:, kt * 4 + k:kt * 4 + k + 1], in1=cacc[:, ob_:ob_ + SL],
                                  op0=ALU.mult, op1=ALU.add),
                               reads=[("cacc",), ("cvw",)] + rk,
                               writes=[("cacc",)], name=f"cv{k}_{kt}_{sb}")
                P.step("scalar", lambda e, kt=kt: e.activation(
                          out=uf[:], in_=cacc[:], func=AF.Silu, bias=cvb[:, kt:kt + 1]),
                       reads=[("cacc",), ("cvb",)], writes=[("uf",)], name=f"cs{kt}")
                P.step("scalar", lambda e, kt=kt: e.copy(out=ubf[kt][:], in_=uf[:]),
                       reads=[("uf",)], writes=[("ubf", kt)], name=f"cb{kt}")
                P.step("gpsimd", lambda e, kt=kt: e.dma_start(
                          out=outs["u1"][kt * 128:(kt + 1) * 128, :], in_=uf[:]),
                       reads=[("uf",)], writes=[("st_u", kt)], dma=True, name=f"cu{kt}")

        wxr = sbuf(nc, es, "wxr", [128, 2048], F32)
        wxr2 = sbuf(nc, es, "wxr2", [64, 2048], F32)
        wxb = sbuf(nc, es, "wxb", [128, 2048], BF16)
        wxb2 = sbuf(nc, es, "wxb2", [64, 2048], BF16)
        P.load(wxr[:], ins["wx"][0:128, :], writes=[("wxr",)], name="lwx")
        P.load(wxr2[:], ins["wx"][128:192, :], writes=[("wxr2",)], name="lwx2")
        P.step("scalar", lambda e: e.copy(out=wxb[:], in_=wxr[:]),
               reads=[("wxr",)], writes=[("wxb",)], name="cwx")
        P.step("scalar", lambda e: e.copy(out=wxb2[:], in_=wxr2[:]),
               reads=[("wxr2",)], writes=[("wxb2",)], name="cwx2")
        wxT = [sbuf(nc, es, f"wxT{i}", [128, 192], BF16) for i in range(16)]
        for kt in range(16):
            P.step("tensor", lambda e, kt=kt: e.transpose(
                      pbt[0][:], wxb[:, kt * 128:(kt + 1) * 128], idtb[:]),
                   reads=[("wxb",), ("idtb",)], writes=[("pbt", 0)], name=f"tx{kt}")
            P.step("scalar", lambda e, kt=kt: e.copy(out=wxT[kt][:, 0:128], in_=pbt[0][:]),
                   reads=[("pbt", 0)], writes=[("wxT", kt)], name=f"tv{kt}")
            P.step("tensor", lambda e, kt=kt: e.transpose(
                      pbt[1][:, 0:64], wxb2[:, kt * 128:(kt + 1) * 128], idtb[0:64, 0:64]),
                   reads=[("wxb2",), ("idtb",)], writes=[("pbt", 1)], name=f"tx2{kt}")
            P.step("scalar", lambda e, kt=kt: e.copy(out=wxT[kt][:, 128:192],
                                                     in_=pbt[1][:, 0:64]),
                   reads=[("pbt", 1)], writes=[("wxT", kt)], name=f"tv2{kt}")
        wdtb = sbuf(nc, es, "s_wdtb", [128, 16 * 64], BF16)
        P.load(wdtb[:].rearrange("p (a r) -> p a r", a=16),
               ins["wdt1"].rearrange("(a p) r -> p a r", p=128),
               writes=[("wdtb",)], q="gpsimd", name="ldwdt1")
        wdtT = sbuf(nc, es, "s_wdtT", [64, 2048], BF16)
        for kt in range(16):
            P.step("tensor", lambda e, kt=kt: e.transpose(
                      pbt[0][0:64, 0:128], wdtb[:, kt * 64:(kt + 1) * 64], idtb[:]),
                   reads=[("wdtb",), ("idtb",)], writes=[("pbt", 0)], name=f"td{kt}")
            P.step("scalar", lambda e, kt=kt: e.copy(
                      out=wdtT[0:64, kt * 128:(kt + 1) * 128], in_=pbt[0][0:64, 0:128]),
                   reads=[("pbt", 0)], writes=[("wdtT", kt)], name=f"tdv{kt}")

        xdt = sbuf(nc, es, "xdt", [64, TL], BF16)
        xdf = sbuf(nc, es, "xdf", [128, 512], F32)
        for mt, (mb, mw) in enumerate([(0, 128), (128, 64)]):
            for ci in range(4):
                c0 = ci * 512
                bank = 2 + ci % 2
                def mmx(e, mb=mb, mw=mw, c0=c0, bank=bank):
                    for kt in range(16):
                        i = e.matmul(pba[bank][0:mw, :], wxT[kt][:, mb:mb + mw],
                                     ubf[kt][:, c0:c0 + 512], start=(kt == 0), stop=(kt == 15))
                    return i
                P.step("tensor", mmx,
                       reads=[("wxT", kt) for kt in range(16)] +
                             [("ubf", kt) for kt in range(16)],
                       writes=[("pbank", bank)], name=f"xd{mt}_{ci}")
                P.step("scalar", lambda e, mw=mw, bank=bank: e.copy(
                          out=xdf[0:mw, :], in_=pba[bank][0:mw, :]),
                       reads=[("pbank", bank)], writes=[("xdf",)], name=f"xde{mt}_{ci}")
                if mt == 0:
                    P.step("scalar", lambda e, c0=c0: e.copy(
                              out=xdt[:, c0:c0 + 512], in_=xdf[0:64, :]),
                           reads=[("xdf",)], writes=[("xdt", c0)], name=f"xdc{ci}")
                    P.step("gpsimd", lambda e, c0=c0: e.dma_start(
                              out=outs["bc1"][0:64, c0:c0 + 512], in_=xdf[64:128, :]),
                           reads=[("xdf",)], writes=[("st_bcb", ci)], dma=True,
                           name=f"xsb{ci}")
                else:
                    P.step("gpsimd", lambda e, c0=c0: e.dma_start(
                              out=outs["bc1"][64:128, c0:c0 + 512], in_=xdf[0:64, :]),
                           reads=[("xdf",)], writes=[("st_bcc", ci)], dma=True,
                           name=f"xsc{ci}")

        r1 = sbuf(nc, es, "r1f", [128, 512], F32)
        r2 = sbuf(nc, es, "r2f", [128, 512], F32)
        for mo in range(16):
            for ci in range(4):
                c0 = ci * 512
                P.step("tensor", lambda e, mo=mo, c0=c0: e.matmul(
                          pba[4][:], wdtT[:, mo * 128:(mo + 1) * 128], xdt[:, c0:c0 + 512],
                          start=True, stop=True),
                       reads=[("wdtT", kt) for kt in range(16)] +
                             [("xdt", k * 512) for k in range(4)],
                       writes=[("pbank", 4)], name=f"dm{mo}_{ci}")
                P.step("scalar", lambda e, mo=mo: e.activation(
                          out=r1[:], in_=pba[4][:], func=AF.Relu, bias=bdt1[:, mo:mo + 1]),
                       reads=[("pbank", 4), ("bdt1",)], writes=[("r1f",)], name=f"dd1{mo}_{ci}")
                P.step("scalar", lambda e, mo=mo: e.activation(
                          out=r2[:], in_=pba[4][:], func=AF.Abs, bias=bdt1[:, mo:mo + 1]),
                       reads=[("pbank", 4), ("bdt1",)], writes=[("r2f",)], name=f"dd2{mo}_{ci}")
                P.step("scalar", lambda e: e.activation(out=r2[:], in_=r2[:], func=AF.Exp, scale=-1.0),
                       reads=[("r2f",)], writes=[("r2f",)], name=f"dd3{mo}_{ci}")
                P.step("scalar", lambda e: e.activation(out=r2[:], in_=r2[:], func=AF.Ln, bias=1.0),
                       reads=[("r2f",)], writes=[("r2f",)], name=f"dd4{mo}_{ci}")
                P.step("vector", lambda e: e.tensor_add(r2[:], r2[:], r1[:]),
                       reads=[("r2f",), ("r1f",)], writes=[("r2f",)], name=f"dd5{mo}_{ci}")
                P.step("gpsimd", lambda e, mo=mo, c0=c0: e.dma_start(
                          out=outs["delta1"][mo * 128:(mo + 1) * 128, c0:c0 + 512],
                          in_=r2[:]),
                       reads=[("r2f",)], writes=[("st_dl", mo, c0)], dma=True,
                       name=f"ds{mo}_{ci}")

        fin = ([("st_x4", ts, si) for ts in range(NTS)
                for si in range(len(_own_segments(ts * 128, ts * 128 + 128)))] +
               [("st_l5", ts, si) for ts in range(NTS)
                for si in range(len(_own_segments(ts * 128, ts * 128 + 128)))] +
               [("st_z", mo, c0, si) for mo in range(16, 32) for c0, cw in NCH5
                for si in range(len(_own_segments(c0, c0 + cw)))] +
               [("st_u", kt) for kt in range(16)] +
               [("st_bcb", ci) for ci in range(4)] + [("st_bcc", ci) for ci in range(4)] +
               [("st_dl", mo, k * 512) for mo in range(16) for k in range(4)])
        P.step("sync", None, reads=fin, name="fin")
    return bld


def _padded_rows(arr_nat, c, fill_zero=False, const=None):
    D = arr_nat.shape[1]
    b0 = arr_nat[c * SL:(c + 1) * SL]
    b1 = arr_nat[SEQ + c * SL:SEQ + (c + 1) * SL]
    if c == 0:
        h0 = np.zeros((3, D), arr_nat.dtype) if const is None else np.full((3, D), const, arr_nat.dtype)
        h1 = h0
    else:
        h0 = arr_nat[c * SL - 3:c * SL]
        h1 = arr_nat[SEQ + c * SL - 3:SEQ + c * SL]
    pad = np.repeat(b1[-1:], 122, axis=0)
    return np.ascontiguousarray(np.concatenate([h0, b0, h1, b1, pad], 0))


def run_FG(inputs, prep, d_outs, e_outs):
    in_specs = {
        "x3n": ((TP, DM), F32), "nTs": ((NCORE, TP, DM), BF16),
        "m2s": ((128, NTS * 8), F32), "d2s": ((128, NTS * 8), F32),
        "wint": ((32, 128, 1024), BF16),
        "cvw": ((2048, 1, 4), F32), "cvb": ((128, 16), F32),
        "wx": ((192, 2048), F32), "wdt1": ((2048, 64), F32), "bdt1": ((128, 16), F32),
        "nw": ((128, 8), F32),
        "idty": ((128, 128), BF16), "idtf": ((128, 128), F32),
    }
    out_specs = {
        "x4": ((TL, DM), F32), "u1": ((2048, TL), F32), "delta1": ((2048, TL), F32),
        "bc1": ((128, TL), F32), "z1": ((2048, TL), BF16), "ln5": ((TL, DM), F32),
    }
    nc = mkprog(build_FG(), in_specs, out_specs)
    x3_T = np.concatenate([o["x3"][:, :SL] for o in d_outs] +
                          [o["x3"][:, SL:] for o in d_outs], axis=1)
    x3_nat = np.ascontiguousarray(x3_T.T)
    nT_nat = [np.ascontiguousarray(o["nT"].T) for o in e_outs]
    m2_flat = np.stack([o["mx2"].T.reshape(-1) for o in e_outs])
    d2_flat = np.stack([o["d2"].T.reshape(-1) for o in e_outs])
    shared = {
        "wint": prep["wint"],
        "cvw": inputs["L1_attn_conv_w"].astype(np.float32),
        "cvb": _colscale(inputs["L1_attn_conv_b"]),
        "wx": inputs["L1_attn_W_x"].astype(np.float32),
        "wdt1": inputs["L1_attn_W_dt"].astype(np.float32),
        "bdt1": _colscale(inputs["L1_attn_b_dt"]),
        "nw": _colscale(inputs["L1_attn_norm_w"]),
        "idty": IDTY_BF, "idtf": IDTY,
    }
    in_maps = []
    for c in range(NCORE):
        nTs = np.stack([_padded_rows(nT_nat[k], c) for k in range(NCORE)])
        m2p = np.stack([_padded_rows(m2_flat[k][:, None], c)[:, 0] for k in range(NCORE)])
        d2p = np.stack([_padded_rows(d2_flat[k][:, None], c, const=1.0)[:, 0]
                        for k in range(NCORE)])
        m2s = np.ascontiguousarray(m2p.reshape(8, NTS, 128).transpose(2, 1, 0).reshape(128, NTS * 8))
        d2s = np.ascontiguousarray(d2p.reshape(8, NTS, 128).transpose(2, 1, 0).reshape(128, NTS * 8))
        in_maps.append(dict(shared, x3n=_padded_rows(x3_nat, c), nTs=nTs,
                            m2s=m2s.astype(np.float32), d2s=d2s.astype(np.float32)))
    res = run_bass_kernel_spmd(nc, in_maps, core_ids=list(range(NCORE)))
    return [res.results[c] for c in range(NCORE)]



def build_H():
    TC = 1024
    NOC = (NB * SEQ) // TC

    def bld(nc, P, ins, outs, es):
        sel = sbuf(nc, es, "s_sel64", [128, 64 * 128], F32)
        P.load(sel[:], ins["sel64"], writes=[("sel",)])
        alog = sbuf(nc, es, "s_alog1", [128, 2 * 64], F32)
        P.load(alog[:].rearrange("p (a n) -> p a n", a=2),
               ins["alog1"].rearrange("(a p) n -> p a n", p=128),
               writes=[("alog",)], name="ldal")
        dpsl = sbuf(nc, es, "s_dpsl", [128, 2], F32)
        P.load(dpsl[:], ins["dpsl"], writes=[("dpsl",)])
        A = sbuf(nc, es, "s_A1", [128, 2 * 64], F32)
        P.step("scalar", lambda e: e.activation(out=A[:], in_=alog[:], func=AF.Exp),
               reads=[("alog",)], writes=[("A",)], name="expA")
        P.step("vector", lambda e: e.tensor_scalar_mul(A[:], A[:], -1.0),
               reads=[("A",)], writes=[("A",)], name="negA")
        carry = [sbuf(nc, es, f"carry{i}", [128, 64], F32) for i in range(2)]
        zcol = sbuf(nc, es, "zcol", [128, 1], F32)
        P.step("gpsimd", lambda e: e.memset(zcol[:], 0.0), writes=[("zcol",)], name="z")
        uc = [[sbuf(nc, es, f"uc{i}_{j}", [128, TC], F32) for i in range(2)] for j in range(2)]
        dc = [[sbuf(nc, es, f"dc{i}_{j}", [128, TC], F32) for i in range(2)] for j in range(2)]
        bcc = [sbuf(nc, es, f"bcc{i}", [128, TC], F32) for i in range(2)]
        wc = [sbuf(nc, es, f"wc{j}", [128, TC], F32) for j in range(2)]
        dac = sbuf(nc, es, "dac", [128, TC], F32)
        bbc = sbuf(nc, es, "bbc", [128, TC], F32)
        hc = sbuf(nc, es, "hc", [128, TC], F32)
        ymc = sbuf(nc, es, "ymc", [128, TC], F32)
        yac = [sbuf(nc, es, f"yac{j}", [128, TC], F32) for j in range(2)]
        pbb = psum(nc, es, "pbb", [128, TC], F32)
        pbc = psum(nc, es, "pbc", [128, TC], F32)

        for oc in range(NOC):
            s = oc % 2
            cs = slice(oc * TC, (oc + 1) * TC)
            for j in range(2):
                P.load(uc[s][j][:], ins["u1f"][j * 128:(j + 1) * 128, cs],
                       writes=[("uc", s, j)], name=f"ldu{oc}_{j}")
                P.load(dc[s][j][:], ins["d1f"][j * 128:(j + 1) * 128, cs],
                       writes=[("dc", s, j)], name=f"ldd{oc}_{j}")
                P.step("vector", lambda e, s=s, j=j: e.tensor_mul(
                          wc[j][:], dc[s][j][:], uc[s][j][:]),
                       reads=[("dc", s, j), ("uc", s, j)], writes=[("wc", j)],
                       name=f"w{oc}_{j}")
            P.load(bcc[s][:], ins["bc1f"][:, cs], writes=[("bcc", s)], name=f"ldbc{oc}")
            for n in range(64):
                def mmb(e, s=s, n=n):
                    i1 = e.matmul(pbb[:, 0:512], sel[0:64, n * 128:(n + 1) * 128],
                                  bcc[s][0:64, 0:512], start=True, stop=True)
                    i2 = e.matmul(pbb[:, 512:1024], sel[0:64, n * 128:(n + 1) * 128],
                                  bcc[s][0:64, 512:1024], start=True, stop=True)
                    return i2
                P.step("tensor", mmb, reads=[("sel",), ("bcc", s)], writes=[("psB",)],
                       name=f"bb{oc}_{n}")
                def mmc(e, s=s, n=n):
                    i1 = e.matmul(pbc[:, 0:512], sel[64:128, n * 128:(n + 1) * 128],
                                  bcc[s][64:128, 0:512], start=True, stop=True)
                    i2 = e.matmul(pbc[:, 512:1024], sel[64:128, n * 128:(n + 1) * 128],
                                  bcc[s][64:128, 512:1024], start=True, stop=True)
                    return i2
                P.step("tensor", mmc, reads=[("sel",), ("bcc", s)], writes=[("psC",)],
                       name=f"cb{oc}_{n}")
                for j in range(2):
                    P.step("scalar", lambda e, s=s, n=n, j=j: e.activation(
                              out=dac[:], in_=dc[s][j][:], func=AF.Exp,
                              scale=A[:, j * 64 + n:j * 64 + n + 1]),
                           reads=[("dc", s, j), ("A",)], writes=[("dac",)],
                           name=f"da{oc}_{n}_{j}")
                    P.step("vector", lambda e, j=j: e.scalar_tensor_tensor(
                              out=bbc[:], in0=wc[j][:], scalar=1.0, in1=pbb[:],
                              op0=ALU.mult, op1=ALU.mult),
                           reads=[("wc", j), ("psB",)], writes=[("bbc",)],
                           name=f"bB{oc}_{n}_{j}")
                    if oc % (NOC // 2) == 0:
                        init, ikey = zcol[:], ("zcol",)
                    else:
                        init, ikey = carry[j][:, n:n + 1], ("carry", j, n)
                    P.step("vector", lambda e, init=init: e.tensor_tensor_scan(
                              out=hc[:], data0=dac[:], data1=bbc[:], initial=init,
                              op0=ALU.mult, op1=ALU.add),
                           reads=[("dac",), ("bbc",), ikey], writes=[("hc",)],
                           name=f"h{oc}_{n}_{j}")
                    P.step("vector", lambda e, j=j, n=n: e.tensor_copy(
                              out=carry[j][:, n:n + 1], in_=hc[:, TC - 1:TC]),
                           reads=[("hc",)], writes=[("carry", j, n)], name=f"cy{oc}_{n}_{j}")
                    if n == 0:
                        P.step("vector", lambda e, j=j: e.scalar_tensor_tensor(
                                  out=yac[j][:], in0=hc[:], scalar=1.0, in1=pbc[:],
                                  op0=ALU.mult, op1=ALU.mult),
                               reads=[("hc",), ("psC",)], writes=[("yac", j)],
                               name=f"y{oc}_{n}_{j}")
                    else:
                        P.step("vector", lambda e: e.scalar_tensor_tensor(
                                  out=ymc[:], in0=hc[:], scalar=1.0, in1=pbc[:],
                                  op0=ALU.mult, op1=ALU.mult),
                               reads=[("hc",), ("psC",)], writes=[("ymc",)],
                               name=f"ym{oc}_{n}_{j}")
                        P.step("vector", lambda e, j=j: e.tensor_add(yac[j][:], yac[j][:], ymc[:]),
                               reads=[("ymc",), ("yac", j)], writes=[("yac", j)],
                               name=f"ya{oc}_{n}_{j}")
            for j in range(2):
                P.step("vector", lambda e, s=s, j=j: e.tensor_scalar_mul(
                          ymc[:], uc[s][j][:], dpsl[:, j:j + 1]),
                       reads=[("uc", s, j), ("dpsl",)], writes=[("ymc",)], name=f"ud{oc}_{j}")
                P.step("vector", lambda e, j=j: e.tensor_add(yac[j][:], yac[j][:], ymc[:]),
                       reads=[("ymc",), ("yac", j)], writes=[("yac", j)], name=f"yd{oc}_{j}")
                P.step("gpsimd", lambda e, j=j, cs=cs: e.dma_start(
                          out=outs["y1"][j * 128:(j + 1) * 128, cs], in_=yac[j][:]),
                       reads=[("yac", j)], writes=[("st", j, oc % 2)], dma=True,
                       name=f"st{oc}_{j}")
        P.step("sync", None, reads=[("st", j, k) for j in range(2) for k in range(2)],
               name="fin")
    return bld


SEL64 = np.zeros((128, 64 * 128), np.float32)
for _n in range(64):
    SEL64[_n, _n * 128:(_n + 1) * 128] = 1.0
    SEL64[64 + _n, _n * 128:(_n + 1) * 128] = 1.0


def run_H(fg_outs, inputs):
    TT_ = NB * SEQ
    in_specs = {
        "u1f": ((256, TT_), F32), "d1f": ((256, TT_), F32), "bc1f": ((128, TT_), F32),
        "alog1": ((256, 64), F32), "dpsl": ((128, 2), F32),
        "sel64": ((128, 64 * 128), F32),
    }
    out_specs = {"y1": ((256, TT_), F32)}
    nc = mkprog(build_H(), in_specs, out_specs)
    ufull = np.concatenate([o["u1"][:, :SL] for o in fg_outs] +
                           [o["u1"][:, SL:] for o in fg_outs], axis=1)
    dfull = np.concatenate([o["delta1"][:, :SL] for o in fg_outs] +
                           [o["delta1"][:, SL:] for o in fg_outs], axis=1)
    bcfull = np.concatenate([o["bc1"][:, :SL] for o in fg_outs] +
                            [o["bc1"][:, SL:] for o in fg_outs], axis=1)
    alog = inputs["L1_attn_A_log"].astype(np.float32)
    dp = inputs["L1_attn_Dp"].astype(np.float32)
    in_maps = []
    for c in range(NCORE):
        rs = slice(c * 256, (c + 1) * 256)
        dpc = dp[rs].reshape(2, 128).T
        in_maps.append({
            "u1f": np.ascontiguousarray(ufull[rs]),
            "d1f": np.ascontiguousarray(dfull[rs]),
            "bc1f": bcfull, "alog1": np.ascontiguousarray(alog[rs]),
            "dpsl": np.ascontiguousarray(dpc), "sel64": SEL64,
        })
    res = run_bass_kernel_spmd(nc, in_maps, core_ids=list(range(NCORE)))
    yfull = np.concatenate([res.results[c]["y1"] for c in range(NCORE)], axis=0)
    y_tok = []
    for c in range(NCORE):
        y_tok.append(np.ascontiguousarray(np.concatenate(
            [yfull[:, c * SL:(c + 1) * SL],
             yfull[:, SEQ + c * SL:SEQ + (c + 1) * SL]], axis=1)))
    return y_tok



def build_I():
    TKC = 512
    NCH = TL // TKC

    def bld(nc, P, ins, outs, es):
        sc = {}
        sc["ones_col"] = sbuf(nc, es, "s_ones_col", [128, 1], F32)
        sc["ones_row"] = sbuf(nc, es, "s_ones_row", [1, 128], F32)
        P.load(sc["ones_col"][:], ins["ones_col"], writes=[("ones_col",)])
        P.load(sc["ones_row"][:], ins["ones_row"], writes=[("ones_row",)])
        gam = sbuf(nc, es, "s_gam", [128, 8], F32)
        P.load(gam[:], ins["gam"], writes=[("gam",)])
        ceps = sbuf(nc, es, "s_ceps", [128, 1], F32)
        P.step("gpsimd", lambda e: e.memset(ceps[:], EPS_LN), writes=[("ceps",)], name="ceps")
        r32 = sbuf(nc, es, "s_r32", [1, 128], F32)
        P.step("gpsimd", lambda e: e.memset(r32[:], float(np.sqrt(DM))),
               writes=[("r32",)], name="c32")
        pb = [psum(nc, es, f"pb{i}", [128, 512], F32) for i in range(8)]
        XR = [sbuf(nc, es, f"XR{i}", [128, TL], F32) for i in range(8)]
        for i in range(8):
            P.load(XR[i][:], ins["x4_T"].rearrange("(a p) t -> a p t", p=128)[i],
                   writes=[("XR", i)], name=f"ldx{i}")
        g = [sbuf(nc, es, f"g{i}", [128, TL], BF16) for i in range(16)]
        zt = [sbuf(nc, es, f"zt{i}", [128, TL], BF16) for i in range(1)]
        yt = [sbuf(nc, es, f"yt{i}", [128, TL], F32) for i in range(1)]
        for i in range(8):
            P.load(yt[0][:], ins["ln5_T"].rearrange("(a p) t -> a p t", p=128)[i],
                   writes=[("yt", 0)], name=f"ldl5{i}")
            P.step("vector", lambda e, i=i: e.tensor_add(XR[i][:], XR[i][:], yt[0][:]),
                   reads=[("yt", 0), ("XR", i)], writes=[("XR", i)], name=f"al5{i}")
        szb = sbuf(nc, es, "szb", [128, TL], BF16)
        for kt in range(16):
            s = 0
            P.load(zt[s][:], ins["z_T"][kt * 128:(kt + 1) * 128, :],
                   writes=[("zt", s)], name=f"ldz{kt}")
            P.load(yt[s][:], ins["y_T"][kt * 128:(kt + 1) * 128, :],
                   writes=[("yt", s)], name=f"ldy{kt}")
            P.step("scalar", lambda e, s=s: e.activation(out=szb[:], in_=zt[s][:], func=AF.Silu),
                   reads=[("zt", s)], writes=[("szb",)], name=f"sz{kt}")
            P.step("vector", lambda e, s=s, kt=kt: e.tensor_mul(g[kt][:], yt[s][:], szb[:]),
                   reads=[("yt", s), ("szb",)], writes=[("g", kt)], name=f"g{kt}")
        wsl = [sbuf(nc, es, f"wsl{i}", [128, 2048], BF16) for i in range(2)]
        tmpo = sbuf(nc, es, "s_tmpo", [128, 512], F32)
        for mo in range(8):
            s = mo % 2
            P.load(wsl[s][:], ins["woutt"][mo], writes=[("wsl", s)], name=f"lw{mo}")
            for ci in range(NCH):
                c0 = ci * TKC
                bank = ci % 2
                def mmo(e, s=s, bank=bank, c0=c0):
                    for kt in range(16):
                        i = e.matmul(pb[bank][:], wsl[s][:, kt * 128:(kt + 1) * 128],
                                     g[kt][:, c0:c0 + TKC], start=(kt == 0), stop=(kt == 15))
                    return i
                P.step("tensor", mmo,
                       reads=[("wsl", s)] + [("g", kt) for kt in range(16)],
                       writes=[("pbank", bank)], name=f"om{mo}_{ci}")
                P.step("scalar", lambda e, bank=bank: e.copy(out=tmpo[:], in_=pb[bank][:]),
                       reads=[("pbank", bank)], writes=[("tmpo",)], name=f"oe{mo}_{ci}")
                P.step("vector", lambda e, mo=mo, c0=c0: e.tensor_add(
                          XR[mo][:, c0:c0 + TKC], XR[mo][:, c0:c0 + TKC], tmpo[:]),
                       reads=[("tmpo",), ("XR", mo)], writes=[("XR", mo)],
                       name=f"ox{mo}_{ci}")
        for i in range(8):
            P.step("gpsimd", lambda e, i=i: e.dma_start(
                      out=outs["x5"][i * 128:(i + 1) * 128, :], in_=XR[i][:]),
                   reads=[("XR", i)], writes=[("st_x5", i)], dma=True, name=f"sx5{i}")
        xn = [sbuf(nc, es, f"xn{i}", [128, TL], BF16) for i in range(8)]
        sq = sbuf(nc, es, "s_sq", [128, 512], F32)
        ap1 = sbuf(nc, es, "s_ap1", [128, 512], F32)
        sa = sbuf(nc, es, "s_sa", [128, 512], BF16)
        for nm in ("mrow", "wrow", "rrow", "mbrow"):
            sc[nm] = sbuf(nc, es, "s_" + nm, [1, 512], F32)
        for ch in range(NCH):
            cs = slice(ch * TKC, (ch + 1) * TKC)
            for dt in range(8):
                P.step("scalar", lambda e, dt=dt, cs=cs: e.square(out=sq[:], in_=XR[dt][:, cs]),
                       reads=[("XR", dt)], writes=[("sq",)], name=f"l6q{ch}_{dt}")
                P.step("tensor", lambda e, dt=dt, cs=cs: e.matmul(
                          pb[0][0:1, :], sc["ones_col"][:], XR[dt][:, cs],
                          start=(dt == 0), stop=(dt == 7)),
                       reads=[("XR", dt), ("ones_col",)], writes=[("pbank", 0)],
                       name=f"l6x{ch}_{dt}")
                P.step("tensor", lambda e, dt=dt: e.matmul(
                          pb[1][0:1, :], sc["ones_col"][:], sq[:],
                          start=(dt == 0), stop=(dt == 7)),
                       reads=[("sq",), ("ones_col",)], writes=[("pbank", 1)],
                       name=f"l6s{ch}_{dt}")
            P.step("vector", lambda e: e.tensor_scalar_mul(sc["mrow"][:], pb[0][0:1, :], 1.0 / 1024),
                   reads=[("pbank", 0)], writes=[("mrow",)], name=f"l6m{ch}")
            P.step("vector", lambda e: e.tensor_scalar_mul(sc["wrow"][:], pb[1][0:1, :], 1.0 / 1024),
                   reads=[("pbank", 1)], writes=[("wrow",)], name=f"l6e{ch}")
            P.step("vector", lambda e: e.tensor_mul(sc["mbrow"][:], sc["mrow"][:], sc["mrow"][:]),
                   reads=[("mrow",)], writes=[("mbrow",)], name=f"l6m2{ch}")
            P.step("vector", lambda e: e.tensor_sub(sc["wrow"][:], sc["wrow"][:], sc["mbrow"][:]),
                   reads=[("mbrow",), ("wrow",)], writes=[("wrow",)], name=f"l6v{ch}")
            P.step("scalar", lambda e: e.activation(out=sc["wrow"][:], in_=sc["wrow"][:],
                                                    func=AF.Ln, bias=ceps[0:1, 0:1]),
                   reads=[("wrow",), ("ceps",)], writes=[("wrow",)], name=f"l6l{ch}")
            P.step("scalar", lambda e: e.activation(out=sc["rrow"][:], in_=sc["wrow"][:],
                                                    func=AF.Exp, scale=-0.5),
                   reads=[("wrow",)], writes=[("rrow",)], name=f"l6r{ch}")
            P.step("vector", lambda e: e.scalar_tensor_tensor(
                      out=sc["mbrow"][:], in0=sc["mrow"][:], scalar=-1.0, in1=sc["rrow"][:],
                      op0=ALU.mult, op1=ALU.mult),
                   reads=[("mrow",), ("rrow",)], writes=[("mbrow",)], name=f"l6mb{ch}")
            P.step("tensor", lambda e: e.matmul(pb[2][:], sc["ones_row"][:], sc["rrow"][:],
                                                start=True, stop=True),
                   reads=[("rrow",), ("ones_row",)], writes=[("pbank", 2)], name=f"l6br{ch}")
            P.step("tensor", lambda e: e.matmul(pb[3][:], sc["ones_row"][:], sc["mbrow"][:],
                                                start=True, stop=True),
                   reads=[("mbrow",), ("ones_row",)], writes=[("pbank", 3)], name=f"l6bm{ch}")
            for dt in range(8):
                P.step("vector", lambda e, dt=dt, cs=cs: e.tensor_mul(
                          ap1[:], XR[dt][:, cs], pb[2][:]),
                       reads=[("XR", dt), ("pbank", 2)], writes=[("ap1",)],
                       name=f"l6a1_{ch}_{dt}")
                P.step("vector", lambda e, dt=dt, cs=cs: e.tensor_add(
                          xn[dt][:, cs], ap1[:], pb[3][:]),
                       reads=[("ap1",), ("pbank", 3)], writes=[("xn", dt, cs.start)],
                       name=f"l6a2_{ch}_{dt}")
        for ch in range(NCH):
            cs = slice(ch * TKC, (ch + 1) * TKC)
            for dt in range(8):
                P.step("scalar", lambda e, dt=dt, cs=cs: e.square(out=sq[:], in_=xn[dt][:, cs]),
                       reads=[("xn", dt, cs.start)], writes=[("sq",)], name=f"q6s{ch}_{dt}")
                P.step("tensor", lambda e, dt=dt: e.matmul(
                          pb[0][0:1, :], sc["ones_col"][:], sq[:],
                          start=(dt == 0), stop=(dt == 7)),
                       reads=[("sq",), ("ones_col",)], writes=[("pbank", 0)],
                       name=f"q6m{ch}_{dt}")
            P.step("scalar", lambda e: e.activation(out=sc["wrow"][:], in_=pb[0][0:1, :],
                                                    func=AF.Ln),
                   reads=[("pbank", 0)], writes=[("wrow",)], name=f"q6l{ch}")
            P.step("scalar", lambda e: e.activation(out=sc["rrow"][:], in_=sc["wrow"][:],
                                                    func=AF.Exp, scale=-0.5),
                   reads=[("wrow",)], writes=[("rrow",)], name=f"q6r{ch}")
            P.step("tensor", lambda e: e.matmul(pb[2][:], r32[:], sc["rrow"][:],
                                                start=True, stop=True),
                   reads=[("rrow",), ("r32",)], writes=[("pbank", 2)], name=f"q6b{ch}")
            for dt in range(8):
                P.step("vector", lambda e, dt=dt, cs=cs: e.scalar_tensor_tensor(
                          out=sa[:], in0=xn[dt][:, cs], scalar=gam[:, dt:dt + 1],
                          in1=pb[2][:], op0=ALU.mult, op1=ALU.mult),
                       reads=[("xn", dt, cs.start), ("gam",), ("pbank", 2)],
                       writes=[("sa",)], name=f"q6x{ch}_{dt}")
                P.step("gpsimd", lambda e, dt=dt, ch=ch: e.dma_start(
                          out=outs["xq2_T"][dt * 128:(dt + 1) * 128, ch * TKC:(ch + 1) * TKC],
                          in_=sa[:]),
                       reads=[("sa",)], writes=[("st_q", dt, ch)], dma=True,
                       name=f"sq6{ch}_{dt}")
        P.step("sync", None,
               reads=[("st_x5", i) for i in range(8)] +
                     [("st_q", dt, ch) for dt in range(8) for ch in range(NCH)],
               name="fin")
    return bld


def run_I(inputs, prep, fg_outs, y1_tok):
    in_specs = {
        "x4_T": ((DM, TL), F32), "ln5_T": ((DM, TL), F32),
        "y_T": ((2048, TL), F32), "z_T": ((2048, TL), BF16),
        "woutt": ((8, 128, 2048), BF16), "gam": ((128, 8), F32),
        "ones_col": ((128, 1), F32), "ones_row": ((1, 128), F32),
    }
    out_specs = {"x5": ((DM, TL), F32), "xq2_T": ((DM, TL), BF16)}
    nc = mkprog(build_I(), in_specs, out_specs)
    shared = {"woutt": prep["woutt"], "gam": _colscale(inputs["L1_ffn_norm_gamma"]),
              "ones_col": ONES_COL, "ones_row": ONES_ROW}
    in_maps = [dict(shared,
                    x4_T=np.ascontiguousarray(fg_outs[c]["x4"].T),
                    ln5_T=np.ascontiguousarray(fg_outs[c]["ln5"].T),
                    y_T=y1_tok[c], z_T=fg_outs[c]["z1"]) for c in range(NCORE)]
    res = run_bass_kernel_spmd(nc, in_maps, core_ids=list(range(NCORE)))
    return [res.results[c] for c in range(NCORE)]


def build_K():
    def bld(nc, P, ins, outs, es):
        m2s = sbuf(nc, es, "s_m2s", [128, 16 * 8], F32)
        P.load(m2s[:], ins["m2s"], writes=[("m2s",)])
        d2s = sbuf(nc, es, "s_d2s", [128, 16 * 8], F32)
        P.load(d2s[:], ins["d2s"], writes=[("d2s",)])
        x5n = sbuf(nc, es, "s_x5n", [128, 1024], F32)
        ntile = sbuf(nc, es, "s_ntile", [128, 1024], BF16)
        acc = sbuf(nc, es, "s_acc", [128, 1024], F32)
        srow8 = sbuf(nc, es, "s_srow8", [128, 8], F32)
        sd8 = sbuf(nc, es, "s_sd8", [128, 8], F32)
        mcol = sbuf(nc, es, "s_mcol", [128, 1], F32)
        dcol = sbuf(nc, es, "s_dcol", [128, 1], F32)
        for ts in range(16):
            P.load(x5n[:], ins["x5n"].rearrange("(c p) d -> c p d", p=128)[ts],
                   writes=[("x5n",)], name=f"ldx{ts}")
            P.step("vector", lambda e, ts=ts: e.reduce_max(
                      mcol[:], m2s[:, ts * 8:(ts + 1) * 8], axis=mybir.AxisListType.X),
                   reads=[("m2s",)], writes=[("mcol",)], name=f"kM{ts}")
            P.step("vector", lambda e: e.tensor_scalar_mul(mcol[:], mcol[:], -1.0),
                   reads=[("mcol",)], writes=[("mcol",)], name=f"kMn{ts}")
            P.step("scalar", lambda e, ts=ts: e.activation(
                      out=srow8[:], in_=m2s[:, ts * 8:(ts + 1) * 8], func=AF.Exp,
                      bias=mcol[:, 0:1]),
                   reads=[("m2s",), ("mcol",)], writes=[("srow8",)], name=f"kS{ts}")
            P.step("vector", lambda e, ts=ts: e.tensor_mul(
                      sd8[:], srow8[:], d2s[:, ts * 8:(ts + 1) * 8]),
                   reads=[("srow8",), ("d2s",)], writes=[("sd8",)], name=f"kSD{ts}")
            P.step("vector", lambda e: e.reduce_sum(dcol[:], sd8[:], axis=mybir.AxisListType.X),
                   reads=[("sd8",)], writes=[("dcol",)], name=f"kden{ts}")
            P.step("vector", lambda e: e.reciprocal(dcol[:], dcol[:]),
                   reads=[("dcol",)], writes=[("dcol",)], name=f"krc{ts}")
            for c in range(NCORE):
                P.load(ntile[:], ins["nTs"].rearrange("e (c p) d -> e c p d", p=128)[c, ts],
                       writes=[("ntile",)], name=f"kn{ts}_{c}")
                if c == 0:
                    P.step("vector", lambda e, c=c: e.tensor_scalar_mul(
                              acc[:], ntile[:], srow8[:, c:c + 1]),
                           reads=[("ntile",), ("srow8",)], writes=[("acc",)],
                           name=f"ka{ts}_{c}")
                else:
                    P.step("vector", lambda e, c=c: e.scalar_tensor_tensor(
                              out=acc[:], in0=ntile[:], scalar=srow8[:, c:c + 1],
                              in1=acc[:], op0=ALU.mult, op1=ALU.add),
                           reads=[("ntile",), ("srow8",), ("acc",)], writes=[("acc",)],
                           name=f"ka{ts}_{c}")
            P.step("vector", lambda e: e.tensor_scalar(
                      out=acc[:], in0=acc[:], scalar1=dcol[:, 0:1], scalar2=0.5,
                      op0=ALU.mult, op1=ALU.mult),
                   reads=[("acc",), ("dcol",)], writes=[("acc",)], name=f"kmx{ts}")
            P.step("vector", lambda e: e.tensor_add(x5n[:], x5n[:], acc[:]),
                   reads=[("acc",), ("x5n",)], writes=[("x5n",)], name=f"kx6{ts}")
            P.step("gpsimd", lambda e, ts=ts: e.dma_start(
                      out=outs["x6"].rearrange("(c p) d -> c p d", p=128)[ts], in_=x5n[:]),
                   reads=[("x5n",)], writes=[("st", ts % 4)], dma=True, name=f"ks{ts}")
        P.step("sync", None, reads=[("st", k) for k in range(4)], name="fin")
    return bld


def run_K(i_outs, j_outs):
    in_specs = {
        "x5n": ((TL, DM), F32), "nTs": ((NCORE, TL, DM), BF16),
        "m2s": ((128, 16 * 8), F32), "d2s": ((128, 16 * 8), F32),
    }
    out_specs = {"x6": ((TL, DM), F32)}
    nc = mkprog(build_K(), in_specs, out_specs)
    nT_nat = [np.ascontiguousarray(o["nT"].T) for o in j_outs]
    m2_flat = np.stack([o["mx2"].T.reshape(-1) for o in j_outs])
    d2_flat = np.stack([o["d2"].T.reshape(-1) for o in j_outs])
    in_maps = []
    for c in range(NCORE):
        tok_idx = np.r_[c * SL:(c + 1) * SL, SEQ + c * SL:SEQ + (c + 1) * SL]
        nTs = np.stack([nT_nat[k][tok_idx] for k in range(NCORE)])
        m2p = m2_flat[:, tok_idx]
        d2p = d2_flat[:, tok_idx]
        m2s = np.ascontiguousarray(m2p.reshape(8, 16, 128).transpose(2, 1, 0).reshape(128, 128))
        d2s = np.ascontiguousarray(d2p.reshape(8, 16, 128).transpose(2, 1, 0).reshape(128, 128))
        in_maps.append({"x5n": np.ascontiguousarray(i_outs[c]["x5"].T), "nTs": nTs,
                        "m2s": m2s.astype(np.float32), "d2s": d2s.astype(np.float32)})
    res = run_bass_kernel_spmd(nc, in_maps, core_ids=list(range(NCORE)))
    return [res.results[c]["x6"] for c in range(NCORE)]


def kernel(**inputs):
    inputs = {k: np.asarray(v) for k, v in inputs.items()}
    prep = run_A(inputs)
    b_outs = run_B(inputs, prep)
    y0_tok = run_C(b_outs, inputs)
    d_outs = run_D(inputs, prep, b_outs, y0_tok)
    xq_full = assemble_xq_full(d_outs)
    e_outs = run_E(inputs, xq_full, "L1_ffn")
    fg_outs = run_FG(inputs, prep, d_outs, e_outs)
    y1_tok = run_H(fg_outs, inputs)
    i_outs = run_I(inputs, prep, fg_outs, y1_tok)
    xq2_full = assemble_xq_full(i_outs, key="xq2_T")
    j_outs = run_E(inputs, xq2_full, "L1_ffn")
    x6 = run_K(i_outs, j_outs)
    return _untoks(x6)


# version 03
# speedup vs baseline: 1.3727x; 1.3727x over previous
import numpy as np
import ml_dtypes
import concourse.bass as bass
import concourse.mybir as mybir
from concourse.bass_utils import run_bass_kernel_spmd
from contextlib import ExitStack

F32 = mybir.dt.float32
BF16 = mybir.dt.bfloat16
AF = mybir.ActivationFunctionType
ALU = mybir.AluOpType
BF = ml_dtypes.bfloat16

NCORE = 8
NB, SEQ, DM = 2, 8192, 1024
SL = SEQ // NCORE
TL = NB * SL
EPS_LN = 1e-5


class Prog:
    COMPUTE = ("tensor", "vector", "scalar", "gpsimd")
    ALL = COMPUTE + ("sync",)
    NDMA = 16


    def __init__(self, nc, es):
        self.nc = nc
        self.es = es
        self.steps = []
        self.bufs = {}
        self.last_dma_on_sem = {}

    def _dma_sem_id(self, key):
        return hash(key) % self.NDMA

    def step(self, eng, emit, reads=(), writes=(), dma=False, name=""):
        raw = set()
        war = set()
        for b in reads:
            st = self.bufs.get(b)
            if st and st["w"] is not None:
                raw.add(st["w"])
        for b in writes:
            st = self.bufs.get(b)
            if st:
                if st["w"] is not None:
                    raw.add(st["w"])
                war.update(st["r"])
        deps = raw | war
        sid = len(self.steps)
        sem_id = None
        if dma:
            key = writes[0] if writes else name
            sem_id = self._dma_sem_id(key)
            prev = self.last_dma_on_sem.get(sem_id)
            if prev is not None:
                deps.add(prev)
            self.last_dma_on_sem[sem_id] = sid
        self.steps.append(dict(eng=eng, emit=emit, deps=deps, dma=dma,
                               sem_id=sem_id, name=name))
        for b in reads:
            self.bufs.setdefault(b, {"w": None, "r": []})["r"].append(sid)
        for b in writes:
            self.bufs[b] = {"w": sid, "r": []}
        return sid

    def load(self, dst_ap, src_ap, reads=(), writes=(), q="sync", name="ld"):
        return self.step(q, lambda e: e.dma_start(out=dst_ap, in_=src_ap),
                         reads=reads, writes=writes, dma=True, name=name)

    def finalize(self, block):
        nc = self.nc
        counts = {}
        for st in self.steps:
            if st["emit"] is None:
                sem = None
            elif st["dma"]:
                sem = f"d{st['sem_id']}"
            else:
                sem = f"e_{st['eng']}"
            if sem is not None:
                counts[sem] = counts.get(sem, 0) + (16 if st["dma"] else 1)
            st["sem"] = sem
            st["val"] = counts.get(sem, 0)
        sem_es = ExitStack()
        sems = {s: sem_es.enter_context(nc.semaphore(s)) for s in counts}
        by_eng = {e: [s for s in self.steps if s["eng"] == e] for e in self.ALL}
        progself = self

        def runner(eng_name):
            def run(eng):
                waited = {}
                for st in by_eng[eng_name]:
                    for d in sorted(st["deps"]):
                        ds = progself.steps[d]
                        if ds["sem"] is None:
                            raise RuntimeError(f"dep on barrier step {ds['name']}")
                        v = ds["val"]
                        if waited.get(ds["sem"], 0) >= v:
                            continue
                        eng.wait_ge(sems[ds["sem"]], v)
                        waited[ds["sem"]] = v
                    if st["emit"] is None:
                        continue
                    insts = st["emit"](eng)
                    if not isinstance(insts, (list, tuple)):
                        insts = [insts]
                    if st["dma"]:
                        if len(insts) != 1:
                            raise RuntimeError("dma steps must emit exactly 1 dma")
                        insts[0].then_inc(sems[st["sem"]], 16)
                    else:
                        insts[-1].then_inc(sems[st["sem"]], 1)
            return run

        for e in self.ALL:
            if not by_eng[e]:
                continue
            getattr(block, e)(runner(e))
        return sem_es


def mkprog(build_fn, in_specs, out_specs):
    nc = bass.Bass("TRN2", target_bir_lowering=False, debug=False)
    ins = {k: nc.dram_tensor(k, list(v[0]), v[1], kind="ExternalInput").ap()
           for k, v in in_specs.items()}
    outs = {k: nc.dram_tensor(k, list(v[0]), v[1], kind="ExternalOutput").ap()
            for k, v in out_specs.items()}
    es = ExitStack()
    with es:
        with nc.Block() as block:
            P = Prog(nc, es)
            build_fn(nc, P, ins, outs, es)
            sem_es = P.finalize(block)
            es.enter_context(sem_es)
    return nc


def sbuf(nc, es, name, shape, dt=F32):
    return es.enter_context(nc.sbuf_tensor(name, list(shape), dt))


def psum(nc, es, name, shape, dt=F32):
    return es.enter_context(nc.psum_tensor(name, list(shape), dt))


def build_A(nt):
    def bld(nc, P, ins, outs, es):
        blocks = ins["blocks"]
        obl = outs["tiles"]
        idt = sbuf(nc, es, "idt", [128, 128], BF16)
        P.load(idt[:], ins["idty"], writes=[("idt",)])
        nslot = 4
        raw = [sbuf(nc, es, f"raw{i}", [128, 128], F32) for i in range(nslot)]
        cst = [sbuf(nc, es, f"cst{i}", [128, 128], BF16) for i in range(nslot)]
        out = [sbuf(nc, es, f"out{i}", [128, 128], BF16) for i in range(nslot)]
        pst = [psum(nc, es, f"pst{i}", [128, 128], BF16) for i in range(nslot)]
        for i in range(nt):
            s = i % nslot
            P.load(raw[s][:], blocks[i], writes=[("raw", s)], name=f"ldw{i}")
            P.step("scalar", lambda e, s=s: e.copy(out=cst[s][:], in_=raw[s][:]),
                   reads=[("raw", s)], writes=[("cst", s)], name=f"cast{i}")
            P.step("tensor", lambda e, s=s: e.transpose(pst[s][:], cst[s][:], idt[:]),
                   reads=[("cst", s), ("idt",)], writes=[("pst", s)], name=f"tr{i}")
            P.step("vector", lambda e, s=s: e.tensor_copy(out=out[s][:], in_=pst[s][:]),
                   reads=[("pst", s)], writes=[("out", s)], name=f"ev{i}")
            P.step("gpsimd", lambda e, s=s, i=i: e.dma_start(out=obl[i], in_=out[s][:]),
                   reads=[("out", s)], writes=[("st", i)], dma=True, name=f"st{i}")
        last = [("st", i) for i in range(nt)]
        P.step("sync", None, reads=last, name="fin")
    return bld


def _toks(a):
    out = []
    for c in range(NCORE):
        out.append(np.ascontiguousarray(
            np.concatenate([a[0, c * SL:(c + 1) * SL], a[1, c * SL:(c + 1) * SL]], 0)))
    return out


def _untoks(slices, D=DM, dtype=np.float32):
    a = np.empty((NB, SEQ, D), dtype)
    for c in range(NCORE):
        a[0, c * SL:(c + 1) * SL] = slices[c][:SL]
        a[1, c * SL:(c + 1) * SL] = slices[c][SL:]
    return a


def _tileblocks(w):
    M, K = w.shape
    assert M % 128 == 0 and K % 128 == 0
    b = w.reshape(M // 128, 128, K // 128, 128).transpose(0, 2, 1, 3)
    return np.ascontiguousarray(b.reshape(-1, 128, 128).astype(np.float32)), (M // 128, K // 128)


def _assemble_T(tiles, mo, ko):
    t = tiles.reshape(mo, ko, 128, 128)
    t = t.transpose(0, 2, 1, 3)
    return np.ascontiguousarray(t.reshape(mo, 128, ko * 128))


def _colscale(s):
    M = s.shape[0]
    return np.ascontiguousarray(s.reshape(M // 128, 128).T.astype(np.float32))


IDTY = np.eye(128, dtype=np.float32)
IDTY_BF = np.eye(128, dtype=BF)
ONES_COL = np.ones((128, 1), np.float32)
ONES_ROW = np.ones((1, 128), np.float32)


def run_A(inputs):
    specs = [
        ("w1t", inputs["L0_ffn_w1_w"]),
        ("w2t", inputs["L0_ffn_w2_w"]),
        ("h1t", inputs["L0_ffn_h1_w"]),
        ("h2t", inputs["L0_ffn_h2_w"]),
        ("w3t", inputs["L0_ffn_w3_w"]),
        ("wint", inputs["L1_attn_W_in"]),
        ("woutt", inputs["L1_attn_W_out"]),
    ]
    blocks = []
    meta = []
    for name, w in specs:
        b, (mo, ko) = _tileblocks(w)
        meta.append((name, mo, ko, len(b)))
        blocks.append(b)
    allb = np.concatenate(blocks, 0)
    ntot = len(allb)
    per = (ntot + NCORE - 1) // NCORE
    pad = per * NCORE - ntot
    if pad:
        allb = np.concatenate([allb, np.zeros((pad, 128, 128), np.float32)], 0)
    nc = mkprog(build_A(per),
                {"blocks": ((per, 128, 128), F32), "idty": ((128, 128), BF16)},
                {"tiles": ((per, 128, 128), BF16)})
    in_maps = [{"blocks": allb[c * per:(c + 1) * per], "idty": IDTY_BF}
               for c in range(NCORE)]
    res = run_bass_kernel_spmd(nc, in_maps, core_ids=list(range(NCORE)))
    tiles = np.concatenate([res.results[c]["tiles"] for c in range(NCORE)], 0)[:ntot]
    out = {}
    off = 0
    for name, mo, ko, n in meta:
        out[name] = _assemble_T(tiles[off:off + n], mo, ko)
        off += n
    return out



def build_B(variant="B"):
    TKC = 512
    NCH = TL // TKC

    def bld(nc, P, ins, outs, es):
        sc = {}
        sc["ones_col"] = sbuf(nc, es, "s_ones_col", [128, 1], F32)
        sc["ones_row"] = sbuf(nc, es, "s_ones_row", [1, 128], F32)
        idtb = sbuf(nc, es, "idtb", [128, 128], BF16)
        P.load(sc["ones_col"][:], ins["ones_col"], writes=[("ones_col",)])
        P.load(sc["ones_row"][:], ins["ones_row"], writes=[("ones_row",)])
        P.load(idtb[:], ins["idty"], writes=[("idtb",)])
        scl_list = [("w1s", [128, 32]), ("w2s", [128, 32]), ("h1s", [128, 32]),
                    ("h2s", [128, 32]), ("w3s", [128, 8])]
        scl_list += [("bdt", [128, 8])] if variant == "B" else [("gam", [128, 8])]
        for nm, shape in scl_list:
            sc[nm] = sbuf(nc, es, "s_" + nm, shape, F32)
            P.load(sc[nm][:], ins[nm], writes=[(nm,)])
        pb = [psum(nc, es, f"pb{i}", [128, 512], F32) for i in range(7)]
        XR = [sbuf(nc, es, f"XR{i}", [128, TL], F32) for i in range(8)]
        xn1 = [sbuf(nc, es, f"xn1_{i}", [128, TL], BF16) for i in range(8)]
        g = [sbuf(nc, es, f"g{i}", [128, TKC], BF16) for i in range(32)]
        g2 = [sbuf(nc, es, f"g2_{i}", [128, TKC], BF16) for i in range(32)]
        wsl = [sbuf(nc, es, f"wsl{i}", [128, 4096], BF16) for i in range(2)]
        if variant == "B":
            for i in range(8):
                P.load(XR[i][:], ins["x_T"].rearrange("(a p) t -> a p t", p=128)[i],
                       writes=[("XR", i)], name=f"ldx{i}")
        else:
            yt = [sbuf(nc, es, f"yt{i}", [128, TL], F32) for i in range(2)]
            for i in range(8):
                P.load(XR[i][:], ins["x1_T"].rearrange("(a p) t -> a p t", p=128)[i],
                       writes=[("XR", i)], name=f"ldx{i}")
                P.load(yt[i % 2][:], ins["y_T"].rearrange("(a p) t -> a p t", p=128)[i],
                       writes=[("yt", i % 2)], name=f"ldy{i}")
                P.step("vector", lambda e, i=i: e.tensor_add(XR[i][:], XR[i][:], yt[i % 2][:]),
                       reads=[("yt", i % 2), ("XR", i)], writes=[("XR", i)], name=f"addy{i}")
        sc["sq"] = sbuf(nc, es, "sq", [128, 512], F32)
        sc["ceps"] = sbuf(nc, es, "ceps", [128, 1], F32)
        P.step("gpsimd", lambda e: e.memset(sc["ceps"][:], EPS_LN), writes=[("ceps",)], name="ceps")
        for nm in ("mrow", "wrow", "rrow", "mbrow"):
            sc[nm] = sbuf(nc, es, nm, [1, 512], F32)
        sc["ap1"] = sbuf(nc, es, "ap1", [128, 512], F32)
        sc["sa"] = sbuf(nc, es, "sa", [128, 512], BF16)

        def ln_pass(pref, dst_put):
            for ch in range(NCH):
                cs = slice(ch * TKC, (ch + 1) * TKC)
                for dt in range(8):
                    P.step("scalar", lambda e, dt=dt, cs=cs: e.square(out=sc["sq"][:], in_=XR[dt][:, cs]),
                           reads=[("XR", dt)], writes=[("sq",)], name=f"{pref}sq{ch}_{dt}")
                    P.step("tensor", lambda e, dt=dt, cs=cs: e.matmul(
                              pb[0][0:1, :], sc["ones_col"][:], XR[dt][:, cs],
                              start=(dt == 0), stop=(dt == 7)),
                           reads=[("XR", dt), ("ones_col",)], writes=[("pbank", 0)],
                           name=f"{pref}sx{ch}_{dt}")
                    P.step("tensor", lambda e, dt=dt: e.matmul(
                              pb[1][0:1, :], sc["ones_col"][:], sc["sq"][:],
                              start=(dt == 0), stop=(dt == 7)),
                           reads=[("sq",), ("ones_col",)], writes=[("pbank", 1)],
                           name=f"{pref}sq2{ch}_{dt}")
                P.step("vector", lambda e: e.tensor_scalar_mul(sc["mrow"][:], pb[0][0:1, :], 1.0 / 1024),
                       reads=[("pbank", 0)], writes=[("mrow",)], name=f"{pref}m{ch}")
                P.step("vector", lambda e: e.tensor_scalar_mul(sc["wrow"][:], pb[1][0:1, :], 1.0 / 1024),
                       reads=[("pbank", 1)], writes=[("wrow",)], name=f"{pref}e{ch}")
                P.step("vector", lambda e: e.tensor_mul(sc["mbrow"][:], sc["mrow"][:], sc["mrow"][:]),
                       reads=[("mrow",)], writes=[("mbrow",)], name=f"{pref}m2{ch}")
                P.step("vector", lambda e: e.tensor_sub(sc["wrow"][:], sc["wrow"][:], sc["mbrow"][:]),
                       reads=[("mbrow",), ("wrow",)], writes=[("wrow",)], name=f"{pref}v{ch}")
                P.step("scalar", lambda e: e.activation(out=sc["wrow"][:], in_=sc["wrow"][:],
                                                        func=AF.Ln, bias=sc["ceps"][0:1, :]),
                       reads=[("wrow",), ("ceps",)], writes=[("wrow",)], name=f"{pref}l{ch}")
                P.step("scalar", lambda e: e.activation(out=sc["rrow"][:], in_=sc["wrow"][:],
                                                        func=AF.Exp, scale=-0.5),
                       reads=[("wrow",)], writes=[("rrow",)], name=f"{pref}r{ch}")
                P.step("vector", lambda e: e.scalar_tensor_tensor(
                          out=sc["mbrow"][:], in0=sc["mrow"][:], scalar=-1.0, in1=sc["rrow"][:],
                          op0=ALU.mult, op1=ALU.mult),
                       reads=[("mrow",), ("rrow",)], writes=[("mbrow",)], name=f"{pref}mb{ch}")
                P.step("tensor", lambda e: e.matmul(pb[2][:], sc["ones_row"][:], sc["rrow"][:],
                                                    start=True, stop=True),
                       reads=[("rrow",), ("ones_row",)], writes=[("pbank", 2)], name=f"{pref}bcr{ch}")
                P.step("tensor", lambda e: e.matmul(pb[3][:], sc["ones_row"][:], sc["mbrow"][:],
                                                    start=True, stop=True),
                       reads=[("mbrow",), ("ones_row",)], writes=[("pbank", 3)], name=f"{pref}bcm{ch}")
                for dt in range(8):
                    P.step("vector", lambda e, dt=dt, cs=cs: e.tensor_mul(
                              sc["ap1"][:], XR[dt][:, cs], pb[2][:]),
                           reads=[("XR", dt), ("pbank", 2)], writes=[("ap1",)],
                           name=f"{pref}ap1_{ch}_{dt}")
                    dst_put(dt, ch)

        def put_xn1(dt, ch):
            cs = slice(ch * TKC, (ch + 1) * TKC)
            P.step("vector", lambda e, dt=dt, cs=cs: e.tensor_add(
                      xn1[dt][:, cs], sc["ap1"][:], pb[3][:]),
                   reads=[("ap1",), ("pbank", 3)], writes=[("xn1", dt, cs.start)],
                   name=f"xn1_{ch}_{dt}")
        ln_pass("ln1", put_xn1)

        def ffn_layer(pref, ch, srcs, src_keys, wname, nko, mos, evac):
            cs = slice(ch * TKC, (ch + 1) * TKC)
            for mi, mo in enumerate(mos):
                slot = mi % len(wsl)
                bank = mi % 2
                P.load(wsl[slot][:, :nko * 128], ins[wname].rearrange("m p f -> m p f")[mo],
                       writes=[("wsl", slot)], q="sync", name=f"{pref}w{ch}_{mo}")
                def mmgroup(e, slot=slot, bank=bank, cs=cs, nko=nko, srcs=srcs):
                    insts = []
                    for ko in range(nko):
                        insts.append(e.matmul(
                            pb[bank][:], wsl[slot][:, ko * 128:(ko + 1) * 128],
                            srcs[ko][:, cs], start=(ko == 0), stop=(ko == nko - 1)))
                    return insts[-1]
                P.step("tensor", mmgroup,
                       reads=[("wsl", slot)] + [src_keys(ko, cs.start) for ko in range(nko)],
                       writes=[("pbank", bank)], name=f"{pref}mm{ch}_{mo}")
                evac(mo, ("pbank", bank), pb[bank])

        for ch in range(NCH):
            cs = slice(ch * TKC, (ch + 1) * TKC)
            for mi in range(32):
                slot = 0
                slot2 = 1
                bank_a, bank_b = (0, 1) if mi % 2 == 0 else (2, 3)
                P.load(wsl[slot][:, :1024], ins["w1t"][mi], writes=[("wsl", slot)],
                       name=f"fw1_{ch}_{mi}")
                def mma(e, slot=slot, bank=bank_a, cs=cs):
                    for ko in range(8):
                        i = e.matmul(pb[bank][:], wsl[slot][:, ko * 128:(ko + 1) * 128],
                                     xn1[ko][:, cs], start=(ko == 0), stop=(ko == 7))
                    return i
                P.step("tensor", mma,
                       reads=[("wsl", slot)] + [("xn1", ko, cs.start) for ko in range(8)],
                       writes=[("pbank", bank_a)], name=f"fma_{ch}_{mi}")
                P.step("scalar", lambda e, bank=bank_a, mi=mi: e.activation(
                          out=sc["sa"][:], in_=pb[bank][:], func=AF.Silu,
                          scale=sc["w1s"][:, mi:mi + 1]),
                       reads=[("pbank", bank_a), ("w1s",)], writes=[("sa",)],
                       name=f"fsa_{ch}_{mi}")
                P.load(wsl[slot2][:, :1024], ins["w2t"][mi], writes=[("wsl", slot2)],
                       name=f"fw2_{ch}_{mi}")
                def mmb(e, slot=slot2, bank=bank_b, cs=cs):
                    for ko in range(8):
                        i = e.matmul(pb[bank][:], wsl[slot][:, ko * 128:(ko + 1) * 128],
                                     xn1[ko][:, cs], start=(ko == 0), stop=(ko == 7))
                    return i
                P.step("tensor", mmb,
                       reads=[("wsl", slot2)] + [("xn1", ko, cs.start) for ko in range(8)],
                       writes=[("pbank", bank_b)], name=f"fmb_{ch}_{mi}")
                P.step("vector", lambda e, mi=mi, bank=bank_b: e.scalar_tensor_tensor(
                          out=g[mi][:], in0=pb[bank][:], scalar=sc["w2s"][:, mi:mi + 1],
                          in1=sc["sa"][:], op0=ALU.mult, op1=ALU.mult),
                       reads=[("pbank", bank_b), ("w2s",), ("sa",)], writes=[("g", mi)],
                       name=f"fg_{ch}_{mi}")
            for mi in range(32):
                slot = mi % 2
                bank = 4 + (mi % 2)
                P.load(wsl[slot][:], ins["h1t"][mi], writes=[("wsl", slot)],
                       name=f"fh1_{ch}_{mi}")
                def mmh(e, slot=slot, bank=bank):
                    for ko in range(32):
                        i = e.matmul(pb[bank][:], wsl[slot][:, ko * 128:(ko + 1) * 128],
                                     g[ko][:], start=(ko == 0), stop=(ko == 31))
                    return i
                P.step("tensor", mmh,
                       reads=[("wsl", slot)] + [("g", ko) for ko in range(32)],
                       writes=[("pbank", bank)], name=f"fmh1_{ch}_{mi}")
                P.step("scalar", lambda e, bank=bank, mi=mi: e.activation(
                          out=g2[mi][:], in_=pb[bank][:], func=AF.Gelu,
                          scale=sc["h1s"][:, mi:mi + 1]),
                       reads=[("pbank", bank), ("h1s",)], writes=[("g2", mi)],
                       name=f"fg2_{ch}_{mi}")
            for mi in range(32):
                slot = mi % 2
                bank = 4 + (mi % 2)
                P.load(wsl[slot][:], ins["h2t"][mi], writes=[("wsl", slot)],
                       name=f"fh2_{ch}_{mi}")
                def mmh2(e, slot=slot, bank=bank):
                    for ko in range(32):
                        i = e.matmul(pb[bank][:], wsl[slot][:, ko * 128:(ko + 1) * 128],
                                     g2[ko][:], start=(ko == 0), stop=(ko == 31))
                    return i
                P.step("tensor", mmh2,
                       reads=[("wsl", slot)] + [("g2", ko) for ko in range(32)],
                       writes=[("pbank", bank)], name=f"fmh2_{ch}_{mi}")
                P.step("scalar", lambda e, bank=bank, mi=mi: e.activation(
                          out=g[mi][:], in_=pb[bank][:], func=AF.Gelu,
                          scale=sc["h2s"][:, mi:mi + 1]),
                       reads=[("pbank", bank), ("h2s",)], writes=[("g", mi)],
                       name=f"fg3_{ch}_{mi}")
            for mi in range(8):
                slot = mi % 2
                bank = 6
                P.load(wsl[slot][:], ins["w3t"][mi], writes=[("wsl", slot)],
                       name=f"fw3_{ch}_{mi}")
                def mmo(e, slot=slot, bank=bank):
                    for ko in range(32):
                        i = e.matmul(pb[bank][:], wsl[slot][:, ko * 128:(ko + 1) * 128],
                                     g[ko][:], start=(ko == 0), stop=(ko == 31))
                    return i
                P.step("tensor", mmo,
                       reads=[("wsl", slot)] + [("g", ko) for ko in range(32)],
                       writes=[("pbank", bank)], name=f"fmo_{ch}_{mi}")
                P.step("scalar", lambda e, bank=bank, mi=mi: e.activation(
                          out=sc["ap1"][:], in_=pb[bank][:], func=AF.Copy,
                          scale=sc["w3s"][:, mi:mi + 1]),
                       reads=[("pbank", bank), ("w3s",)], writes=[("ap1",)],
                       name=f"fto_{ch}_{mi}")
                P.step("vector", lambda e, mi=mi, cs=cs: e.scalar_tensor_tensor(
                          out=XR[mi][:, cs], in0=sc["ap1"][:], scalar=0.5,
                          in1=XR[mi][:, cs], op0=ALU.mult, op1=ALU.add),
                       reads=[("ap1",), ("XR", mi)], writes=[("XR", mi)],
                       name=f"fxr_{ch}_{mi}")

        if variant == "D":
            def put_ln4(dt, ch):
                cs = slice(ch * TKC, (ch + 1) * TKC)
                P.step("vector", lambda e, dt=dt, cs=cs: e.tensor_add(
                          xn1[dt][:, cs], sc["ap1"][:], pb[3][:]),
                       reads=[("ap1",), ("pbank", 3)], writes=[("xn1", dt, cs.start)],
                       name=f"ln4_{ch}_{dt}")
            ln_pass("ln4", put_ln4)
            r32 = sbuf(nc, es, "r32", [1, 128], F32)
            P.step("gpsimd", lambda e: e.memset(r32[:], float(np.sqrt(DM))),
                   writes=[("r32",)], name="c32")
            for ch in range(NCH):
                cs = slice(ch * TKC, (ch + 1) * TKC)
                for dt in range(8):
                    P.step("scalar", lambda e, dt=dt, cs=cs: e.square(
                              out=sc["sq"][:], in_=xn1[dt][:, cs]),
                           reads=[("xn1", dt, cs.start)], writes=[("sq",)],
                           name=f"qsq{ch}_{dt}")
                    P.step("tensor", lambda e, dt=dt: e.matmul(
                              pb[0][0:1, :], sc["ones_col"][:], sc["sq"][:],
                              start=(dt == 0), stop=(dt == 7)),
                           reads=[("sq",), ("ones_col",)], writes=[("pbank", 0)],
                           name=f"qs{ch}_{dt}")
                P.step("scalar", lambda e: e.activation(out=sc["wrow"][:], in_=pb[0][0:1, :],
                                                        func=AF.Ln),
                       reads=[("pbank", 0)], writes=[("wrow",)], name=f"qln{ch}")
                P.step("scalar", lambda e: e.activation(out=sc["rrow"][:], in_=sc["wrow"][:],
                                                        func=AF.Exp, scale=-0.5),
                       reads=[("wrow",)], writes=[("rrow",)], name=f"qr{ch}")
                P.step("tensor", lambda e: e.matmul(pb[2][:], r32[:], sc["rrow"][:],
                                                    start=True, stop=True),
                       reads=[("rrow",), ("r32",)], writes=[("pbank", 2)], name=f"qbc{ch}")
                for dt in range(8):
                    P.step("vector", lambda e, dt=dt, cs=cs: e.scalar_tensor_tensor(
                              out=sc["sa"][:], in0=xn1[dt][:, cs],
                              scalar=sc["gam"][:, dt:dt + 1], in1=pb[2][:],
                              op0=ALU.mult, op1=ALU.mult),
                           reads=[("xn1", dt, cs.start), ("gam",), ("pbank", 2)],
                           writes=[("sa",)], name=f"qm{ch}_{dt}")
                    P.step("gpsimd", lambda e, dt=dt, ch=ch: e.dma_start(
                              out=outs["xq_T"][dt * 128:(dt + 1) * 128,
                                               ch * TKC:(ch + 1) * TKC],
                              in_=sc["sa"][:]),
                           reads=[("sa",)], writes=[("st_xq", dt, ch)], dma=True,
                           name=f"stxq{ch}_{dt}")
            for i in range(8):
                P.step("gpsimd", lambda e, i=i: e.dma_start(
                          out=outs["x3"][i * 128:(i + 1) * 128, :], in_=XR[i][:]),
                       reads=[("XR", i)], writes=[("st_x1", i)], dma=True, name=f"stx3_{i}")
            fin_reads = ([("st_x1", i) for i in range(8)] +
                         [("st_xq", dt, ch) for dt in range(8) for ch in range(NCH)])
            P.step("sync", None, reads=fin_reads, name="fin")
            return

        def put_xn2(dt, ch):
            cs = slice(ch * TKC, (ch + 1) * TKC)
            P.step("vector", lambda e, dt=dt: e.tensor_add(
                      sc["sq"][:], sc["ap1"][:], pb[3][:]),
                   reads=[("ap1",), ("pbank", 3)], writes=[("sq",)], name=f"xn2f_{ch}_{dt}")
            P.step("scalar", lambda e, dt=dt, cs=cs: e.copy(out=xn1[dt][:, cs], in_=sc["sq"][:]),
                   reads=[("sq",)], writes=[("xn1", dt, cs.start)], name=f"xn2b_{ch}_{dt}")
            P.step("gpsimd", lambda e, dt=dt, ch=ch: e.dma_start(
                      out=outs["xn2"][dt * 128:(dt + 1) * 128, ch * TKC:(ch + 1) * TKC],
                      in_=sc["sq"][:]),
                   reads=[("sq",)], writes=[("st_xn2", dt, ch)], dma=True,
                   name=f"stxn2_{ch}_{dt}")
        ln_pass("ln2", put_xn2)

        wxpb = sbuf(nc, es, "wxpb", [96, 1024], BF16)
        P.load(wxpb[:], ins["wxp"], writes=[("wxpb",)], q="gpsimd", name="ldwxp")
        xpjT = sbuf(nc, es, "xpjT", [128, 8 * 96], BF16)
        pT = psum(nc, es, "pT", [128, 128], BF16)
        for ko in range(8):
            P.step("tensor", lambda e, ko=ko: e.transpose(
                      pT[:, 0:96], wxpb[:, ko * 128:(ko + 1) * 128], idtb[0:96, 0:96]),
                   reads=[("wxpb",), ("idtb",)], writes=[("pT",)], name=f"trxp{ko}")
            P.step("scalar", lambda e, ko=ko: e.copy(out=xpjT[:, ko * 96:(ko + 1) * 96],
                                                     in_=pT[:, 0:96]),
                   reads=[("pT",)], writes=[("xpjT", ko)], name=f"evxp{ko}")
        xpT = sbuf(nc, es, "xpT", [96, 512], F32)
        xpdt = sbuf(nc, es, "xpdt", [64, 512], BF16)
        zrow = sbuf(nc, es, "zrow", [16, 512], BF16)
        P.step("gpsimd", lambda e: e.memset(zrow[:], 0.0), writes=[("zrow",)], name="z0")
        wdtb = sbuf(nc, es, "wdtb", [128, 8 * 64], BF16)
        P.load(wdtb[:].rearrange("p (a r) -> p a r", a=8),
               ins["wdt"].rearrange("(a p) r -> p a r", p=128),
               writes=[("wdtb",)], q="gpsimd", name="ldwdt")
        wdtT = sbuf(nc, es, "wdtT", [64, 1024], BF16)
        for dt in range(8):
            P.step("tensor", lambda e, dt=dt: e.transpose(
                      pT[0:64, 0:128], wdtb[:, dt * 64:(dt + 1) * 64], idtb[:]),
                   reads=[("wdtb",), ("idtb",)], writes=[("pT",)], name=f"trdt{dt}")
            P.step("scalar", lambda e, dt=dt: e.copy(out=wdtT[0:64, dt * 128:(dt + 1) * 128],
                                                     in_=pT[0:64, 0:128]),
                   reads=[("pT",)], writes=[("wdtT", dt)], name=f"evdt{dt}")
        r1 = sc["ap1"]
        r2 = sc["sq"]
        for ch in range(NCH):
            cs = slice(ch * TKC, (ch + 1) * TKC)
            def mmxp(e, cs=cs):
                for ko in range(8):
                    i = e.matmul(pb[0][0:96, :], xpjT[:, ko * 96:(ko + 1) * 96],
                                 xn1[ko][:, cs], start=(ko == 0), stop=(ko == 7))
                return i
            P.step("tensor", mmxp,
                   reads=[("xpjT", ko) for ko in range(8)] +
                         [("xn1", ko, cs.start) for ko in range(8)],
                   writes=[("pbank", 0)], name=f"mmxp{ch}")
            P.step("scalar", lambda e: e.copy(out=xpT[:], in_=pb[0][0:96, :]),
                   reads=[("pbank", 0)], writes=[("xpT",)], name=f"evxpT{ch}")
            P.step("scalar", lambda e: e.copy(out=xpdt[:], in_=xpT[0:64, :]),
                   reads=[("xpT",)], writes=[("xpdt",)], name=f"cxpdt{ch}")
            P.step("gpsimd", lambda e, cs=cs: e.dma_start(
                      out=outs["bc0"][0:16, cs], in_=xpT[64:80, :]),
                   reads=[("xpT",)], writes=[("st_b", cs.start)], dma=True, name=f"stB{ch}")
            P.step("gpsimd", lambda e, cs=cs: e.dma_start(
                      out=outs["bc0"][32:48, cs], in_=xpT[80:96, :]),
                   reads=[("xpT",)], writes=[("st_c", cs.start)], dma=True, name=f"stC{ch}")
            P.step("gpsimd", lambda e, cs=cs: e.dma_start(
                      out=outs["bc0"][16:32, cs], in_=zrow[:]),
                   reads=[("zrow",)], writes=[("st_z", cs.start)], dma=True, name=f"stZ{ch}")
            for mo in range(8):
                P.step("tensor", lambda e, mo=mo: e.matmul(
                          pb[1][:], wdtT[:, mo * 128:(mo + 1) * 128], xpdt[:],
                          start=True, stop=True),
                       reads=[("wdtT", mo), ("xpdt",)], writes=[("pbank", 1)],
                       name=f"mmdt{mo}_{ch}")
                P.step("scalar", lambda e, mo=mo: e.activation(
                          out=r1[:], in_=pb[1][:], func=AF.Relu, bias=sc["bdt"][:, mo:mo + 1]),
                       reads=[("pbank", 1), ("bdt",)], writes=[("ap1",)], name=f"d1_{mo}_{ch}")
                P.step("scalar", lambda e, mo=mo: e.activation(
                          out=r2[:], in_=pb[1][:], func=AF.Abs, bias=sc["bdt"][:, mo:mo + 1]),
                       reads=[("pbank", 1), ("bdt",)], writes=[("sq",)], name=f"d2_{mo}_{ch}")
                P.step("scalar", lambda e: e.activation(out=r2[:], in_=r2[:], func=AF.Exp, scale=-1.0),
                       reads=[("sq",)], writes=[("sq",)], name=f"d3_{mo}_{ch}")
                P.step("scalar", lambda e: e.activation(out=r2[:], in_=r2[:], func=AF.Ln, bias=1.0),
                       reads=[("sq",)], writes=[("sq",)], name=f"d4_{mo}_{ch}")
                P.step("vector", lambda e: e.tensor_add(r2[:], r2[:], r1[:]),
                       reads=[("sq",), ("ap1",)], writes=[("sq",)], name=f"d5_{mo}_{ch}")
                P.step("gpsimd", lambda e, mo=mo, ch=ch: e.dma_start(
                          out=outs["delta0"][mo * 128:(mo + 1) * 128, ch * TKC:(ch + 1) * TKC],
                          in_=r2[:]),
                       reads=[("sq",)], writes=[("st_d", mo, ch)], dma=True,
                       name=f"std{mo}_{ch}")


        for i in range(8):
            P.step("gpsimd", lambda e, i=i: e.dma_start(
                      out=outs["x1"][i * 128:(i + 1) * 128, :], in_=XR[i][:]),
                   reads=[("XR", i)], writes=[("st_x1", i)], dma=True, name=f"stx1_{i}")

        fin_reads = ([("st_x1", i) for i in range(8)] + [("st_b", k * 512) for k in range(4)] + [("st_c", k * 512) for k in range(4)] + [("st_z", k * 512) for k in range(4)] +
                     [("st_xn2", dt, ch) for dt in range(8) for ch in range(NCH)] +
                     [("st_d", mo, ch) for mo in range(8) for ch in range(NCH)])
        P.step("sync", None, reads=fin_reads, name="fin")
    return bld


def run_B(inputs, prep):
    in_specs = {
        "x_T": ((DM, TL), F32),
        "w1t": ((32, 128, 1024), BF16), "w2t": ((32, 128, 1024), BF16),
        "h1t": ((32, 128, 4096), BF16), "h2t": ((32, 128, 4096), BF16),
        "w3t": ((8, 128, 4096), BF16),
        "w1s": ((128, 32), F32), "w2s": ((128, 32), F32),
        "h1s": ((128, 32), F32), "h2s": ((128, 32), F32), "w3s": ((128, 8), F32),
        "wxp": ((96, 1024), F32), "wdt": ((1024, 64), F32), "bdt": ((128, 8), F32),
        "idty": ((128, 128), BF16),
        "ones_col": ((128, 1), F32), "ones_row": ((1, 128), F32),
    }
    out_specs = {
        "x1": ((DM, TL), F32), "xn2": ((DM, TL), F32),
        "delta0": ((DM, TL), F32), "bc0": ((48, TL), F32),
    }
    nc = mkprog(build_B("B"), in_specs, out_specs)
    xs = _toks(inputs["L0_attn_x"])
    shared = {
        "w1t": prep["w1t"], "w2t": prep["w2t"], "h1t": prep["h1t"],
        "h2t": prep["h2t"], "w3t": prep["w3t"],
        "w1s": _colscale(inputs["L0_ffn_w1_s"]), "w2s": _colscale(inputs["L0_ffn_w2_s"]),
        "h1s": _colscale(inputs["L0_ffn_h1_s"]), "h2s": _colscale(inputs["L0_ffn_h2_s"]),
        "w3s": _colscale(inputs["L0_ffn_w3_s"]),
        "wxp": inputs["L0_attn_W_xproj"].astype(np.float32),
        "wdt": inputs["L0_attn_W_dt"].astype(np.float32),
        "bdt": _colscale(inputs["L0_attn_b_dt"]),
        "idty": IDTY_BF, "ones_col": ONES_COL, "ones_row": ONES_ROW,
    }
    in_maps = [dict(shared, x_T=np.ascontiguousarray(xs[c].T)) for c in range(NCORE)]
    res = run_bass_kernel_spmd(nc, in_maps, core_ids=list(range(NCORE)))
    return [res.results[c] for c in range(NCORE)]



def run_D(inputs, prep, b_outs, y_tok):
    in_specs = {
        "x1_T": ((DM, TL), F32), "y_T": ((DM, TL), F32),
        "w1t": ((32, 128, 1024), BF16), "w2t": ((32, 128, 1024), BF16),
        "h1t": ((32, 128, 4096), BF16), "h2t": ((32, 128, 4096), BF16),
        "w3t": ((8, 128, 4096), BF16),
        "w1s": ((128, 32), F32), "w2s": ((128, 32), F32),
        "h1s": ((128, 32), F32), "h2s": ((128, 32), F32), "w3s": ((128, 8), F32),
        "gam": ((128, 8), F32),
        "idty": ((128, 128), BF16),
        "ones_col": ((128, 1), F32), "ones_row": ((1, 128), F32),
    }
    out_specs = {"x3": ((DM, TL), F32), "xq_T": ((DM, TL), BF16)}
    nc = mkprog(build_B("D"), in_specs, out_specs)
    shared = {
        "w1t": prep["w1t"], "w2t": prep["w2t"], "h1t": prep["h1t"],
        "h2t": prep["h2t"], "w3t": prep["w3t"],
        "w1s": _colscale(inputs["L0_ffn_w1_s"]), "w2s": _colscale(inputs["L0_ffn_w2_s"]),
        "h1s": _colscale(inputs["L0_ffn_h1_s"]), "h2s": _colscale(inputs["L0_ffn_h2_s"]),
        "w3s": _colscale(inputs["L0_ffn_w3_s"]),
        "gam": _colscale(inputs["L1_ffn_norm_gamma"]),
        "idty": IDTY_BF, "ones_col": ONES_COL, "ones_row": ONES_ROW,
    }
    in_maps = [dict(shared, x1_T=b_outs[c]["x1"], y_T=y_tok[c]) for c in range(NCORE)]
    res = run_bass_kernel_spmd(nc, in_maps, core_ids=list(range(NCORE)))
    return [res.results[c] for c in range(NCORE)]


def build_C():
    TC = 1024
    NOC = (NB * SEQ) // TC

    def bld(nc, P, ins, outs, es):
        sel = sbuf(nc, es, "s_sel", [48, 16 * 128], F32)
        P.load(sel[:], ins["sel16"], writes=[("sel",)])
        idtb = sbuf(nc, es, "s_idtb", [128, 128], BF16)
        P.load(idtb[:], ins["idty"], writes=[("idtb",)])
        alog = sbuf(nc, es, "s_alog", [128, 16], F32)
        P.load(alog[:], ins["alog"], writes=[("alog",)])
        dsl = sbuf(nc, es, "s_dsl", [128, 1], F32)
        P.load(dsl[:], ins["dsl"], writes=[("dsl",)])
        A = sbuf(nc, es, "A", [128, 16], F32)
        P.step("scalar", lambda e: e.activation(out=A[:], in_=alog[:], func=AF.Exp),
               reads=[("alog",)], writes=[("A",)], name="expA")
        P.step("vector", lambda e: e.tensor_scalar_mul(A[:], A[:], -1.0),
               reads=[("A",)], writes=[("A",)], name="negA")
        carry = sbuf(nc, es, "carry", [128, 16], F32)
        zcol = sbuf(nc, es, "zcol", [128, 1], F32)
        P.step("gpsimd", lambda e: e.memset(zcol[:], 0.0), writes=[("zcol",)], name="z")
        uc = [sbuf(nc, es, f"uc{i}", [128, TC], F32) for i in range(2)]
        dc = [sbuf(nc, es, f"dc{i}", [128, TC], F32) for i in range(2)]
        bcc = [sbuf(nc, es, f"bcc{i}", [48, TC], F32) for i in range(2)]
        wcb = sbuf(nc, es, "wcb", [128, TC], BF16)
        Bbs = [sbuf(nc, es, f"Bbs{i}", [128, TC], BF16) for i in range(3)]
        Cbs = [sbuf(nc, es, f"Cbs{i}", [128, TC], BF16) for i in range(3)]
        dac = [sbuf(nc, es, f"dac{i}", [128, TC], BF16) for i in range(4)]
        bbc = [sbuf(nc, es, f"bbc{i}", [128, TC], BF16) for i in range(4)]
        hcb = [sbuf(nc, es, f"hcb{i}", [128, TC], BF16) for i in range(4)]
        ymb = [sbuf(nc, es, f"ymb{i}", [128, TC], BF16) for i in range(4)]
        yac = sbuf(nc, es, "yac", [128, TC], F32)
        ymf = sbuf(nc, es, "ymf", [128, TC], F32)
        pbb = psum(nc, es, "pbb", [128, TC], F32)
        pbc = psum(nc, es, "pbc", [128, TC], F32)
        pyy = psum(nc, es, "pyy", [128, TC], F32)

        for oc in range(NOC):
            s = oc % 2
            cs = slice(oc * TC, (oc + 1) * TC)
            P.load(uc[s][:], ins["u0"][:, cs], writes=[("uc", s)], name=f"ldu{oc}")
            P.load(dc[s][:], ins["d0"][:, cs], writes=[("dc", s)], name=f"ldd{oc}")
            P.load(bcc[s][:], ins["bc"][:, cs], writes=[("bcc", s)], name=f"ldbc{oc}")
            P.step("vector", lambda e, s=s: e.tensor_mul(wcb[:], dc[s][:], uc[s][:]),
                   reads=[("dc", s), ("uc", s)], writes=[("wcb",)], name=f"w{oc}")

            def emit_bcast(n):
                p = n % 3
                def mmb(e, s=s, n=n):
                    e.matmul(pbb[:, 0:512], sel[0:16, n * 128:(n + 1) * 128],
                             bcc[s][0:16, 0:512], start=True, stop=True)
                    return e.matmul(pbb[:, 512:1024], sel[0:16, n * 128:(n + 1) * 128],
                                    bcc[s][0:16, 512:1024], start=True, stop=True)
                P.step("tensor", mmb, reads=[("sel",), ("bcc", s)], writes=[("psB",)],
                       name=f"bb{oc}_{n}")
                P.step("scalar", lambda e, p=p: e.copy(out=Bbs[p][:], in_=pbb[:]),
                       reads=[("psB",)], writes=[("Bbs", p)], name=f"be{oc}_{n}")
                def mmc(e, s=s, n=n):
                    e.matmul(pbc[:, 0:512], sel[32:48, n * 128:(n + 1) * 128],
                             bcc[s][32:48, 0:512], start=True, stop=True)
                    return e.matmul(pbc[:, 512:1024], sel[32:48, n * 128:(n + 1) * 128],
                                    bcc[s][32:48, 512:1024], start=True, stop=True)
                P.step("tensor", mmc, reads=[("sel",), ("bcc", s)], writes=[("psC",)],
                       name=f"cb{oc}_{n}")
                P.step("scalar", lambda e, p=p: e.copy(out=Cbs[p][:], in_=pbc[:]),
                       reads=[("psC",)], writes=[("Cbs", p)], name=f"ce{oc}_{n}")
                q = n % 4
                P.step("scalar", lambda e, s=s, n=n, q=q: e.activation(
                          out=dac[q][:], in_=dc[s][:], func=AF.Exp, scale=A[:, n:n + 1]),
                       reads=[("dc", s), ("A",)], writes=[("dac", q)], name=f"da{oc}_{n}")

            emit_bcast(0)
            for n in range(16):
                p = n % 3
                q = n % 4
                if n + 1 < 16:
                    emit_bcast(n + 1)
                P.step("vector", lambda e, p=p, q=q: e.tensor_mul(
                          bbc[q][:], wcb[:], Bbs[p][:]),
                       reads=[("wcb",), ("Bbs", p)], writes=[("bbc", q)], name=f"bB{oc}_{n}")
                if oc % (NOC // 2) == 0:
                    init, ikey = zcol[:], ("zcol",)
                else:
                    init, ikey = carry[:, n:n + 1], ("carry", n)
                P.step("vector", lambda e, init=init, q=q: e.tensor_tensor_scan(
                          out=hcb[q][:], data0=dac[q][:], data1=bbc[q][:], initial=init,
                          op0=ALU.mult, op1=ALU.add),
                       reads=[("dac", q), ("bbc", q), ikey], writes=[("hcb", q)],
                       name=f"h{oc}_{n}")
                P.step("vector", lambda e, n=n, q=q: e.tensor_copy(
                          out=carry[:, n:n + 1], in_=hcb[q][:, TC - 1:TC]),
                       reads=[("hcb", q)], writes=[("carry", n)], name=f"cy{oc}_{n}")
                P.step("vector", lambda e, p=p, q=q: e.tensor_mul(
                          ymb[q][:], hcb[q][:], Cbs[p][:]),
                       reads=[("hcb", q), ("Cbs", p)], writes=[("ymb", q)],
                       name=f"ym{oc}_{n}")
                def mmy(e, q=q, n=n):
                    e.matmul(pyy[:, 0:512], idtb[:], ymb[q][:, 0:512],
                             start=(n == 0), stop=(n == 15))
                    return e.matmul(pyy[:, 512:1024], idtb[:], ymb[q][:, 512:1024],
                                    start=(n == 0), stop=(n == 15))
                P.step("tensor", mmy, reads=[("ymb", q), ("idtb",)],
                       writes=[("psY",)] if n in (0, 15) else [], name=f"ya{oc}_{n}")
            P.step("scalar", lambda e: e.copy(out=yac[:], in_=pyy[:]),
                   reads=[("psY",)], writes=[("yac",)], name=f"ye{oc}")
            P.step("vector", lambda e, s=s: e.tensor_scalar_mul(ymf[:], uc[s][:], dsl[:, 0:1]),
                   reads=[("uc", s), ("dsl",)], writes=[("ymf",)], name=f"ud{oc}")
            P.step("vector", lambda e: e.tensor_add(yac[:], yac[:], ymf[:]),
                   reads=[("ymf",), ("yac",)], writes=[("yac",)], name=f"yd{oc}")
            P.step("gpsimd", lambda e, cs=cs: e.dma_start(out=outs["y0"][:, cs], in_=yac[:]),
                   reads=[("yac",)], writes=[("st", oc % 4)], dma=True, name=f"st{oc}")
        P.step("sync", None, reads=[("st", i) for i in range(4)], name="fin")
    return bld


SEL16 = np.zeros((48, 16 * 128), np.float32)
for _n in range(16):
    SEL16[_n, _n * 128:(_n + 1) * 128] = 1.0
    SEL16[32 + _n, _n * 128:(_n + 1) * 128] = 1.0


def run_C(b_outs, inputs):
    TT_ = NB * SEQ
    in_specs = {
        "u0": ((128, TT_), F32), "d0": ((128, TT_), F32), "bc": ((48, TT_), F32),
        "alog": ((128, 16), F32), "dsl": ((128, 1), F32),
        "sel16": ((48, 16 * 128), F32), "idty": ((128, 128), BF16),
    }
    out_specs = {"y0": ((128, TT_), F32)}
    nc = mkprog(build_C(), in_specs, out_specs)
    ufull = np.concatenate([o["xn2"][:, :SL] for o in b_outs] +
                           [o["xn2"][:, SL:] for o in b_outs], axis=1)
    dfull = np.concatenate([o["delta0"][:, :SL] for o in b_outs] +
                           [o["delta0"][:, SL:] for o in b_outs], axis=1)
    bcfull = np.concatenate([o["bc0"][:, :SL] for o in b_outs] +
                            [o["bc0"][:, SL:] for o in b_outs], axis=1)
    alog = inputs["L0_attn_A_log"].astype(np.float32)
    dvec = inputs["L0_attn_D"].astype(np.float32)
    in_maps = []
    for c in range(NCORE):
        rs = slice(c * 128, (c + 1) * 128)
        in_maps.append({
            "u0": np.ascontiguousarray(ufull[rs]),
            "d0": np.ascontiguousarray(dfull[rs]),
            "bc": bcfull, "alog": alog[rs], "dsl": dvec[rs, None],
            "sel16": SEL16, "idty": IDTY_BF,
        })
    res = run_bass_kernel_spmd(nc, in_maps, core_ids=list(range(NCORE)))
    yfull = np.concatenate([res.results[c]["y0"] for c in range(NCORE)], axis=0)
    y_tok = []
    for c in range(NCORE):
        y_tok.append(np.ascontiguousarray(np.concatenate(
            [yfull[:, c * SL:(c + 1) * SL],
             yfull[:, SEQ + c * SL:SEQ + (c + 1) * SL]], axis=1)))
    return y_tok



def build_E():
    TC = 512
    NCHT = (NB * SEQ) // TC
    CPB = NCHT // NB

    def bld(nc, P, ins, outs, es):
        idtb = sbuf(nc, es, "s_idtb", [128, 128], BF16)
        idtf = sbuf(nc, es, "s_idtf", [128, 128], F32)
        P.load(idtb[:], ins["idty"], writes=[("idtb",)])
        P.load(idtf[:], ins["idtf"], writes=[("idtf",)])
        sgam = sbuf(nc, es, "s_sgam", [128, 8], F32)
        P.load(sgam[:], ins["sgam"], writes=[("sgam",)])
        b1s = sbuf(nc, es, "s_b1s", [128, 32], F32)
        P.load(b1s[:], ins["b1s"], writes=[("b1s",)])
        b2s = sbuf(nc, es, "s_b2s", [128, 8], F32)
        P.load(b2s[:], ins["b2s"], writes=[("b2s",)])
        pba = [psum(nc, es, f"pba{i}", [128, 512], F32) for i in range(6)]
        pbt = [psum(nc, es, f"pbt{i}", [128, 128], BF16) for i in range(2)]
        lg = nc.dram_tensor("lg", [512, NB * SEQ], F32, kind="Internal").ap()

        snT = [sbuf(nc, es, f"snT{i}", [128, 512], BF16) for i in range(8)]
        se = sbuf(nc, es, "se", [128, 1024], F32)
        seb = sbuf(nc, es, "seb", [128, 1024], BF16)
        srow = sbuf(nc, es, "srow", [128, 1], F32)
        ssq = sbuf(nc, es, "ssq", [128, 1024], F32)
        for st in range(4):
            P.load(se[:], ins["semb"].rearrange("(a p) d -> a p d", p=128)[st],
                   writes=[("se",)], name=f"ldse{st}")
            P.step("scalar", lambda e: e.square(out=ssq[:], in_=se[:]),
                   reads=[("se",)], writes=[("ssq",)], name=f"ssq{st}")
            P.step("vector", lambda e: e.reduce_sum(srow[:], ssq[:], axis=mybir.AxisListType.X),
                   reads=[("ssq",)], writes=[("srow",)], name=f"snrm{st}")
            P.step("scalar", lambda e: e.activation(out=srow[:], in_=srow[:], func=AF.Ln),
                   reads=[("srow",)], writes=[("srow",)], name=f"snl{st}")
            P.step("scalar", lambda e: e.activation(out=srow[:], in_=srow[:], func=AF.Exp, scale=-0.5),
                   reads=[("srow",)], writes=[("srow",)], name=f"sne{st}")
            P.step("vector", lambda e: e.tensor_scalar_mul(srow[:], srow[:], float(np.sqrt(DM))),
                   reads=[("srow",)], writes=[("srow",)], name=f"sn32{st}")
            P.step("vector", lambda e: e.tensor_scalar_mul(seb[:], se[:], srow[:, 0:1]),
                   reads=[("se",), ("srow",)], writes=[("seb",)], name=f"snn{st}")
            for dt in range(8):
                P.step("tensor", lambda e, dt=dt: e.transpose(
                          pbt[0][:], seb[:, dt * 128:(dt + 1) * 128], idtb[:]),
                       reads=[("seb",), ("idtb",)], writes=[("pbt", 0)], name=f"snt{st}_{dt}")
                P.step("scalar", lambda e, dt=dt, st=st: e.activation(
                          out=snT[dt][:, st * 128:(st + 1) * 128], in_=pbt[0][:],
                          func=AF.Copy, scale=sgam[:, dt:dt + 1]),
                       reads=[("pbt", 0), ("sgam",)], writes=[("snT", dt, st)],
                       name=f"sne{st}_{dt}")

        xqt = [[sbuf(nc, es, f"xqt{r}_{i}", [128, 512], BF16) for i in range(8)]
               for r in range(2)]
        lgb = [sbuf(nc, es, f"lgb{r}", [128, 512], F32) for r in range(4)]
        rmax = sbuf(nc, es, "rmax", [128, 8], F32)
        tcol = [sbuf(nc, es, f"tcol{r}", [128, 1], F32) for r in range(4)]

        def load_xqt(ch):
            r = ch % 2
            cs = slice(ch * TC, (ch + 1) * TC)
            for ko in range(8):
                P.load(xqt[r][ko][:], ins["xq_T"].rearrange("(a p) t -> a p t", p=128)[ko, :, cs],
                       writes=[("xqt", r, ko)], name=f"lqt{ch}_{ko}")
        load_xqt(0)
        for ch in range(NCHT):
            b = ch // CPB
            r = ch % 2
            cs = slice(ch * TC, (ch + 1) * TC)
            if ch + 1 < NCHT:
                load_xqt(ch + 1)
            for et in range(4):
                def mml(e, et=et, r=r):
                    for ko in range(8):
                        i = e.matmul(pba[et % 2][:], snT[ko][:, et * 128:(et + 1) * 128],
                                     xqt[r][ko][:], start=(ko == 0), stop=(ko == 7))
                    return i
                P.step("tensor", mml,
                       reads=[("snT", ko, st) for ko in range(8) for st in range(4)] +
                             [("xqt", r, ko) for ko in range(8)],
                       writes=[("pbank", et % 2)], name=f"lmm{ch}_{et}")
                P.step("scalar", lambda e, et=et: e.copy(out=lgb[et][:], in_=pba[et % 2][:]),
                       reads=[("pbank", et % 2)], writes=[("lgb", et)], name=f"lev{ch}_{et}")
                P.step("vector", lambda e, et=et: e.reduce_max(
                          tcol[et][:], lgb[et][:], axis=mybir.AxisListType.X),
                       reads=[("lgb", et)], writes=[("tcol", et)], name=f"lmx{ch}_{et}")
                col = 2 * et + b
                if ch % CPB == 0:
                    P.step("vector", lambda e, col=col, et=et: e.tensor_copy(
                              out=rmax[:, col:col + 1], in_=tcol[et][:]),
                           reads=[("tcol", et)], writes=[("rmax", col)], name=f"lrm{ch}_{et}")
                else:
                    P.step("vector", lambda e, col=col, et=et: e.tensor_max(
                              rmax[:, col:col + 1], rmax[:, col:col + 1], tcol[et][:]),
                           reads=[("tcol", et), ("rmax", col)], writes=[("rmax", col)],
                           name=f"lrm{ch}_{et}")
                P.step("gpsimd", lambda e, et=et, cs=cs: e.dma_start(
                          out=lg[et * 128:(et + 1) * 128, cs], in_=lgb[et][:]),
                       reads=[("lgb", et)], writes=[("st_lg", et)], dma=True,
                       name=f"lst{ch}_{et}")

        nrmax = sbuf(nc, es, "nrmax", [128, 8], F32)
        rsum = sbuf(nc, es, "rsum", [128, 8], F32)
        lgc = [sbuf(nc, es, f"lgc{i}", [128, 512], F32) for i in range(4)]
        ec = [sbuf(nc, es, f"ec{r}", [128, 512], BF16) for r in range(2)]
        dT = [sbuf(nc, es, f"dT{i}", [128, 512], BF16) for i in range(4)]
        xqn = [sbuf(nc, es, f"xqn{i}", [128, 1024], BF16) for i in range(4)]
        slots = [sbuf(nc, es, f"slots{i}", [128, 512], F32) for i in range(8)]
        sT = [sbuf(nc, es, f"sT{i}", [128, 512], BF16) for i in range(8)]
        hT = [sbuf(nc, es, f"hT{i}", [128, 512], BF16) for i in range(32)]
        eon = [sbuf(nc, es, f"eon{i}", [128, 1024], BF16) for i in range(4)]
        wf = [sbuf(nc, es, f"wf{i}", [128, 4096], F32) for i in range(2)]
        wb = [sbuf(nc, es, f"wb{i}", [128, 4096], BF16) for i in range(2)]
        rrow = sbuf(nc, es, "rrow", [1, 512], F32)
        onesr = sbuf(nc, es, "onesr", [1, 128], F32)
        P.load(onesr[:], ins["ones_row"], writes=[("ones_row",)])
        lgn = [sbuf(nc, es, f"lgn{i}", [128, 512], F32) for i in range(4)]
        cnT = [sbuf(nc, es, f"cnT{i}", [128, 512], BF16) for i in range(4)]
        cn = [sbuf(nc, es, f"cn{r}", [128, 512], BF16) for r in range(2)]
        ncb = sbuf(nc, es, "ncb", [128, 512], BF16)
        d2sb = sbuf(nc, es, "d2sb", [128, 128], F32)
        mx2sb = sbuf(nc, es, "mx2sb", [128, 128], F32)

        for b in range(NB):
            P.step("vector", lambda e: e.tensor_scalar_mul(nrmax[:], rmax[:], -1.0),
                   reads=[("rmax", c) for c in range(8)], writes=[("nrmax",)],
                   name=f"nrm{b}")
            for ci in range(CPB):
                ch = b * CPB + ci
                cs = slice(ch * TC, (ch + 1) * TC)
                for ts in range(4):
                    P.load(xqn[ts][:], ins["xq_nat"].rearrange("(c p) d -> c p d", p=128)[4 * ch + ts],
                           writes=[("xqn", ts)], name=f"dqn{ch}_{ts}")
                for et in range(4):
                    P.load(lgc[et][:], lg[et * 128:(et + 1) * 128, cs],
                           reads=[("st_lg", et)], writes=[("lgc", et)], name=f"dlg{ch}_{et}")
                    col = 2 * et + b
                    er = et % 2
                    P.step("scalar", lambda e, et=et, col=col, er=er: e.activation(
                              out=ec[er][:], in_=lgc[et][:], func=AF.Exp,
                              bias=nrmax[:, col:col + 1]),
                           reads=[("lgc", et), ("nrmax",)], writes=[("ec", er)],
                           name=f"dex{ch}_{et}")
                    P.step("vector", lambda e, et=et, er=er: e.reduce_sum(
                              tcol[et][:], ec[er][:], axis=mybir.AxisListType.X),
                           reads=[("ec", er)], writes=[("tcol", et)], name=f"dsm{ch}_{et}")
                    if ci == 0:
                        P.step("vector", lambda e, col=col, et=et: e.tensor_copy(
                                  out=rsum[:, col:col + 1], in_=tcol[et][:]),
                               reads=[("tcol", et)], writes=[("rsum", col)], name=f"drs{ch}_{et}")
                    else:
                        P.step("vector", lambda e, col=col, et=et: e.tensor_add(
                                  rsum[:, col:col + 1], rsum[:, col:col + 1], tcol[et][:]),
                               reads=[("tcol", et), ("rsum", col)], writes=[("rsum", col)],
                               name=f"drs{ch}_{et}")
                    for ts in range(4):
                        P.step("tensor", lambda e, ts=ts, er=er: e.transpose(
                                  pbt[ts % 2][:], ec[er][:, ts * 128:(ts + 1) * 128], idtb[:]),
                               reads=[("ec", er), ("idtb",)], writes=[("pbt", ts % 2)],
                               name=f"dtr{ch}_{et}_{ts}")
                        P.step("scalar", lambda e, ts=ts, et=et: e.copy(
                                  out=dT[ts][:, et * 128:(et + 1) * 128], in_=pbt[ts % 2][:]),
                               reads=[("pbt", ts % 2)], writes=[("dT", ts, et)],
                               name=f"dev{ch}_{et}_{ts}")
                for dt in range(8):
                    bank = 2 + (dt % 2)
                    def mms(e, dt=dt, bank=bank):
                        for ts in range(4):
                            i = e.matmul(pba[bank][:], xqn[ts][:, dt * 128:(dt + 1) * 128],
                                         dT[ts][:], start=(ts == 0), stop=(ts == 3))
                        return i
                    P.step("tensor", mms,
                           reads=[("xqn", ts) for ts in range(4)] +
                                 [("dT", ts, et) for ts in range(4) for et in range(4)],
                           writes=[("pbank", bank)], name=f"dss{ch}_{dt}")
                    if ci == 0:
                        P.step("scalar", lambda e, dt=dt, bank=bank: e.copy(
                                  out=slots[dt][:], in_=pba[bank][:]),
                               reads=[("pbank", bank)], writes=[("slots", dt)],
                               name=f"dsa{ch}_{dt}")
                    else:
                        P.step("vector", lambda e, dt=dt, bank=bank: e.tensor_add(
                                  slots[dt][:], slots[dt][:], pba[bank][:]),
                               reads=[("pbank", bank), ("slots", dt)], writes=[("slots", dt)],
                               name=f"dsa{ch}_{dt}")
            P.step("vector", lambda e: e.reciprocal(nrmax[:], rsum[:]),
                   reads=[("rsum", c) for c in range(8)], writes=[("nrmax",)],
                   name=f"rcp{b}")
            for et in range(4):
                col = 2 * et + b
                P.step("tensor", lambda e, et=et, col=col: e.transpose(
                          pba[4][0:1, 0:128], nrmax[:, col:col + 1], idtf[:]),
                       reads=[("nrmax",), ("idtf",)], writes=[("pbank", 4)],
                       name=f"rtr{b}_{et}")
                P.step("scalar", lambda e, et=et: e.copy(
                          out=rrow[:, et * 128:(et + 1) * 128], in_=pba[4][0:1, 0:128]),
                       reads=[("pbank", 4)], writes=[("rrow", et)], name=f"rev{b}_{et}")
            P.step("tensor", lambda e: e.matmul(pba[5][:], onesr[:], rrow[:],
                                                start=True, stop=True),
                   reads=[("rrow", et) for et in range(4)] + [("ones_row",)],
                   writes=[("pbank", 5)], name=f"rbc{b}")
            for dt in range(8):
                P.step("vector", lambda e, dt=dt: e.tensor_mul(slots[dt][:], slots[dt][:], pba[5][:]),
                       reads=[("pbank", 5), ("slots", dt)], writes=[("slots", dt)],
                       name=f"rnm{b}_{dt}")
                P.step("scalar", lambda e, dt=dt: e.copy(out=sT[dt][:], in_=slots[dt][:]),
                       reads=[("slots", dt)], writes=[("sT", dt)], name=f"rcast{b}_{dt}")
            for hm in range(32):
                s = hm % 2
                P.load(wf[s][:, 0:1024], ins["w1e"][hm], writes=[("wf", s)], name=f"f1l{b}_{hm}")
                P.step("scalar", lambda e, s=s: e.copy(out=wb[s][:, 0:1024], in_=wf[s][:, 0:1024]),
                       reads=[("wf", s)], writes=[("wb", s)], name=f"f1c{b}_{hm}")
                def mmh(e, s=s, hm=hm):
                    for dt in range(8):
                        i = e.matmul(pba[4][:], wb[s][:, dt * 128:(dt + 1) * 128],
                                     sT[dt][:], start=(dt == 0), stop=(dt == 7))
                    return i
                P.step("tensor", mmh,
                       reads=[("wb", s)] + [("sT", dt) for dt in range(8)],
                       writes=[("pbank", 4)], name=f"f1m{b}_{hm}")
                P.step("scalar", lambda e, hm=hm: e.activation(
                          out=hT[hm][:], in_=pba[4][:], func=AF.Gelu,
                          bias=b1s[:, hm:hm + 1]),
                       reads=[("pbank", 4), ("b1s",)], writes=[("hT", hm)],
                       name=f"f1e{b}_{hm}")
            for dm in range(8):
                s = dm % 2
                P.load(wf[s][:], ins["w2e"][dm], writes=[("wf", s)], name=f"f2l{b}_{dm}")
                P.step("scalar", lambda e, s=s: e.copy(out=wb[s][:], in_=wf[s][:]),
                       reads=[("wf", s)], writes=[("wb", s)], name=f"f2c{b}_{dm}")
                def mme(e, s=s):
                    for hm in range(32):
                        i = e.matmul(pba[5][:], wb[s][:, hm * 128:(hm + 1) * 128],
                                     hT[hm][:], start=(hm == 0), stop=(hm == 31))
                    return i
                P.step("tensor", mme,
                       reads=[("wb", s)] + [("hT", hm) for hm in range(32)],
                       writes=[("pbank", 5)], name=f"f2m{b}_{dm}")
                P.step("vector", lambda e, dm=dm: e.tensor_scalar_add(
                          ncb[:], pba[5][:], b2s[:, dm:dm + 1]),
                       reads=[("pbank", 5), ("b2s",)], writes=[("ncb",)], name=f"f2e{b}_{dm}")
                for ts in range(4):
                    P.step("tensor", lambda e, ts=ts: e.transpose(
                              pbt[ts % 2][:], ncb[:, ts * 128:(ts + 1) * 128], idtb[:]),
                           reads=[("ncb",), ("idtb",)], writes=[("pbt", ts % 2)],
                           name=f"f2t{b}_{dm}_{ts}")
                    P.step("scalar", lambda e, ts=ts, dm=dm: e.copy(
                              out=eon[ts][:, dm * 128:(dm + 1) * 128], in_=pbt[ts % 2][:]),
                           reads=[("pbt", ts % 2)], writes=[("eon", ts, dm)],
                           name=f"f2v{b}_{dm}_{ts}")
            for ci in range(CPB):
                ch = b * CPB + ci
                cs = slice(ch * TC, (ch + 1) * TC)
                for et in range(4):
                    P.load(lgc[et][:], lg[et * 128:(et + 1) * 128, cs],
                           reads=[("st_lg", et)], writes=[("lgc", et)], name=f"clg{ch}_{et}")
                for ts in range(4):
                    for et in range(4):
                        P.step("tensor", lambda e, ts=ts, et=et: e.transpose(
                                  pba[4][:, 0:128], lgc[et][:, ts * 128:(ts + 1) * 128], idtf[:]),
                               reads=[("lgc", et), ("idtf",)], writes=[("pbank", 4)],
                               name=f"ctr{ch}_{ts}_{et}")
                        P.step("scalar", lambda e, ts=ts, et=et: e.copy(
                                  out=lgn[ts][:, et * 128:(et + 1) * 128], in_=pba[4][:, 0:128]),
                               reads=[("pbank", 4)], writes=[("lgn", ts, et)],
                               name=f"cev{ch}_{ts}_{et}")
                for ts in range(4):
                    tg = 4 * ch + ts
                    cr = ts % 2
                    P.step("vector", lambda e, ts=ts: e.reduce_max(
                              tcol[ts][:], lgn[ts][:], axis=mybir.AxisListType.X),
                           reads=[("lgn", ts, et) for et in range(4)], writes=[("tcol", ts)],
                           name=f"cmx{ch}_{ts}")
                    P.step("vector", lambda e, tg=tg, ts=ts: e.tensor_copy(
                              out=mx2sb[:, tg % 128:tg % 128 + 1], in_=tcol[ts][:]),
                           reads=[("tcol", ts)], writes=[("mx2", tg)], name=f"cmc{ch}_{ts}")
                    P.step("vector", lambda e, ts=ts: e.tensor_scalar_mul(
                              tcol[ts][:], tcol[ts][:], -1.0),
                           reads=[("tcol", ts)], writes=[("tcol", ts)], name=f"cmn{ch}_{ts}")
                    P.step("scalar", lambda e, ts=ts, cr=cr: e.activation(
                              out=cn[cr][:], in_=lgn[ts][:], func=AF.Exp, bias=tcol[ts][:, 0:1]),
                           reads=[("lgn", ts, et) for et in range(4)] + [("tcol", ts)],
                           writes=[("cn", cr)], name=f"cex{ch}_{ts}")
                    P.step("vector", lambda e, ts=ts, cr=cr: e.reduce_sum(
                              tcol[ts][:], cn[cr][:], axis=mybir.AxisListType.X),
                           reads=[("cn", cr)], writes=[("tcol", ts)], name=f"csm{ch}_{ts}")
                    P.step("vector", lambda e, tg=tg, ts=ts: e.tensor_copy(
                              out=d2sb[:, tg % 128:tg % 128 + 1], in_=tcol[ts][:]),
                           reads=[("tcol", ts)], writes=[("d2", tg)], name=f"cdc{ch}_{ts}")
                    for et in range(4):
                        P.step("tensor", lambda e, et=et, cr=cr: e.transpose(
                                  pbt[et % 2][:], cn[cr][:, et * 128:(et + 1) * 128], idtb[:]),
                               reads=[("cn", cr), ("idtb",)], writes=[("pbt", et % 2)],
                               name=f"cct{ch}_{ts}_{et}")
                        P.step("scalar", lambda e, ts=ts, et=et: e.copy(
                                  out=cnT[et][:, ts * 128:(ts + 1) * 128], in_=pbt[et % 2][:]),
                               reads=[("pbt", et % 2)], writes=[("cnT", et, ts)],
                               name=f"ccv{ch}_{ts}_{et}")
                for dm in range(8):
                    bank = 2 + (dm % 2)
                    def mmn(e, dm=dm, bank=bank):
                        for et in range(4):
                            i = e.matmul(pba[bank][:], eon[et][:, dm * 128:(dm + 1) * 128],
                                         cnT[et][:], start=(et == 0), stop=(et == 3))
                        return i
                    P.step("tensor", mmn,
                           reads=[("cnT", et, ts) for et in range(4) for ts in range(4)] +
                                 [("eon", et, dmx) for et in range(4) for dmx in range(8)],
                           writes=[("pbank", bank)], name=f"cnm{ch}_{dm}")
                    P.step("scalar", lambda e, bank=bank: e.copy(out=ncb[:], in_=pba[bank][:]),
                           reads=[("pbank", bank)], writes=[("ncb",)], name=f"cne{ch}_{dm}")
                    P.step("gpsimd", lambda e, dm=dm, cs=cs: e.dma_start(
                              out=outs["nT"][dm * 128:(dm + 1) * 128, cs], in_=ncb[:]),
                           reads=[("ncb",)], writes=[("st_n", dm)], dma=True,
                           name=f"cns{ch}_{dm}")
            P.step("gpsimd", lambda e, b=b: e.dma_start(
                      out=outs["d2"][:, b * 64:(b + 1) * 64], in_=d2sb[:, b * 64:(b + 1) * 64]),
                   reads=[("d2", tg) for tg in range(b * 64, (b + 1) * 64)],
                   writes=[("st_d2", b)], dma=True, name=f"std2{b}")
            P.step("gpsimd", lambda e, b=b: e.dma_start(
                      out=outs["mx2"][:, b * 64:(b + 1) * 64], in_=mx2sb[:, b * 64:(b + 1) * 64]),
                   reads=[("mx2", tg) for tg in range(b * 64, (b + 1) * 64)],
                   writes=[("st_m2", b)], dma=True, name=f"stm2{b}")
        P.step("sync", None,
               reads=[("st_n", dm) for dm in range(8)] +
                     [("st_d2", b) for b in range(NB)] + [("st_m2", b) for b in range(NB)],
               name="fin")
    return bld


def run_E(inputs, xq_T_full, layer_prefix="L1_ffn"):
    TT_ = NB * SEQ
    in_specs = {
        "xq_T": ((DM, TT_), BF16), "xq_nat": ((TT_, DM), BF16),
        "semb": ((512, DM), F32), "sgam": ((128, 8), F32),
        "w1e": ((32, 128, 1024), F32), "w2e": ((8, 128, 4096), F32),
        "b1s": ((128, 32), F32), "b2s": ((128, 8), F32),
        "idty": ((128, 128), BF16), "idtf": ((128, 128), F32),
        "ones_row": ((1, 128), F32),
    }
    out_specs = {"nT": ((DM, TT_), BF16), "d2": ((128, 128), F32),
                 "mx2": ((128, 128), F32)}
    nc = mkprog(build_E(), in_specs, out_specs)
    xq_nat = np.ascontiguousarray(xq_T_full.T)
    w1 = inputs[layer_prefix + "_w1"].astype(np.float32)
    w2 = inputs[layer_prefix + "_w2"].astype(np.float32)
    b1 = inputs[layer_prefix + "_b1"].astype(np.float32)
    b2 = inputs[layer_prefix + "_b2"].astype(np.float32)
    semb = inputs[layer_prefix + "_slot_embeds"].astype(np.float32)
    sgam = _colscale(inputs[layer_prefix + "_slot_norm_gamma"])
    in_maps = []
    for c in range(NCORE):
        in_maps.append({
            "xq_T": xq_T_full, "xq_nat": xq_nat,
            "semb": np.ascontiguousarray(semb[c]), "sgam": sgam,
            "w1e": np.ascontiguousarray(
                w1[c].reshape(8, 128, 32, 128).transpose(2, 1, 0, 3).reshape(32, 128, 1024)),
            "w2e": np.ascontiguousarray(
                w2[c].reshape(32, 128, 8, 128).transpose(2, 1, 0, 3).reshape(8, 128, 4096)),
            "b1s": _colscale(b1[c]), "b2s": _colscale(b2[c]),
            "idty": IDTY_BF, "idtf": IDTY, "ones_row": ONES_ROW,
        })
    res = run_bass_kernel_spmd(nc, in_maps, core_ids=list(range(NCORE)))
    return [res.results[c] for c in range(NCORE)]


def assemble_xq_full(d_outs, key="xq_T"):
    return np.ascontiguousarray(np.concatenate(
        [o[key][:, :SL] for o in d_outs] + [o[key][:, SL:] for o in d_outs], axis=1))



TP = 2176
NTS = TP // 128


def _own_segments(p0, p1):
    segs = []
    for lo, hi, off in ((3, 1027, -3), (1030, 2054, -6)):
        a, b = max(p0, lo), min(p1, hi)
        if a < b:
            segs.append((a, b - a, a + off))
    return segs


def build_FG():
    def bld(nc, P, ins, outs, es):
        idtb = sbuf(nc, es, "s_idtb", [128, 128], BF16)
        idtf = sbuf(nc, es, "s_idtf", [128, 128], F32)
        P.load(idtb[:], ins["idty"], writes=[("idtb",)])
        P.load(idtf[:], ins["idtf"], writes=[("idtf",)])
        cvw = sbuf(nc, es, "s_cvw", [128, 16 * 4], F32)
        P.load(cvw[:].rearrange("p (a k) -> p a k", a=16),
               ins["cvw"].rearrange("(a p) o k -> p a (o k)", p=128), writes=[("cvw",)])
        cvb = sbuf(nc, es, "s_cvb", [128, 16], F32)
        P.load(cvb[:], ins["cvb"], writes=[("cvb",)])
        bdt1 = sbuf(nc, es, "s_bdt1", [128, 16], F32)
        P.load(bdt1[:], ins["bdt1"], writes=[("bdt1",)])
        pba = [psum(nc, es, f"pba{i}", [128, 512], F32) for i in range(6)]
        pbt = [psum(nc, es, f"pbt{i}", [128, 128], BF16) for i in range(2)]

        m2s = sbuf(nc, es, "s_m2s", [128, NTS * 8], F32)
        P.load(m2s[:], ins["m2s"], writes=[("m2s",)])
        d2s = sbuf(nc, es, "s_d2s", [128, NTS * 8], F32)
        P.load(d2s[:], ins["d2s"], writes=[("d2s",)])
        ceps = sbuf(nc, es, "s_ceps", [128, 1], F32)
        P.step("gpsimd", lambda e: e.memset(ceps[:], EPS_LN), writes=[("ceps",)], name="ceps")
        ceps2 = sbuf(nc, es, "s_ceps2", [128, 1], F32)
        P.step("gpsimd", lambda e: e.memset(ceps2[:], float(np.finfo(np.float32).eps)),
               writes=[("ceps2",)], name="ceps2")
        nw = sbuf(nc, es, "s_nw", [128, 8], F32)
        P.load(nw[:], ins["nw"], writes=[("nw",)])
        sd8 = sbuf(nc, es, "sd8", [128, 8], F32)
        m2c = sbuf(nc, es, "m2c", [128, 1], F32)
        x3n = sbuf(nc, es, "s_x3n", [128, 1024], F32)
        ntile = sbuf(nc, es, "ntile", [128, 1024], BF16)
        acc = sbuf(nc, es, "acc", [128, 1024], F32)
        srow8 = sbuf(nc, es, "srow8", [128, 8], F32)
        mcol = sbuf(nc, es, "mcol", [128, 1], F32)
        dcol = sbuf(nc, es, "dcol", [128, 1], F32)
        sqn = sbuf(nc, es, "sqn", [128, 1024], F32)
        xn5n = sbuf(nc, es, "xn5n", [128, 1024], BF16)
        xn5T = [sbuf(nc, es, f"xn5T{i}", [128, TP], BF16) for i in range(8)]
        for ts in range(NTS):
            P.load(x3n[:], ins["x3n"].rearrange("(c p) d -> c p d", p=128)[ts],
                   writes=[("x3n",)], name=f"ldx3{ts}")
            P.step("vector", lambda e, ts=ts: e.reduce_max(
                      mcol[:], m2s[:, ts * 8:(ts + 1) * 8], axis=mybir.AxisListType.X),
                   reads=[("m2s",)], writes=[("mcol",)], name=f"fM{ts}")
            P.step("vector", lambda e: e.tensor_scalar_mul(mcol[:], mcol[:], -1.0),
                   reads=[("mcol",)], writes=[("mcol",)], name=f"fMn{ts}")
            P.step("scalar", lambda e, ts=ts: e.activation(
                      out=srow8[:], in_=m2s[:, ts * 8:(ts + 1) * 8], func=AF.Exp,
                      bias=mcol[:, 0:1]),
                   reads=[("m2s",), ("mcol",)], writes=[("srow8",)], name=f"fS{ts}")
            P.step("vector", lambda e, ts=ts: e.tensor_mul(
                      sd8[:], srow8[:], d2s[:, ts * 8:(ts + 1) * 8]),
                   reads=[("srow8",), ("d2s",)], writes=[("sd8",)], name=f"fSD{ts}")
            P.step("vector", lambda e: e.reduce_sum(dcol[:], sd8[:], axis=mybir.AxisListType.X),
                   reads=[("sd8",)], writes=[("dcol",)], name=f"fden{ts}")
            P.step("vector", lambda e: e.reciprocal(dcol[:], dcol[:]),
                   reads=[("dcol",)], writes=[("dcol",)], name=f"frc{ts}")
            for c in range(NCORE):
                P.load(ntile[:], ins["nTs"].rearrange("e (c p) d -> e c p d", p=128)[c, ts],
                       writes=[("ntile",)], name=f"ldn{ts}_{c}")
                if c == 0:
                    P.step("vector", lambda e, c=c: e.tensor_scalar_mul(
                              acc[:], ntile[:], srow8[:, c:c + 1]),
                           reads=[("ntile",), ("srow8",)], writes=[("acc",)],
                           name=f"fa{ts}_{c}")
                else:
                    P.step("vector", lambda e, c=c: e.scalar_tensor_tensor(
                              out=acc[:], in0=ntile[:], scalar=srow8[:, c:c + 1],
                              in1=acc[:], op0=ALU.mult, op1=ALU.add),
                           reads=[("ntile",), ("srow8",), ("acc",)], writes=[("acc",)],
                           name=f"fa{ts}_{c}")
            P.step("vector", lambda e: e.tensor_scalar(
                      out=acc[:], in0=acc[:], scalar1=dcol[:, 0:1], scalar2=0.5,
                      op0=ALU.mult, op1=ALU.mult),
                   reads=[("acc",), ("dcol",)], writes=[("acc",)], name=f"fmx{ts}")
            P.step("vector", lambda e: e.tensor_add(x3n[:], x3n[:], acc[:]),
                   reads=[("acc",), ("x3n",)], writes=[("x3n",)], name=f"fx4{ts}")
            for si, (ps, ln_, os_) in enumerate(_own_segments(ts * 128, ts * 128 + 128)):
                P.step("gpsimd", lambda e, ps=ps, ln_=ln_, os_=os_, ts=ts: e.dma_start(
                          out=outs["x4"][os_:os_ + ln_, :],
                          in_=x3n[ps - ts * 128:ps - ts * 128 + ln_, :]),
                       reads=[("x3n",)], writes=[("st_x4", ts, si)], dma=True,
                       name=f"sx4{ts}_{si}")
            P.step("vector", lambda e: e.reduce_sum(mcol[:], x3n[:], axis=mybir.AxisListType.X),
                   reads=[("x3n",)], writes=[("mcol",)], name=f"l5s{ts}")
            P.step("vector", lambda e: e.tensor_scalar_mul(mcol[:], mcol[:], 1.0 / 1024),
                   reads=[("mcol",)], writes=[("mcol",)], name=f"l5m{ts}")
            P.step("scalar", lambda e: e.square(out=sqn[:], in_=x3n[:]),
                   reads=[("x3n",)], writes=[("sqn",)], name=f"l5q{ts}")
            P.step("vector", lambda e: e.reduce_sum(dcol[:], sqn[:], axis=mybir.AxisListType.X),
                   reads=[("sqn",)], writes=[("dcol",)], name=f"l5e{ts}")
            P.step("vector", lambda e: e.tensor_scalar_mul(dcol[:], dcol[:], 1.0 / 1024),
                   reads=[("dcol",)], writes=[("dcol",)], name=f"l5e2{ts}")
            P.step("vector", lambda e: e.scalar_tensor_tensor(
                      out=m2c[:], in0=mcol[:], scalar=-1.0, in1=mcol[:],
                      op0=ALU.mult, op1=ALU.mult),
                   reads=[("mcol",)], writes=[("m2c",)], name=f"l5m2{ts}")
            P.step("vector", lambda e: e.tensor_add(dcol[:], dcol[:], m2c[:]),
                   reads=[("m2c",), ("dcol",)], writes=[("dcol",)], name=f"l5v{ts}")
            P.step("scalar", lambda e: e.activation(out=dcol[:], in_=dcol[:], func=AF.Ln,
                                                    bias=ceps[:, 0:1]),
                   reads=[("dcol",), ("ceps",)], writes=[("dcol",)], name=f"l5l{ts}")
            P.step("scalar", lambda e: e.activation(out=dcol[:], in_=dcol[:], func=AF.Exp,
                                                    scale=-0.5),
                   reads=[("dcol",)], writes=[("dcol",)], name=f"l5r{ts}")
            P.step("vector", lambda e: e.tensor_scalar_mul(mcol[:], mcol[:], -1.0),
                   reads=[("mcol",)], writes=[("mcol",)], name=f"l5nm{ts}")
            P.step("vector", lambda e: e.tensor_scalar(
                      out=sqn[:], in0=x3n[:], scalar1=mcol[:, 0:1], scalar2=dcol[:, 0:1],
                      op0=ALU.add, op1=ALU.mult),
                   reads=[("x3n",), ("mcol",), ("dcol",)], writes=[("sqn",)],
                   name=f"l5a{ts}")
            P.step("vector", lambda e: e.tensor_copy(out=xn5n[:], in_=sqn[:]),
                   reads=[("sqn",)], writes=[("xn5n",)], name=f"l5c{ts}")
            for si, (ps, ln_, os_) in enumerate(_own_segments(ts * 128, ts * 128 + 128)):
                P.step("gpsimd", lambda e, ps=ps, ln_=ln_, os_=os_, ts=ts: e.dma_start(
                          out=outs["ln5"][os_:os_ + ln_, :],
                          in_=sqn[ps - ts * 128:ps - ts * 128 + ln_, :]),
                       reads=[("sqn",)], writes=[("st_l5", ts, si)], dma=True,
                       name=f"sl5{ts}_{si}")
            P.step("scalar", lambda e: e.square(out=sqn[:], in_=xn5n[:]),
                   reads=[("xn5n",)], writes=[("sqn",)], name=f"r5q{ts}")
            P.step("vector", lambda e: e.reduce_sum(dcol[:], sqn[:], axis=mybir.AxisListType.X),
                   reads=[("sqn",)], writes=[("dcol",)], name=f"r5s{ts}")
            P.step("vector", lambda e: e.tensor_scalar_mul(dcol[:], dcol[:], 1.0 / 1024),
                   reads=[("dcol",)], writes=[("dcol",)], name=f"r5m{ts}")
            P.step("scalar", lambda e: e.activation(out=dcol[:], in_=dcol[:], func=AF.Ln,
                                                    bias=ceps2[:, 0:1]),
                   reads=[("dcol",), ("ceps2",)], writes=[("dcol",)], name=f"r5l{ts}")
            P.step("scalar", lambda e: e.activation(out=dcol[:], in_=dcol[:], func=AF.Exp,
                                                    scale=-0.5),
                   reads=[("dcol",)], writes=[("dcol",)], name=f"r5r{ts}")
            P.step("vector", lambda e: e.tensor_scalar_mul(xn5n[:], xn5n[:], dcol[:, 0:1]),
                   reads=[("dcol",), ("xn5n",)], writes=[("xn5n",)], name=f"r5a{ts}")
            for dt in range(8):
                P.step("tensor", lambda e, dt=dt: e.transpose(
                          pbt[dt % 2][:], xn5n[:, dt * 128:(dt + 1) * 128], idtb[:]),
                       reads=[("xn5n",), ("idtb",)], writes=[("pbt", dt % 2)],
                       name=f"l5t{ts}_{dt}")
                P.step("scalar", lambda e, dt=dt, ts=ts: e.activation(
                          out=xn5T[dt][:, ts * 128:(ts + 1) * 128], in_=pbt[dt % 2][:],
                          func=AF.Copy, scale=nw[:, dt:dt + 1]),
                       reads=[("pbt", dt % 2), ("nw",)], writes=[("xn5T", dt, ts)],
                       name=f"l5tv{ts}_{dt}")

        xmr = [sbuf(nc, es, f"xmr{i}", [128, TP], BF16) for i in range(2)]
        wsl = [sbuf(nc, es, f"wsl{i}", [128, 1024], BF16) for i in range(2)]
        zb = sbuf(nc, es, "zb", [128, 512], BF16)
        NCH5 = [(0, 512), (512, 512), (1024, 512), (1536, 512), (2048, 128)]
        ubf = [sbuf(nc, es, f"ubf{i}", [128, TL], BF16) for i in range(16)]
        uf = sbuf(nc, es, "uf", [128, TL], F32)
        cacc = sbuf(nc, es, "cacc", [128, TL], F32)
        for mo in range(32):
            s = mo % 2
            P.load(wsl[s][:], ins["wint"][mo], writes=[("wsl", s)], name=f"wi{mo}")
            for ci, (c0, cw) in enumerate(NCH5):
                bank = ci % 2
                def mmw(e, s=s, bank=bank, c0=c0, cw=cw):
                    for ko in range(8):
                        i = e.matmul(pba[bank][:, 0:cw], wsl[s][:, ko * 128:(ko + 1) * 128],
                                     xn5T[ko][:, c0:c0 + cw], start=(ko == 0), stop=(ko == 7))
                    return i
                P.step("tensor", mmw,
                       reads=[("wsl", s)] + [("xn5T", ko, ts) for ko in range(8)
                                             for ts in range(c0 // 128, (c0 + cw) // 128)],
                       writes=[("pbank", bank)], name=f"wm{mo}_{ci}")
                if mo < 16:
                    P.step("scalar", lambda e, mo=mo, bank=bank, c0=c0, cw=cw: e.copy(
                              out=xmr[mo % 2][:, c0:c0 + cw], in_=pba[bank][:, 0:cw]),
                           reads=[("pbank", bank)], writes=[("xmr", mo % 2, c0)],
                           name=f"we{mo}_{ci}")
                else:
                    P.step("scalar", lambda e, bank=bank, cw=cw: e.copy(
                              out=zb[:, 0:cw], in_=pba[bank][:, 0:cw]),
                           reads=[("pbank", bank)], writes=[("zb",)], name=f"ze{mo}_{ci}")
                    for si, (ps, ln_, os_) in enumerate(_own_segments(c0, c0 + cw)):
                        P.step("gpsimd", lambda e, mo=mo, c0=c0, ps=ps, ln_=ln_, os_=os_: e.dma_start(
                                  out=outs["z1"][(mo - 16) * 128:(mo - 15) * 128, os_:os_ + ln_],
                                  in_=zb[:, ps - c0:ps - c0 + ln_]),
                               reads=[("zb",)], writes=[("st_z", mo, c0, si)], dma=True,
                               name=f"zs{mo}_{ci}_{si}")

            if mo < 16:
                kt = mo
                sl = mo % 2
                rk = [("xmr", sl, c0) for c0, _ in NCH5]
                for sb, (pb_, ob_) in enumerate(((0, 0), (1027, 1024))):
                    P.step("vector", lambda e, sl=sl, kt=kt, pb_=pb_, ob_=ob_: e.tensor_scalar_mul(
                              cacc[:, ob_:ob_ + SL], xmr[sl][:, pb_:pb_ + SL],
                              cvw[:, kt * 4:kt * 4 + 1]),
                           reads=rk + [("cvw",)], writes=[("cacc",)], name=f"cv0_{kt}_{sb}")
                    for k in range(1, 4):
                        P.step("vector", lambda e, sl=sl, kt=kt, k=k, pb_=pb_, ob_=ob_: e.scalar_tensor_tensor(
                                  out=cacc[:, ob_:ob_ + SL], in0=xmr[sl][:, pb_ + k:pb_ + k + SL],
                                  scalar=cvw[:, kt * 4 + k:kt * 4 + k + 1], in1=cacc[:, ob_:ob_ + SL],
                                  op0=ALU.mult, op1=ALU.add),
                               reads=[("cacc",), ("cvw",)] + rk,
                               writes=[("cacc",)], name=f"cv{k}_{kt}_{sb}")
                P.step("scalar", lambda e, kt=kt: e.activation(
                          out=uf[:], in_=cacc[:], func=AF.Silu, bias=cvb[:, kt:kt + 1]),
                       reads=[("cacc",), ("cvb",)], writes=[("uf",)], name=f"cs{kt}")
                P.step("scalar", lambda e, kt=kt: e.copy(out=ubf[kt][:], in_=uf[:]),
                       reads=[("uf",)], writes=[("ubf", kt)], name=f"cb{kt}")
                P.step("gpsimd", lambda e, kt=kt: e.dma_start(
                          out=outs["u1"][kt * 128:(kt + 1) * 128, :], in_=uf[:]),
                       reads=[("uf",)], writes=[("st_u", kt)], dma=True, name=f"cu{kt}")

        wxr = sbuf(nc, es, "wxr", [128, 2048], F32)
        wxr2 = sbuf(nc, es, "wxr2", [64, 2048], F32)
        wxb = sbuf(nc, es, "wxb", [128, 2048], BF16)
        wxb2 = sbuf(nc, es, "wxb2", [64, 2048], BF16)
        P.load(wxr[:], ins["wx"][0:128, :], writes=[("wxr",)], name="lwx")
        P.load(wxr2[:], ins["wx"][128:192, :], writes=[("wxr2",)], name="lwx2")
        P.step("scalar", lambda e: e.copy(out=wxb[:], in_=wxr[:]),
               reads=[("wxr",)], writes=[("wxb",)], name="cwx")
        P.step("scalar", lambda e: e.copy(out=wxb2[:], in_=wxr2[:]),
               reads=[("wxr2",)], writes=[("wxb2",)], name="cwx2")
        wxT = [sbuf(nc, es, f"wxT{i}", [128, 192], BF16) for i in range(16)]
        for kt in range(16):
            P.step("tensor", lambda e, kt=kt: e.transpose(
                      pbt[0][:], wxb[:, kt * 128:(kt + 1) * 128], idtb[:]),
                   reads=[("wxb",), ("idtb",)], writes=[("pbt", 0)], name=f"tx{kt}")
            P.step("scalar", lambda e, kt=kt: e.copy(out=wxT[kt][:, 0:128], in_=pbt[0][:]),
                   reads=[("pbt", 0)], writes=[("wxT", kt)], name=f"tv{kt}")
            P.step("tensor", lambda e, kt=kt: e.transpose(
                      pbt[1][:, 0:64], wxb2[:, kt * 128:(kt + 1) * 128], idtb[0:64, 0:64]),
                   reads=[("wxb2",), ("idtb",)], writes=[("pbt", 1)], name=f"tx2{kt}")
            P.step("scalar", lambda e, kt=kt: e.copy(out=wxT[kt][:, 128:192],
                                                     in_=pbt[1][:, 0:64]),
                   reads=[("pbt", 1)], writes=[("wxT", kt)], name=f"tv2{kt}")
        wdtb = sbuf(nc, es, "s_wdtb", [128, 16 * 64], BF16)
        P.load(wdtb[:].rearrange("p (a r) -> p a r", a=16),
               ins["wdt1"].rearrange("(a p) r -> p a r", p=128),
               writes=[("wdtb",)], q="gpsimd", name="ldwdt1")
        wdtT = sbuf(nc, es, "s_wdtT", [64, 2048], BF16)
        for kt in range(16):
            P.step("tensor", lambda e, kt=kt: e.transpose(
                      pbt[0][0:64, 0:128], wdtb[:, kt * 64:(kt + 1) * 64], idtb[:]),
                   reads=[("wdtb",), ("idtb",)], writes=[("pbt", 0)], name=f"td{kt}")
            P.step("scalar", lambda e, kt=kt: e.copy(
                      out=wdtT[0:64, kt * 128:(kt + 1) * 128], in_=pbt[0][0:64, 0:128]),
                   reads=[("pbt", 0)], writes=[("wdtT", kt)], name=f"tdv{kt}")

        xdt = sbuf(nc, es, "xdt", [64, TL], BF16)
        xdf = sbuf(nc, es, "xdf", [128, 512], F32)
        for mt, (mb, mw) in enumerate([(0, 128), (128, 64)]):
            for ci in range(4):
                c0 = ci * 512
                bank = 2 + ci % 2
                def mmx(e, mb=mb, mw=mw, c0=c0, bank=bank):
                    for kt in range(16):
                        i = e.matmul(pba[bank][0:mw, :], wxT[kt][:, mb:mb + mw],
                                     ubf[kt][:, c0:c0 + 512], start=(kt == 0), stop=(kt == 15))
                    return i
                P.step("tensor", mmx,
                       reads=[("wxT", kt) for kt in range(16)] +
                             [("ubf", kt) for kt in range(16)],
                       writes=[("pbank", bank)], name=f"xd{mt}_{ci}")
                P.step("scalar", lambda e, mw=mw, bank=bank: e.copy(
                          out=xdf[0:mw, :], in_=pba[bank][0:mw, :]),
                       reads=[("pbank", bank)], writes=[("xdf",)], name=f"xde{mt}_{ci}")
                if mt == 0:
                    P.step("scalar", lambda e, c0=c0: e.copy(
                              out=xdt[:, c0:c0 + 512], in_=xdf[0:64, :]),
                           reads=[("xdf",)], writes=[("xdt", c0)], name=f"xdc{ci}")
                    P.step("gpsimd", lambda e, c0=c0: e.dma_start(
                              out=outs["bc1"][0:64, c0:c0 + 512], in_=xdf[64:128, :]),
                           reads=[("xdf",)], writes=[("st_bcb", ci)], dma=True,
                           name=f"xsb{ci}")
                else:
                    P.step("gpsimd", lambda e, c0=c0: e.dma_start(
                              out=outs["bc1"][64:128, c0:c0 + 512], in_=xdf[0:64, :]),
                           reads=[("xdf",)], writes=[("st_bcc", ci)], dma=True,
                           name=f"xsc{ci}")

        r1 = sbuf(nc, es, "r1f", [128, 512], F32)
        r2 = sbuf(nc, es, "r2f", [128, 512], F32)
        for mo in range(16):
            for ci in range(4):
                c0 = ci * 512
                P.step("tensor", lambda e, mo=mo, c0=c0: e.matmul(
                          pba[4][:], wdtT[:, mo * 128:(mo + 1) * 128], xdt[:, c0:c0 + 512],
                          start=True, stop=True),
                       reads=[("wdtT", kt) for kt in range(16)] +
                             [("xdt", k * 512) for k in range(4)],
                       writes=[("pbank", 4)], name=f"dm{mo}_{ci}")
                P.step("scalar", lambda e, mo=mo: e.activation(
                          out=r1[:], in_=pba[4][:], func=AF.Relu, bias=bdt1[:, mo:mo + 1]),
                       reads=[("pbank", 4), ("bdt1",)], writes=[("r1f",)], name=f"dd1{mo}_{ci}")
                P.step("scalar", lambda e, mo=mo: e.activation(
                          out=r2[:], in_=pba[4][:], func=AF.Abs, bias=bdt1[:, mo:mo + 1]),
                       reads=[("pbank", 4), ("bdt1",)], writes=[("r2f",)], name=f"dd2{mo}_{ci}")
                P.step("scalar", lambda e: e.activation(out=r2[:], in_=r2[:], func=AF.Exp, scale=-1.0),
                       reads=[("r2f",)], writes=[("r2f",)], name=f"dd3{mo}_{ci}")
                P.step("scalar", lambda e: e.activation(out=r2[:], in_=r2[:], func=AF.Ln, bias=1.0),
                       reads=[("r2f",)], writes=[("r2f",)], name=f"dd4{mo}_{ci}")
                P.step("vector", lambda e: e.tensor_add(r2[:], r2[:], r1[:]),
                       reads=[("r2f",), ("r1f",)], writes=[("r2f",)], name=f"dd5{mo}_{ci}")
                P.step("gpsimd", lambda e, mo=mo, c0=c0: e.dma_start(
                          out=outs["delta1"][mo * 128:(mo + 1) * 128, c0:c0 + 512],
                          in_=r2[:]),
                       reads=[("r2f",)], writes=[("st_dl", mo, c0)], dma=True,
                       name=f"ds{mo}_{ci}")

        fin = ([("st_x4", ts, si) for ts in range(NTS)
                for si in range(len(_own_segments(ts * 128, ts * 128 + 128)))] +
               [("st_l5", ts, si) for ts in range(NTS)
                for si in range(len(_own_segments(ts * 128, ts * 128 + 128)))] +
               [("st_z", mo, c0, si) for mo in range(16, 32) for c0, cw in NCH5
                for si in range(len(_own_segments(c0, c0 + cw)))] +
               [("st_u", kt) for kt in range(16)] +
               [("st_bcb", ci) for ci in range(4)] + [("st_bcc", ci) for ci in range(4)] +
               [("st_dl", mo, k * 512) for mo in range(16) for k in range(4)])
        P.step("sync", None, reads=fin, name="fin")
    return bld


def _padded_rows(arr_nat, c, fill_zero=False, const=None):
    D = arr_nat.shape[1]
    b0 = arr_nat[c * SL:(c + 1) * SL]
    b1 = arr_nat[SEQ + c * SL:SEQ + (c + 1) * SL]
    if c == 0:
        h0 = np.zeros((3, D), arr_nat.dtype) if const is None else np.full((3, D), const, arr_nat.dtype)
        h1 = h0
    else:
        h0 = arr_nat[c * SL - 3:c * SL]
        h1 = arr_nat[SEQ + c * SL - 3:SEQ + c * SL]
    pad = np.repeat(b1[-1:], 122, axis=0)
    return np.ascontiguousarray(np.concatenate([h0, b0, h1, b1, pad], 0))


def run_FG(inputs, prep, d_outs, e_outs):
    in_specs = {
        "x3n": ((TP, DM), F32), "nTs": ((NCORE, TP, DM), BF16),
        "m2s": ((128, NTS * 8), F32), "d2s": ((128, NTS * 8), F32),
        "wint": ((32, 128, 1024), BF16),
        "cvw": ((2048, 1, 4), F32), "cvb": ((128, 16), F32),
        "wx": ((192, 2048), F32), "wdt1": ((2048, 64), F32), "bdt1": ((128, 16), F32),
        "nw": ((128, 8), F32),
        "idty": ((128, 128), BF16), "idtf": ((128, 128), F32),
    }
    out_specs = {
        "x4": ((TL, DM), F32), "u1": ((2048, TL), F32), "delta1": ((2048, TL), F32),
        "bc1": ((128, TL), F32), "z1": ((2048, TL), BF16), "ln5": ((TL, DM), F32),
    }
    nc = mkprog(build_FG(), in_specs, out_specs)
    x3_T = np.concatenate([o["x3"][:, :SL] for o in d_outs] +
                          [o["x3"][:, SL:] for o in d_outs], axis=1)
    x3_nat = np.ascontiguousarray(x3_T.T)
    nT_nat = [np.ascontiguousarray(o["nT"].T) for o in e_outs]
    m2_flat = np.stack([o["mx2"].T.reshape(-1) for o in e_outs])
    d2_flat = np.stack([o["d2"].T.reshape(-1) for o in e_outs])
    shared = {
        "wint": prep["wint"],
        "cvw": inputs["L1_attn_conv_w"].astype(np.float32),
        "cvb": _colscale(inputs["L1_attn_conv_b"]),
        "wx": inputs["L1_attn_W_x"].astype(np.float32),
        "wdt1": inputs["L1_attn_W_dt"].astype(np.float32),
        "bdt1": _colscale(inputs["L1_attn_b_dt"]),
        "nw": _colscale(inputs["L1_attn_norm_w"]),
        "idty": IDTY_BF, "idtf": IDTY,
    }
    in_maps = []
    for c in range(NCORE):
        nTs = np.stack([_padded_rows(nT_nat[k], c) for k in range(NCORE)])
        m2p = np.stack([_padded_rows(m2_flat[k][:, None], c)[:, 0] for k in range(NCORE)])
        d2p = np.stack([_padded_rows(d2_flat[k][:, None], c, const=1.0)[:, 0]
                        for k in range(NCORE)])
        m2s = np.ascontiguousarray(m2p.reshape(8, NTS, 128).transpose(2, 1, 0).reshape(128, NTS * 8))
        d2s = np.ascontiguousarray(d2p.reshape(8, NTS, 128).transpose(2, 1, 0).reshape(128, NTS * 8))
        in_maps.append(dict(shared, x3n=_padded_rows(x3_nat, c), nTs=nTs,
                            m2s=m2s.astype(np.float32), d2s=d2s.astype(np.float32)))
    res = run_bass_kernel_spmd(nc, in_maps, core_ids=list(range(NCORE)))
    return [res.results[c] for c in range(NCORE)]



def build_H():
    TC = 1024
    NOC = (NB * SEQ) // TC

    def bld(nc, P, ins, outs, es):
        sel = sbuf(nc, es, "s_sel64", [128, 64 * 128], F32)
        P.load(sel[:], ins["sel64"], writes=[("sel",)])
        idtb = sbuf(nc, es, "s_idtb", [128, 128], BF16)
        P.load(idtb[:], ins["idty"], writes=[("idtb",)])
        alog = sbuf(nc, es, "s_alog1", [128, 2 * 64], F32)
        P.load(alog[:].rearrange("p (a n) -> p a n", a=2),
               ins["alog1"].rearrange("(a p) n -> p a n", p=128),
               writes=[("alog",)], name="ldal")
        dpsl = sbuf(nc, es, "s_dpsl", [128, 2], F32)
        P.load(dpsl[:], ins["dpsl"], writes=[("dpsl",)])
        A = sbuf(nc, es, "s_A1", [128, 2 * 64], F32)
        P.step("scalar", lambda e: e.activation(out=A[:], in_=alog[:], func=AF.Exp),
               reads=[("alog",)], writes=[("A",)], name="expA")
        P.step("vector", lambda e: e.tensor_scalar_mul(A[:], A[:], -1.0),
               reads=[("A",)], writes=[("A",)], name="negA")
        carry = [sbuf(nc, es, f"carry{i}", [128, 64], F32) for i in range(2)]
        zcol = sbuf(nc, es, "zcol", [128, 1], F32)
        P.step("gpsimd", lambda e: e.memset(zcol[:], 0.0), writes=[("zcol",)], name="z")
        uc = [[sbuf(nc, es, f"uc{i}_{j}", [128, TC], F32) for i in range(2)] for j in range(2)]
        dc = [[sbuf(nc, es, f"dc{i}_{j}", [128, TC], F32) for i in range(2)] for j in range(2)]
        bcc = [sbuf(nc, es, f"bcc{i}", [128, TC], F32) for i in range(2)]
        wcb = [sbuf(nc, es, f"wcb{j}", [128, TC], BF16) for j in range(2)]
        Bbs = [sbuf(nc, es, f"Bbs{i}", [128, TC], BF16) for i in range(3)]
        Cbs = [sbuf(nc, es, f"Cbs{i}", [128, TC], BF16) for i in range(3)]
        dac = [sbuf(nc, es, f"dac{i}", [128, TC], BF16) for i in range(4)]
        bbc = [sbuf(nc, es, f"bbc{i}", [128, TC], BF16) for i in range(4)]
        hcb = [sbuf(nc, es, f"hcb{i}", [128, TC], BF16) for i in range(4)]
        ymb = [sbuf(nc, es, f"ymb{i}", [128, TC], BF16) for i in range(4)]
        yac = sbuf(nc, es, "yac", [128, TC], F32)
        ymf = sbuf(nc, es, "ymf", [128, TC], F32)
        pbb = psum(nc, es, "pbb", [128, TC], F32)
        pbc = psum(nc, es, "pbc", [128, TC], F32)
        pyy = [psum(nc, es, f"pyy{j}", [128, TC], F32) for j in range(2)]

        for oc in range(NOC):
            s = oc % 2
            cs = slice(oc * TC, (oc + 1) * TC)
            for j in range(2):
                P.load(uc[s][j][:], ins["u1f"][j * 128:(j + 1) * 128, cs],
                       writes=[("uc", s, j)], name=f"ldu{oc}_{j}")
                P.load(dc[s][j][:], ins["d1f"][j * 128:(j + 1) * 128, cs],
                       writes=[("dc", s, j)], name=f"ldd{oc}_{j}")
                P.step("vector", lambda e, s=s, j=j: e.tensor_mul(
                          wcb[j][:], dc[s][j][:], uc[s][j][:]),
                       reads=[("dc", s, j), ("uc", s, j)], writes=[("wcb", j)],
                       name=f"w{oc}_{j}")
            P.load(bcc[s][:], ins["bc1f"][:, cs], writes=[("bcc", s)], name=f"ldbc{oc}")

            def emit_bcast(n):
                p = n % 3
                def mmb(e, s=s, n=n):
                    e.matmul(pbb[:, 0:512], sel[0:64, n * 128:(n + 1) * 128],
                             bcc[s][0:64, 0:512], start=True, stop=True)
                    return e.matmul(pbb[:, 512:1024], sel[0:64, n * 128:(n + 1) * 128],
                                    bcc[s][0:64, 512:1024], start=True, stop=True)
                P.step("tensor", mmb, reads=[("sel",), ("bcc", s)], writes=[("psB",)],
                       name=f"bb{oc}_{n}")
                P.step("scalar", lambda e, p=p: e.copy(out=Bbs[p][:], in_=pbb[:]),
                       reads=[("psB",)], writes=[("Bbs", p)], name=f"be{oc}_{n}")
                def mmc(e, s=s, n=n):
                    e.matmul(pbc[:, 0:512], sel[64:128, n * 128:(n + 1) * 128],
                             bcc[s][64:128, 0:512], start=True, stop=True)
                    return e.matmul(pbc[:, 512:1024], sel[64:128, n * 128:(n + 1) * 128],
                                    bcc[s][64:128, 512:1024], start=True, stop=True)
                P.step("tensor", mmc, reads=[("sel",), ("bcc", s)], writes=[("psC",)],
                       name=f"cb{oc}_{n}")
                P.step("scalar", lambda e, p=p: e.copy(out=Cbs[p][:], in_=pbc[:]),
                       reads=[("psC",)], writes=[("Cbs", p)], name=f"ce{oc}_{n}")
                for j in range(2):
                    q = (2 * n + j) % 4
                    P.step("scalar", lambda e, s=s, n=n, j=j, q=q: e.activation(
                              out=dac[q][:], in_=dc[s][j][:], func=AF.Exp,
                              scale=A[:, j * 64 + n:j * 64 + n + 1]),
                           reads=[("dc", s, j), ("A",)], writes=[("dac", q)],
                           name=f"da{oc}_{n}_{j}")

            emit_bcast(0)
            for n in range(64):
                p = n % 3
                if n + 1 < 64:
                    emit_bcast(n + 1)
                for j in range(2):
                    q = (2 * n + j) % 4
                    P.step("vector", lambda e, j=j, p=p, q=q: e.tensor_mul(
                              bbc[q][:], wcb[j][:], Bbs[p][:]),
                           reads=[("wcb", j), ("Bbs", p)], writes=[("bbc", q)],
                           name=f"bB{oc}_{n}_{j}")
                    if oc % (NOC // 2) == 0:
                        init, ikey = zcol[:], ("zcol",)
                    else:
                        init, ikey = carry[j][:, n:n + 1], ("carry", j, n)
                    P.step("vector", lambda e, init=init, q=q: e.tensor_tensor_scan(
                              out=hcb[q][:], data0=dac[q][:], data1=bbc[q][:], initial=init,
                              op0=ALU.mult, op1=ALU.add),
                           reads=[("dac", q), ("bbc", q), ikey], writes=[("hcb", q)],
                           name=f"h{oc}_{n}_{j}")
                    P.step("vector", lambda e, j=j, n=n, q=q: e.tensor_copy(
                              out=carry[j][:, n:n + 1], in_=hcb[q][:, TC - 1:TC]),
                           reads=[("hcb", q)], writes=[("carry", j, n)], name=f"cy{oc}_{n}_{j}")
                    P.step("vector", lambda e, p=p, q=q: e.tensor_mul(
                              ymb[q][:], hcb[q][:], Cbs[p][:]),
                           reads=[("hcb", q), ("Cbs", p)], writes=[("ymb", q)],
                           name=f"ym{oc}_{n}_{j}")
                    def mmy(e, j=j, q=q, n=n):
                        e.matmul(pyy[j][:, 0:512], idtb[:], ymb[q][:, 0:512],
                                 start=(n == 0), stop=(n == 63))
                        return e.matmul(pyy[j][:, 512:1024], idtb[:], ymb[q][:, 512:1024],
                                        start=(n == 0), stop=(n == 63))
                    P.step("tensor", mmy, reads=[("ymb", q), ("idtb",)],
                           writes=[("psY", j)] if n in (0, 63) else [],
                           name=f"ya{oc}_{n}_{j}")
            for j in range(2):
                P.step("scalar", lambda e, j=j: e.copy(out=yac[:], in_=pyy[j][:]),
                       reads=[("psY", j)], writes=[("yac",)], name=f"ye{oc}_{j}")
                P.step("vector", lambda e, s=s, j=j: e.tensor_scalar_mul(
                          ymf[:], uc[s][j][:], dpsl[:, j:j + 1]),
                       reads=[("uc", s, j), ("dpsl",)], writes=[("ymf",)], name=f"ud{oc}_{j}")
                P.step("vector", lambda e, j=j: e.tensor_add(yac[:], yac[:], ymf[:]),
                       reads=[("ymf",), ("yac",)], writes=[("yac",)], name=f"yd{oc}_{j}")
                P.step("gpsimd", lambda e, j=j, cs=cs: e.dma_start(
                          out=outs["y1"][j * 128:(j + 1) * 128, cs], in_=yac[:]),
                       reads=[("yac",)], writes=[("st", j, oc % 2)], dma=True,
                       name=f"st{oc}_{j}")
        P.step("sync", None, reads=[("st", j, k) for j in range(2) for k in range(2)],
               name="fin")
    return bld


SEL64 = np.zeros((128, 64 * 128), np.float32)
for _n in range(64):
    SEL64[_n, _n * 128:(_n + 1) * 128] = 1.0
    SEL64[64 + _n, _n * 128:(_n + 1) * 128] = 1.0


def run_H(fg_outs, inputs):
    TT_ = NB * SEQ
    in_specs = {
        "u1f": ((256, TT_), F32), "d1f": ((256, TT_), F32), "bc1f": ((128, TT_), F32),
        "alog1": ((256, 64), F32), "dpsl": ((128, 2), F32),
        "sel64": ((128, 64 * 128), F32), "idty": ((128, 128), BF16),
    }
    out_specs = {"y1": ((256, TT_), F32)}
    nc = mkprog(build_H(), in_specs, out_specs)
    ufull = np.concatenate([o["u1"][:, :SL] for o in fg_outs] +
                           [o["u1"][:, SL:] for o in fg_outs], axis=1)
    dfull = np.concatenate([o["delta1"][:, :SL] for o in fg_outs] +
                           [o["delta1"][:, SL:] for o in fg_outs], axis=1)
    bcfull = np.concatenate([o["bc1"][:, :SL] for o in fg_outs] +
                            [o["bc1"][:, SL:] for o in fg_outs], axis=1)
    alog = inputs["L1_attn_A_log"].astype(np.float32)
    dp = inputs["L1_attn_Dp"].astype(np.float32)
    in_maps = []
    for c in range(NCORE):
        rs = slice(c * 256, (c + 1) * 256)
        dpc = dp[rs].reshape(2, 128).T
        in_maps.append({
            "u1f": np.ascontiguousarray(ufull[rs]),
            "d1f": np.ascontiguousarray(dfull[rs]),
            "bc1f": bcfull, "alog1": np.ascontiguousarray(alog[rs]),
            "dpsl": np.ascontiguousarray(dpc), "sel64": SEL64, "idty": IDTY_BF,
        })
    res = run_bass_kernel_spmd(nc, in_maps, core_ids=list(range(NCORE)))
    yfull = np.concatenate([res.results[c]["y1"] for c in range(NCORE)], axis=0)
    y_tok = []
    for c in range(NCORE):
        y_tok.append(np.ascontiguousarray(np.concatenate(
            [yfull[:, c * SL:(c + 1) * SL],
             yfull[:, SEQ + c * SL:SEQ + (c + 1) * SL]], axis=1)))
    return y_tok



def build_I():
    TKC = 512
    NCH = TL // TKC

    def bld(nc, P, ins, outs, es):
        sc = {}
        sc["ones_col"] = sbuf(nc, es, "s_ones_col", [128, 1], F32)
        sc["ones_row"] = sbuf(nc, es, "s_ones_row", [1, 128], F32)
        P.load(sc["ones_col"][:], ins["ones_col"], writes=[("ones_col",)])
        P.load(sc["ones_row"][:], ins["ones_row"], writes=[("ones_row",)])
        gam = sbuf(nc, es, "s_gam", [128, 8], F32)
        P.load(gam[:], ins["gam"], writes=[("gam",)])
        ceps = sbuf(nc, es, "s_ceps", [128, 1], F32)
        P.step("gpsimd", lambda e: e.memset(ceps[:], EPS_LN), writes=[("ceps",)], name="ceps")
        r32 = sbuf(nc, es, "s_r32", [1, 128], F32)
        P.step("gpsimd", lambda e: e.memset(r32[:], float(np.sqrt(DM))),
               writes=[("r32",)], name="c32")
        pb = [psum(nc, es, f"pb{i}", [128, 512], F32) for i in range(8)]
        XR = [sbuf(nc, es, f"XR{i}", [128, TL], F32) for i in range(8)]
        for i in range(8):
            P.load(XR[i][:], ins["x4_T"].rearrange("(a p) t -> a p t", p=128)[i],
                   writes=[("XR", i)], name=f"ldx{i}")
        g = [sbuf(nc, es, f"g{i}", [128, TL], BF16) for i in range(16)]
        zt = [sbuf(nc, es, f"zt{i}", [128, TL], BF16) for i in range(1)]
        yt = [sbuf(nc, es, f"yt{i}", [128, TL], F32) for i in range(1)]
        for i in range(8):
            P.load(yt[0][:], ins["ln5_T"].rearrange("(a p) t -> a p t", p=128)[i],
                   writes=[("yt", 0)], name=f"ldl5{i}")
            P.step("vector", lambda e, i=i: e.tensor_add(XR[i][:], XR[i][:], yt[0][:]),
                   reads=[("yt", 0), ("XR", i)], writes=[("XR", i)], name=f"al5{i}")
        szb = sbuf(nc, es, "szb", [128, TL], BF16)
        for kt in range(16):
            s = 0
            P.load(zt[s][:], ins["z_T"][kt * 128:(kt + 1) * 128, :],
                   writes=[("zt", s)], name=f"ldz{kt}")
            P.load(yt[s][:], ins["y_T"][kt * 128:(kt + 1) * 128, :],
                   writes=[("yt", s)], name=f"ldy{kt}")
            P.step("scalar", lambda e, s=s: e.activation(out=szb[:], in_=zt[s][:], func=AF.Silu),
                   reads=[("zt", s)], writes=[("szb",)], name=f"sz{kt}")
            P.step("vector", lambda e, s=s, kt=kt: e.tensor_mul(g[kt][:], yt[s][:], szb[:]),
                   reads=[("yt", s), ("szb",)], writes=[("g", kt)], name=f"g{kt}")
        wsl = [sbuf(nc, es, f"wsl{i}", [128, 2048], BF16) for i in range(2)]
        tmpo = sbuf(nc, es, "s_tmpo", [128, 512], F32)
        for mo in range(8):
            s = mo % 2
            P.load(wsl[s][:], ins["woutt"][mo], writes=[("wsl", s)], name=f"lw{mo}")
            for ci in range(NCH):
                c0 = ci * TKC
                bank = ci % 2
                def mmo(e, s=s, bank=bank, c0=c0):
                    for kt in range(16):
                        i = e.matmul(pb[bank][:], wsl[s][:, kt * 128:(kt + 1) * 128],
                                     g[kt][:, c0:c0 + TKC], start=(kt == 0), stop=(kt == 15))
                    return i
                P.step("tensor", mmo,
                       reads=[("wsl", s)] + [("g", kt) for kt in range(16)],
                       writes=[("pbank", bank)], name=f"om{mo}_{ci}")
                P.step("scalar", lambda e, bank=bank: e.copy(out=tmpo[:], in_=pb[bank][:]),
                       reads=[("pbank", bank)], writes=[("tmpo",)], name=f"oe{mo}_{ci}")
                P.step("vector", lambda e, mo=mo, c0=c0: e.tensor_add(
                          XR[mo][:, c0:c0 + TKC], XR[mo][:, c0:c0 + TKC], tmpo[:]),
                       reads=[("tmpo",), ("XR", mo)], writes=[("XR", mo)],
                       name=f"ox{mo}_{ci}")
        for i in range(8):
            P.step("gpsimd", lambda e, i=i: e.dma_start(
                      out=outs["x5"][i * 128:(i + 1) * 128, :], in_=XR[i][:]),
                   reads=[("XR", i)], writes=[("st_x5", i)], dma=True, name=f"sx5{i}")
        xn = [sbuf(nc, es, f"xn{i}", [128, TL], BF16) for i in range(8)]
        sq = sbuf(nc, es, "s_sq", [128, 512], F32)
        ap1 = sbuf(nc, es, "s_ap1", [128, 512], F32)
        sa = sbuf(nc, es, "s_sa", [128, 512], BF16)
        for nm in ("mrow", "wrow", "rrow", "mbrow"):
            sc[nm] = sbuf(nc, es, "s_" + nm, [1, 512], F32)
        for ch in range(NCH):
            cs = slice(ch * TKC, (ch + 1) * TKC)
            for dt in range(8):
                P.step("scalar", lambda e, dt=dt, cs=cs: e.square(out=sq[:], in_=XR[dt][:, cs]),
                       reads=[("XR", dt)], writes=[("sq",)], name=f"l6q{ch}_{dt}")
                P.step("tensor", lambda e, dt=dt, cs=cs: e.matmul(
                          pb[0][0:1, :], sc["ones_col"][:], XR[dt][:, cs],
                          start=(dt == 0), stop=(dt == 7)),
                       reads=[("XR", dt), ("ones_col",)], writes=[("pbank", 0)],
                       name=f"l6x{ch}_{dt}")
                P.step("tensor", lambda e, dt=dt: e.matmul(
                          pb[1][0:1, :], sc["ones_col"][:], sq[:],
                          start=(dt == 0), stop=(dt == 7)),
                       reads=[("sq",), ("ones_col",)], writes=[("pbank", 1)],
                       name=f"l6s{ch}_{dt}")
            P.step("vector", lambda e: e.tensor_scalar_mul(sc["mrow"][:], pb[0][0:1, :], 1.0 / 1024),
                   reads=[("pbank", 0)], writes=[("mrow",)], name=f"l6m{ch}")
            P.step("vector", lambda e: e.tensor_scalar_mul(sc["wrow"][:], pb[1][0:1, :], 1.0 / 1024),
                   reads=[("pbank", 1)], writes=[("wrow",)], name=f"l6e{ch}")
            P.step("vector", lambda e: e.tensor_mul(sc["mbrow"][:], sc["mrow"][:], sc["mrow"][:]),
                   reads=[("mrow",)], writes=[("mbrow",)], name=f"l6m2{ch}")
            P.step("vector", lambda e: e.tensor_sub(sc["wrow"][:], sc["wrow"][:], sc["mbrow"][:]),
                   reads=[("mbrow",), ("wrow",)], writes=[("wrow",)], name=f"l6v{ch}")
            P.step("scalar", lambda e: e.activation(out=sc["wrow"][:], in_=sc["wrow"][:],
                                                    func=AF.Ln, bias=ceps[0:1, 0:1]),
                   reads=[("wrow",), ("ceps",)], writes=[("wrow",)], name=f"l6l{ch}")
            P.step("scalar", lambda e: e.activation(out=sc["rrow"][:], in_=sc["wrow"][:],
                                                    func=AF.Exp, scale=-0.5),
                   reads=[("wrow",)], writes=[("rrow",)], name=f"l6r{ch}")
            P.step("vector", lambda e: e.scalar_tensor_tensor(
                      out=sc["mbrow"][:], in0=sc["mrow"][:], scalar=-1.0, in1=sc["rrow"][:],
                      op0=ALU.mult, op1=ALU.mult),
                   reads=[("mrow",), ("rrow",)], writes=[("mbrow",)], name=f"l6mb{ch}")
            P.step("tensor", lambda e: e.matmul(pb[2][:], sc["ones_row"][:], sc["rrow"][:],
                                                start=True, stop=True),
                   reads=[("rrow",), ("ones_row",)], writes=[("pbank", 2)], name=f"l6br{ch}")
            P.step("tensor", lambda e: e.matmul(pb[3][:], sc["ones_row"][:], sc["mbrow"][:],
                                                start=True, stop=True),
                   reads=[("mbrow",), ("ones_row",)], writes=[("pbank", 3)], name=f"l6bm{ch}")
            for dt in range(8):
                P.step("vector", lambda e, dt=dt, cs=cs: e.tensor_mul(
                          ap1[:], XR[dt][:, cs], pb[2][:]),
                       reads=[("XR", dt), ("pbank", 2)], writes=[("ap1",)],
                       name=f"l6a1_{ch}_{dt}")
                P.step("vector", lambda e, dt=dt, cs=cs: e.tensor_add(
                          xn[dt][:, cs], ap1[:], pb[3][:]),
                       reads=[("ap1",), ("pbank", 3)], writes=[("xn", dt, cs.start)],
                       name=f"l6a2_{ch}_{dt}")
        for ch in range(NCH):
            cs = slice(ch * TKC, (ch + 1) * TKC)
            for dt in range(8):
                P.step("scalar", lambda e, dt=dt, cs=cs: e.square(out=sq[:], in_=xn[dt][:, cs]),
                       reads=[("xn", dt, cs.start)], writes=[("sq",)], name=f"q6s{ch}_{dt}")
                P.step("tensor", lambda e, dt=dt: e.matmul(
                          pb[0][0:1, :], sc["ones_col"][:], sq[:],
                          start=(dt == 0), stop=(dt == 7)),
                       reads=[("sq",), ("ones_col",)], writes=[("pbank", 0)],
                       name=f"q6m{ch}_{dt}")
            P.step("scalar", lambda e: e.activation(out=sc["wrow"][:], in_=pb[0][0:1, :],
                                                    func=AF.Ln),
                   reads=[("pbank", 0)], writes=[("wrow",)], name=f"q6l{ch}")
            P.step("scalar", lambda e: e.activation(out=sc["rrow"][:], in_=sc["wrow"][:],
                                                    func=AF.Exp, scale=-0.5),
                   reads=[("wrow",)], writes=[("rrow",)], name=f"q6r{ch}")
            P.step("tensor", lambda e: e.matmul(pb[2][:], r32[:], sc["rrow"][:],
                                                start=True, stop=True),
                   reads=[("rrow",), ("r32",)], writes=[("pbank", 2)], name=f"q6b{ch}")
            for dt in range(8):
                P.step("vector", lambda e, dt=dt, cs=cs: e.scalar_tensor_tensor(
                          out=sa[:], in0=xn[dt][:, cs], scalar=gam[:, dt:dt + 1],
                          in1=pb[2][:], op0=ALU.mult, op1=ALU.mult),
                       reads=[("xn", dt, cs.start), ("gam",), ("pbank", 2)],
                       writes=[("sa",)], name=f"q6x{ch}_{dt}")
                P.step("gpsimd", lambda e, dt=dt, ch=ch: e.dma_start(
                          out=outs["xq2_T"][dt * 128:(dt + 1) * 128, ch * TKC:(ch + 1) * TKC],
                          in_=sa[:]),
                       reads=[("sa",)], writes=[("st_q", dt, ch)], dma=True,
                       name=f"sq6{ch}_{dt}")
        P.step("sync", None,
               reads=[("st_x5", i) for i in range(8)] +
                     [("st_q", dt, ch) for dt in range(8) for ch in range(NCH)],
               name="fin")
    return bld


def run_I(inputs, prep, fg_outs, y1_tok):
    in_specs = {
        "x4_T": ((DM, TL), F32), "ln5_T": ((DM, TL), F32),
        "y_T": ((2048, TL), F32), "z_T": ((2048, TL), BF16),
        "woutt": ((8, 128, 2048), BF16), "gam": ((128, 8), F32),
        "ones_col": ((128, 1), F32), "ones_row": ((1, 128), F32),
    }
    out_specs = {"x5": ((DM, TL), F32), "xq2_T": ((DM, TL), BF16)}
    nc = mkprog(build_I(), in_specs, out_specs)
    shared = {"woutt": prep["woutt"], "gam": _colscale(inputs["L1_ffn_norm_gamma"]),
              "ones_col": ONES_COL, "ones_row": ONES_ROW}
    in_maps = [dict(shared,
                    x4_T=np.ascontiguousarray(fg_outs[c]["x4"].T),
                    ln5_T=np.ascontiguousarray(fg_outs[c]["ln5"].T),
                    y_T=y1_tok[c], z_T=fg_outs[c]["z1"]) for c in range(NCORE)]
    res = run_bass_kernel_spmd(nc, in_maps, core_ids=list(range(NCORE)))
    return [res.results[c] for c in range(NCORE)]


def build_K():
    def bld(nc, P, ins, outs, es):
        m2s = sbuf(nc, es, "s_m2s", [128, 16 * 8], F32)
        P.load(m2s[:], ins["m2s"], writes=[("m2s",)])
        d2s = sbuf(nc, es, "s_d2s", [128, 16 * 8], F32)
        P.load(d2s[:], ins["d2s"], writes=[("d2s",)])
        x5n = sbuf(nc, es, "s_x5n", [128, 1024], F32)
        ntile = sbuf(nc, es, "s_ntile", [128, 1024], BF16)
        acc = sbuf(nc, es, "s_acc", [128, 1024], F32)
        srow8 = sbuf(nc, es, "s_srow8", [128, 8], F32)
        sd8 = sbuf(nc, es, "s_sd8", [128, 8], F32)
        mcol = sbuf(nc, es, "s_mcol", [128, 1], F32)
        dcol = sbuf(nc, es, "s_dcol", [128, 1], F32)
        for ts in range(16):
            P.load(x5n[:], ins["x5n"].rearrange("(c p) d -> c p d", p=128)[ts],
                   writes=[("x5n",)], name=f"ldx{ts}")
            P.step("vector", lambda e, ts=ts: e.reduce_max(
                      mcol[:], m2s[:, ts * 8:(ts + 1) * 8], axis=mybir.AxisListType.X),
                   reads=[("m2s",)], writes=[("mcol",)], name=f"kM{ts}")
            P.step("vector", lambda e: e.tensor_scalar_mul(mcol[:], mcol[:], -1.0),
                   reads=[("mcol",)], writes=[("mcol",)], name=f"kMn{ts}")
            P.step("scalar", lambda e, ts=ts: e.activation(
                      out=srow8[:], in_=m2s[:, ts * 8:(ts + 1) * 8], func=AF.Exp,
                      bias=mcol[:, 0:1]),
                   reads=[("m2s",), ("mcol",)], writes=[("srow8",)], name=f"kS{ts}")
            P.step("vector", lambda e, ts=ts: e.tensor_mul(
                      sd8[:], srow8[:], d2s[:, ts * 8:(ts + 1) * 8]),
                   reads=[("srow8",), ("d2s",)], writes=[("sd8",)], name=f"kSD{ts}")
            P.step("vector", lambda e: e.reduce_sum(dcol[:], sd8[:], axis=mybir.AxisListType.X),
                   reads=[("sd8",)], writes=[("dcol",)], name=f"kden{ts}")
            P.step("vector", lambda e: e.reciprocal(dcol[:], dcol[:]),
                   reads=[("dcol",)], writes=[("dcol",)], name=f"krc{ts}")
            for c in range(NCORE):
                P.load(ntile[:], ins["nTs"].rearrange("e (c p) d -> e c p d", p=128)[c, ts],
                       writes=[("ntile",)], name=f"kn{ts}_{c}")
                if c == 0:
                    P.step("vector", lambda e, c=c: e.tensor_scalar_mul(
                              acc[:], ntile[:], srow8[:, c:c + 1]),
                           reads=[("ntile",), ("srow8",)], writes=[("acc",)],
                           name=f"ka{ts}_{c}")
                else:
                    P.step("vector", lambda e, c=c: e.scalar_tensor_tensor(
                              out=acc[:], in0=ntile[:], scalar=srow8[:, c:c + 1],
                              in1=acc[:], op0=ALU.mult, op1=ALU.add),
                           reads=[("ntile",), ("srow8",), ("acc",)], writes=[("acc",)],
                           name=f"ka{ts}_{c}")
            P.step("vector", lambda e: e.tensor_scalar(
                      out=acc[:], in0=acc[:], scalar1=dcol[:, 0:1], scalar2=0.5,
                      op0=ALU.mult, op1=ALU.mult),
                   reads=[("acc",), ("dcol",)], writes=[("acc",)], name=f"kmx{ts}")
            P.step("vector", lambda e: e.tensor_add(x5n[:], x5n[:], acc[:]),
                   reads=[("acc",), ("x5n",)], writes=[("x5n",)], name=f"kx6{ts}")
            P.step("gpsimd", lambda e, ts=ts: e.dma_start(
                      out=outs["x6"].rearrange("(c p) d -> c p d", p=128)[ts], in_=x5n[:]),
                   reads=[("x5n",)], writes=[("st", ts % 4)], dma=True, name=f"ks{ts}")
        P.step("sync", None, reads=[("st", k) for k in range(4)], name="fin")
    return bld


def run_K(i_outs, j_outs):
    in_specs = {
        "x5n": ((TL, DM), F32), "nTs": ((NCORE, TL, DM), BF16),
        "m2s": ((128, 16 * 8), F32), "d2s": ((128, 16 * 8), F32),
    }
    out_specs = {"x6": ((TL, DM), F32)}
    nc = mkprog(build_K(), in_specs, out_specs)
    nT_nat = [np.ascontiguousarray(o["nT"].T) for o in j_outs]
    m2_flat = np.stack([o["mx2"].T.reshape(-1) for o in j_outs])
    d2_flat = np.stack([o["d2"].T.reshape(-1) for o in j_outs])
    in_maps = []
    for c in range(NCORE):
        tok_idx = np.r_[c * SL:(c + 1) * SL, SEQ + c * SL:SEQ + (c + 1) * SL]
        nTs = np.stack([nT_nat[k][tok_idx] for k in range(NCORE)])
        m2p = m2_flat[:, tok_idx]
        d2p = d2_flat[:, tok_idx]
        m2s = np.ascontiguousarray(m2p.reshape(8, 16, 128).transpose(2, 1, 0).reshape(128, 128))
        d2s = np.ascontiguousarray(d2p.reshape(8, 16, 128).transpose(2, 1, 0).reshape(128, 128))
        in_maps.append({"x5n": np.ascontiguousarray(i_outs[c]["x5"].T), "nTs": nTs,
                        "m2s": m2s.astype(np.float32), "d2s": d2s.astype(np.float32)})
    res = run_bass_kernel_spmd(nc, in_maps, core_ids=list(range(NCORE)))
    return [res.results[c]["x6"] for c in range(NCORE)]


def kernel(**inputs):
    inputs = {k: np.asarray(v) for k, v in inputs.items()}
    prep = run_A(inputs)
    b_outs = run_B(inputs, prep)
    y0_tok = run_C(b_outs, inputs)
    d_outs = run_D(inputs, prep, b_outs, y0_tok)
    xq_full = assemble_xq_full(d_outs)
    e_outs = run_E(inputs, xq_full, "L1_ffn")
    fg_outs = run_FG(inputs, prep, d_outs, e_outs)
    y1_tok = run_H(fg_outs, inputs)
    i_outs = run_I(inputs, prep, fg_outs, y1_tok)
    xq2_full = assemble_xq_full(i_outs, key="xq2_T")
    j_outs = run_E(inputs, xq2_full, "L1_ffn")
    x6 = run_K(i_outs, j_outs)
    return _untoks(x6)


# version 04
# speedup vs baseline: 1.4788x; 1.0773x over previous
import numpy as np
import ml_dtypes
import concourse.bass as bass
import concourse.mybir as mybir
from concourse.bass_utils import run_bass_kernel_spmd
from contextlib import ExitStack

F32 = mybir.dt.float32
BF16 = mybir.dt.bfloat16
AF = mybir.ActivationFunctionType
ALU = mybir.AluOpType
BF = ml_dtypes.bfloat16

NCORE = 8
NB, SEQ, DM = 2, 8192, 1024
SL = SEQ // NCORE
TL = NB * SL
EPS_LN = 1e-5


class Prog:
    COMPUTE = ("tensor", "vector", "scalar", "gpsimd")
    ALL = COMPUTE + ("sync",)
    NDMA = 16


    def __init__(self, nc, es):
        self.nc = nc
        self.es = es
        self.steps = []
        self.bufs = {}
        self.last_dma_on_sem = {}

    def _dma_sem_id(self, key):
        return hash(key) % self.NDMA

    def step(self, eng, emit, reads=(), writes=(), dma=False, name=""):
        raw = set()
        war = set()
        for b in reads:
            st = self.bufs.get(b)
            if st and st["w"] is not None:
                raw.add(st["w"])
        for b in writes:
            st = self.bufs.get(b)
            if st:
                if st["w"] is not None:
                    raw.add(st["w"])
                war.update(st["r"])
        deps = raw | war
        sid = len(self.steps)
        sem_id = None
        if dma:
            key = writes[0] if writes else name
            sem_id = self._dma_sem_id(key)
            prev = self.last_dma_on_sem.get(sem_id)
            if prev is not None:
                deps.add(prev)
            self.last_dma_on_sem[sem_id] = sid
        self.steps.append(dict(eng=eng, emit=emit, deps=deps, dma=dma,
                               sem_id=sem_id, name=name))
        for b in reads:
            self.bufs.setdefault(b, {"w": None, "r": []})["r"].append(sid)
        for b in writes:
            self.bufs[b] = {"w": sid, "r": []}
        return sid

    def load(self, dst_ap, src_ap, reads=(), writes=(), q="sync", name="ld"):
        return self.step(q, lambda e: e.dma_start(out=dst_ap, in_=src_ap),
                         reads=reads, writes=writes, dma=True, name=name)

    def finalize(self, block):
        nc = self.nc
        counts = {}
        for st in self.steps:
            if st["emit"] is None:
                sem = None
            elif st["dma"]:
                sem = f"d{st['sem_id']}"
            else:
                sem = f"e_{st['eng']}"
            if sem is not None:
                counts[sem] = counts.get(sem, 0) + (16 if st["dma"] else 1)
            st["sem"] = sem
            st["val"] = counts.get(sem, 0)
        sem_es = ExitStack()
        sems = {s: sem_es.enter_context(nc.semaphore(s)) for s in counts}
        by_eng = {e: [s for s in self.steps if s["eng"] == e] for e in self.ALL}
        progself = self

        def runner(eng_name):
            def run(eng):
                waited = {}
                for st in by_eng[eng_name]:
                    for d in sorted(st["deps"]):
                        ds = progself.steps[d]
                        if ds["sem"] is None:
                            raise RuntimeError(f"dep on barrier step {ds['name']}")
                        v = ds["val"]
                        if waited.get(ds["sem"], 0) >= v:
                            continue
                        eng.wait_ge(sems[ds["sem"]], v)
                        waited[ds["sem"]] = v
                    if st["emit"] is None:
                        continue
                    insts = st["emit"](eng)
                    if not isinstance(insts, (list, tuple)):
                        insts = [insts]
                    if st["dma"]:
                        if len(insts) != 1:
                            raise RuntimeError("dma steps must emit exactly 1 dma")
                        insts[0].then_inc(sems[st["sem"]], 16)
                    else:
                        insts[-1].then_inc(sems[st["sem"]], 1)
            return run

        for e in self.ALL:
            if not by_eng[e]:
                continue
            getattr(block, e)(runner(e))
        return sem_es


def mkprog(build_fn, in_specs, out_specs):
    nc = bass.Bass("TRN2", target_bir_lowering=False, debug=False)
    ins = {k: nc.dram_tensor(k, list(v[0]), v[1], kind="ExternalInput").ap()
           for k, v in in_specs.items()}
    outs = {k: nc.dram_tensor(k, list(v[0]), v[1], kind="ExternalOutput").ap()
            for k, v in out_specs.items()}
    es = ExitStack()
    with es:
        with nc.Block() as block:
            P = Prog(nc, es)
            build_fn(nc, P, ins, outs, es)
            sem_es = P.finalize(block)
            es.enter_context(sem_es)
    return nc


def sbuf(nc, es, name, shape, dt=F32):
    return es.enter_context(nc.sbuf_tensor(name, list(shape), dt))


def psum(nc, es, name, shape, dt=F32):
    return es.enter_context(nc.psum_tensor(name, list(shape), dt))


def build_A(nt):
    def bld(nc, P, ins, outs, es):
        blocks = ins["blocks"]
        obl = outs["tiles"]
        idt = sbuf(nc, es, "idt", [128, 128], BF16)
        P.load(idt[:], ins["idty"], writes=[("idt",)])
        nslot = 4
        raw = [sbuf(nc, es, f"raw{i}", [128, 128], F32) for i in range(nslot)]
        cst = [sbuf(nc, es, f"cst{i}", [128, 128], BF16) for i in range(nslot)]
        out = [sbuf(nc, es, f"out{i}", [128, 128], BF16) for i in range(nslot)]
        pst = [psum(nc, es, f"pst{i}", [128, 128], BF16) for i in range(nslot)]
        for i in range(nt):
            s = i % nslot
            P.load(raw[s][:], blocks[i], writes=[("raw", s)], name=f"ldw{i}")
            P.step("scalar", lambda e, s=s: e.copy(out=cst[s][:], in_=raw[s][:]),
                   reads=[("raw", s)], writes=[("cst", s)], name=f"cast{i}")
            P.step("tensor", lambda e, s=s: e.transpose(pst[s][:], cst[s][:], idt[:]),
                   reads=[("cst", s), ("idt",)], writes=[("pst", s)], name=f"tr{i}")
            P.step("vector", lambda e, s=s: e.tensor_copy(out=out[s][:], in_=pst[s][:]),
                   reads=[("pst", s)], writes=[("out", s)], name=f"ev{i}")
            P.step("gpsimd", lambda e, s=s, i=i: e.dma_start(out=obl[i], in_=out[s][:]),
                   reads=[("out", s)], writes=[("st", i)], dma=True, name=f"st{i}")
        last = [("st", i) for i in range(nt)]
        P.step("sync", None, reads=last, name="fin")
    return bld


def _toks(a):
    out = []
    for c in range(NCORE):
        out.append(np.ascontiguousarray(
            np.concatenate([a[0, c * SL:(c + 1) * SL], a[1, c * SL:(c + 1) * SL]], 0)))
    return out


def _untoks(slices, D=DM, dtype=np.float32):
    a = np.empty((NB, SEQ, D), dtype)
    for c in range(NCORE):
        a[0, c * SL:(c + 1) * SL] = slices[c][:SL]
        a[1, c * SL:(c + 1) * SL] = slices[c][SL:]
    return a


def _tileblocks(w):
    M, K = w.shape
    assert M % 128 == 0 and K % 128 == 0
    b = w.reshape(M // 128, 128, K // 128, 128).transpose(0, 2, 1, 3)
    return np.ascontiguousarray(b.reshape(-1, 128, 128).astype(np.float32)), (M // 128, K // 128)


def _assemble_T(tiles, mo, ko):
    t = tiles.reshape(mo, ko, 128, 128)
    t = t.transpose(0, 2, 1, 3)
    return np.ascontiguousarray(t.reshape(mo, 128, ko * 128))


def _colscale(s):
    M = s.shape[0]
    return np.ascontiguousarray(s.reshape(M // 128, 128).T.astype(np.float32))


IDTY = np.eye(128, dtype=np.float32)
IDTY_BF = np.eye(128, dtype=BF)
ONES_COL = np.ones((128, 1), np.float32)
ONES_ROW = np.ones((1, 128), np.float32)


def run_A(inputs):
    specs = [
        ("w1t", inputs["L0_ffn_w1_w"]),
        ("w2t", inputs["L0_ffn_w2_w"]),
        ("h1t", inputs["L0_ffn_h1_w"]),
        ("h2t", inputs["L0_ffn_h2_w"]),
        ("w3t", inputs["L0_ffn_w3_w"]),
        ("wint", inputs["L1_attn_W_in"]),
        ("woutt", inputs["L1_attn_W_out"]),
    ]
    blocks = []
    meta = []
    for name, w in specs:
        b, (mo, ko) = _tileblocks(w)
        meta.append((name, mo, ko, len(b)))
        blocks.append(b)
    allb = np.concatenate(blocks, 0)
    ntot = len(allb)
    per = (ntot + NCORE - 1) // NCORE
    pad = per * NCORE - ntot
    if pad:
        allb = np.concatenate([allb, np.zeros((pad, 128, 128), np.float32)], 0)
    nc = mkprog(build_A(per),
                {"blocks": ((per, 128, 128), F32), "idty": ((128, 128), BF16)},
                {"tiles": ((per, 128, 128), BF16)})
    in_maps = [{"blocks": allb[c * per:(c + 1) * per], "idty": IDTY_BF}
               for c in range(NCORE)]
    res = run_bass_kernel_spmd(nc, in_maps, core_ids=list(range(NCORE)))
    tiles = np.concatenate([res.results[c]["tiles"] for c in range(NCORE)], 0)[:ntot]
    out = {}
    off = 0
    for name, mo, ko, n in meta:
        out[name] = _assemble_T(tiles[off:off + n], mo, ko)
        off += n
    return out



def build_B(variant="B"):
    TKC = 512
    NCH = TL // TKC

    def bld(nc, P, ins, outs, es):
        sc = {}
        sc["ones_col"] = sbuf(nc, es, "s_ones_col", [128, 1], F32)
        sc["ones_row"] = sbuf(nc, es, "s_ones_row", [1, 128], F32)
        idtb = sbuf(nc, es, "idtb", [128, 128], BF16)
        P.load(sc["ones_col"][:], ins["ones_col"], writes=[("ones_col",)])
        P.load(sc["ones_row"][:], ins["ones_row"], writes=[("ones_row",)])
        P.load(idtb[:], ins["idty"], writes=[("idtb",)])
        scl_list = [("w1s", [128, 32]), ("w2s", [128, 32]), ("h1s", [128, 32]),
                    ("h2s", [128, 32]), ("w3s", [128, 8])]
        scl_list += [("bdt", [128, 8])] if variant == "B" else [("gam", [128, 8])]
        for nm, shape in scl_list:
            sc[nm] = sbuf(nc, es, "s_" + nm, shape, F32)
            P.load(sc[nm][:], ins[nm], writes=[(nm,)])
        pb = [psum(nc, es, f"pb{i}", [128, 512], F32) for i in range(7)]
        XR = [sbuf(nc, es, f"XR{i}", [128, TL], F32) for i in range(8)]
        xn1 = [sbuf(nc, es, f"xn1_{i}", [128, TL], BF16) for i in range(8)]
        g = [sbuf(nc, es, f"g{i}", [128, TKC], BF16) for i in range(32)]
        g2 = [sbuf(nc, es, f"g2_{i}", [128, TKC], BF16) for i in range(32)]
        wsl = [sbuf(nc, es, f"wsl{i}", [128, 4096], BF16) for i in range(2)]
        if variant == "B":
            for i in range(8):
                P.load(XR[i][:], ins["x_T"].rearrange("(a p) t -> a p t", p=128)[i],
                       writes=[("XR", i)], name=f"ldx{i}")
        else:
            yt = [sbuf(nc, es, f"yt{i}", [128, TL], F32) for i in range(2)]
            for i in range(8):
                P.load(XR[i][:], ins["x1_T"].rearrange("(a p) t -> a p t", p=128)[i],
                       writes=[("XR", i)], name=f"ldx{i}")
                P.load(yt[i % 2][:], ins["y_T"].rearrange("(a p) t -> a p t", p=128)[i],
                       writes=[("yt", i % 2)], name=f"ldy{i}")
                P.step("vector", lambda e, i=i: e.tensor_add(XR[i][:], XR[i][:], yt[i % 2][:]),
                       reads=[("yt", i % 2), ("XR", i)], writes=[("XR", i)], name=f"addy{i}")
        sc["sq"] = sbuf(nc, es, "sq", [128, 512], F32)
        sc["ceps"] = sbuf(nc, es, "ceps", [128, 1], F32)
        P.step("gpsimd", lambda e: e.memset(sc["ceps"][:], EPS_LN), writes=[("ceps",)], name="ceps")
        for nm in ("mrow", "wrow", "rrow", "mbrow"):
            sc[nm] = sbuf(nc, es, nm, [1, 512], F32)
        sc["ap1"] = sbuf(nc, es, "ap1", [128, 512], F32)
        sc["sa"] = sbuf(nc, es, "sa", [128, 512], BF16)

        def ln_pass(pref, dst_put):
            for ch in range(NCH):
                cs = slice(ch * TKC, (ch + 1) * TKC)
                for dt in range(8):
                    P.step("scalar", lambda e, dt=dt, cs=cs: e.square(out=sc["sq"][:], in_=XR[dt][:, cs]),
                           reads=[("XR", dt)], writes=[("sq",)], name=f"{pref}sq{ch}_{dt}")
                    P.step("tensor", lambda e, dt=dt, cs=cs: e.matmul(
                              pb[0][0:1, :], sc["ones_col"][:], XR[dt][:, cs],
                              start=(dt == 0), stop=(dt == 7)),
                           reads=[("XR", dt), ("ones_col",)], writes=[("pbank", 0)],
                           name=f"{pref}sx{ch}_{dt}")
                    P.step("tensor", lambda e, dt=dt: e.matmul(
                              pb[1][0:1, :], sc["ones_col"][:], sc["sq"][:],
                              start=(dt == 0), stop=(dt == 7)),
                           reads=[("sq",), ("ones_col",)], writes=[("pbank", 1)],
                           name=f"{pref}sq2{ch}_{dt}")
                P.step("vector", lambda e: e.tensor_scalar_mul(sc["mrow"][:], pb[0][0:1, :], 1.0 / 1024),
                       reads=[("pbank", 0)], writes=[("mrow",)], name=f"{pref}m{ch}")
                P.step("vector", lambda e: e.tensor_scalar_mul(sc["wrow"][:], pb[1][0:1, :], 1.0 / 1024),
                       reads=[("pbank", 1)], writes=[("wrow",)], name=f"{pref}e{ch}")
                P.step("vector", lambda e: e.tensor_mul(sc["mbrow"][:], sc["mrow"][:], sc["mrow"][:]),
                       reads=[("mrow",)], writes=[("mbrow",)], name=f"{pref}m2{ch}")
                P.step("vector", lambda e: e.tensor_sub(sc["wrow"][:], sc["wrow"][:], sc["mbrow"][:]),
                       reads=[("mbrow",), ("wrow",)], writes=[("wrow",)], name=f"{pref}v{ch}")
                P.step("scalar", lambda e: e.activation(out=sc["wrow"][:], in_=sc["wrow"][:],
                                                        func=AF.Ln, bias=sc["ceps"][0:1, :]),
                       reads=[("wrow",), ("ceps",)], writes=[("wrow",)], name=f"{pref}l{ch}")
                P.step("scalar", lambda e: e.activation(out=sc["rrow"][:], in_=sc["wrow"][:],
                                                        func=AF.Exp, scale=-0.5),
                       reads=[("wrow",)], writes=[("rrow",)], name=f"{pref}r{ch}")
                P.step("vector", lambda e: e.scalar_tensor_tensor(
                          out=sc["mbrow"][:], in0=sc["mrow"][:], scalar=-1.0, in1=sc["rrow"][:],
                          op0=ALU.mult, op1=ALU.mult),
                       reads=[("mrow",), ("rrow",)], writes=[("mbrow",)], name=f"{pref}mb{ch}")
                P.step("tensor", lambda e: e.matmul(pb[2][:], sc["ones_row"][:], sc["rrow"][:],
                                                    start=True, stop=True),
                       reads=[("rrow",), ("ones_row",)], writes=[("pbank", 2)], name=f"{pref}bcr{ch}")
                P.step("tensor", lambda e: e.matmul(pb[3][:], sc["ones_row"][:], sc["mbrow"][:],
                                                    start=True, stop=True),
                       reads=[("mbrow",), ("ones_row",)], writes=[("pbank", 3)], name=f"{pref}bcm{ch}")
                for dt in range(8):
                    P.step("vector", lambda e, dt=dt, cs=cs: e.tensor_mul(
                              sc["ap1"][:], XR[dt][:, cs], pb[2][:]),
                           reads=[("XR", dt), ("pbank", 2)], writes=[("ap1",)],
                           name=f"{pref}ap1_{ch}_{dt}")
                    dst_put(dt, ch)

        def put_xn1(dt, ch):
            cs = slice(ch * TKC, (ch + 1) * TKC)
            P.step("vector", lambda e, dt=dt, cs=cs: e.tensor_add(
                      xn1[dt][:, cs], sc["ap1"][:], pb[3][:]),
                   reads=[("ap1",), ("pbank", 3)], writes=[("xn1", dt, cs.start)],
                   name=f"xn1_{ch}_{dt}")
        ln_pass("ln1", put_xn1)

        def ffn_layer(pref, ch, srcs, src_keys, wname, nko, mos, evac):
            cs = slice(ch * TKC, (ch + 1) * TKC)
            for mi, mo in enumerate(mos):
                slot = mi % len(wsl)
                bank = mi % 2
                P.load(wsl[slot][:, :nko * 128], ins[wname].rearrange("m p f -> m p f")[mo],
                       writes=[("wsl", slot)], q="sync", name=f"{pref}w{ch}_{mo}")
                def mmgroup(e, slot=slot, bank=bank, cs=cs, nko=nko, srcs=srcs):
                    insts = []
                    for ko in range(nko):
                        insts.append(e.matmul(
                            pb[bank][:], wsl[slot][:, ko * 128:(ko + 1) * 128],
                            srcs[ko][:, cs], start=(ko == 0), stop=(ko == nko - 1)))
                    return insts[-1]
                P.step("tensor", mmgroup,
                       reads=[("wsl", slot)] + [src_keys(ko, cs.start) for ko in range(nko)],
                       writes=[("pbank", bank)], name=f"{pref}mm{ch}_{mo}")
                evac(mo, ("pbank", bank), pb[bank])

        for ch in range(NCH):
            cs = slice(ch * TKC, (ch + 1) * TKC)
            for mi in range(32):
                slot = 0
                slot2 = 1
                bank_a, bank_b = (0, 1) if mi % 2 == 0 else (2, 3)
                P.load(wsl[slot][:, :1024], ins["w1t"][mi], writes=[("wsl", slot)],
                       q="sync", name=f"fw1_{ch}_{mi}")
                def mma(e, slot=slot, bank=bank_a, cs=cs):
                    for ko in range(8):
                        i = e.matmul(pb[bank][:], wsl[slot][:, ko * 128:(ko + 1) * 128],
                                     xn1[ko][:, cs], start=(ko == 0), stop=(ko == 7))
                    return i
                P.step("tensor", mma,
                       reads=[("wsl", slot)] + [("xn1", ko, cs.start) for ko in range(8)],
                       writes=[("pbank", bank_a)], name=f"fma_{ch}_{mi}")
                P.step("scalar", lambda e, bank=bank_a, mi=mi: e.activation(
                          out=sc["sa"][:], in_=pb[bank][:], func=AF.Silu,
                          scale=sc["w1s"][:, mi:mi + 1]),
                       reads=[("pbank", bank_a), ("w1s",)], writes=[("sa",)],
                       name=f"fsa_{ch}_{mi}")
                P.load(wsl[slot2][:, :1024], ins["w2t"][mi], writes=[("wsl", slot2)],
                       q="sync", name=f"fw2_{ch}_{mi}")
                def mmb(e, slot=slot2, bank=bank_b, cs=cs):
                    for ko in range(8):
                        i = e.matmul(pb[bank][:], wsl[slot][:, ko * 128:(ko + 1) * 128],
                                     xn1[ko][:, cs], start=(ko == 0), stop=(ko == 7))
                    return i
                P.step("tensor", mmb,
                       reads=[("wsl", slot2)] + [("xn1", ko, cs.start) for ko in range(8)],
                       writes=[("pbank", bank_b)], name=f"fmb_{ch}_{mi}")
                P.step("vector", lambda e, mi=mi, bank=bank_b: e.scalar_tensor_tensor(
                          out=g[mi][:], in0=pb[bank][:], scalar=sc["w2s"][:, mi:mi + 1],
                          in1=sc["sa"][:], op0=ALU.mult, op1=ALU.mult),
                       reads=[("pbank", bank_b), ("w2s",), ("sa",)], writes=[("g", mi)],
                       name=f"fg_{ch}_{mi}")
            for mi in range(32):
                slot = mi % 2
                bank = 4 + (mi % 2)
                P.load(wsl[slot][:], ins["h1t"][mi], writes=[("wsl", slot)],
                       q="sync", name=f"fh1_{ch}_{mi}")
                def mmh(e, slot=slot, bank=bank):
                    for ko in range(32):
                        i = e.matmul(pb[bank][:], wsl[slot][:, ko * 128:(ko + 1) * 128],
                                     g[ko][:], start=(ko == 0), stop=(ko == 31))
                    return i
                P.step("tensor", mmh,
                       reads=[("wsl", slot)] + [("g", ko) for ko in range(32)],
                       writes=[("pbank", bank)], name=f"fmh1_{ch}_{mi}")
                P.step("scalar", lambda e, bank=bank, mi=mi: e.activation(
                          out=g2[mi][:], in_=pb[bank][:], func=AF.Gelu,
                          scale=sc["h1s"][:, mi:mi + 1]),
                       reads=[("pbank", bank), ("h1s",)], writes=[("g2", mi)],
                       name=f"fg2_{ch}_{mi}")
            for mi in range(32):
                slot = mi % 2
                bank = 4 + (mi % 2)
                P.load(wsl[slot][:], ins["h2t"][mi], writes=[("wsl", slot)],
                       q="sync", name=f"fh2_{ch}_{mi}")
                def mmh2(e, slot=slot, bank=bank):
                    for ko in range(32):
                        i = e.matmul(pb[bank][:], wsl[slot][:, ko * 128:(ko + 1) * 128],
                                     g2[ko][:], start=(ko == 0), stop=(ko == 31))
                    return i
                P.step("tensor", mmh2,
                       reads=[("wsl", slot)] + [("g2", ko) for ko in range(32)],
                       writes=[("pbank", bank)], name=f"fmh2_{ch}_{mi}")
                P.step("scalar", lambda e, bank=bank, mi=mi: e.activation(
                          out=g[mi][:], in_=pb[bank][:], func=AF.Gelu,
                          scale=sc["h2s"][:, mi:mi + 1]),
                       reads=[("pbank", bank), ("h2s",)], writes=[("g", mi)],
                       name=f"fg3_{ch}_{mi}")
            for mi in range(8):
                slot = mi % 2
                bank = 6
                P.load(wsl[slot][:], ins["w3t"][mi], writes=[("wsl", slot)],
                       q="sync", name=f"fw3_{ch}_{mi}")
                def mmo(e, slot=slot, bank=bank):
                    for ko in range(32):
                        i = e.matmul(pb[bank][:], wsl[slot][:, ko * 128:(ko + 1) * 128],
                                     g[ko][:], start=(ko == 0), stop=(ko == 31))
                    return i
                P.step("tensor", mmo,
                       reads=[("wsl", slot)] + [("g", ko) for ko in range(32)],
                       writes=[("pbank", bank)], name=f"fmo_{ch}_{mi}")
                P.step("scalar", lambda e, bank=bank, mi=mi: e.activation(
                          out=sc["ap1"][:], in_=pb[bank][:], func=AF.Copy,
                          scale=sc["w3s"][:, mi:mi + 1]),
                       reads=[("pbank", bank), ("w3s",)], writes=[("ap1",)],
                       name=f"fto_{ch}_{mi}")
                P.step("vector", lambda e, mi=mi, cs=cs: e.scalar_tensor_tensor(
                          out=XR[mi][:, cs], in0=sc["ap1"][:], scalar=0.5,
                          in1=XR[mi][:, cs], op0=ALU.mult, op1=ALU.add),
                       reads=[("ap1",), ("XR", mi)], writes=[("XR", mi)],
                       name=f"fxr_{ch}_{mi}")

        if variant == "D":
            def put_ln4(dt, ch):
                cs = slice(ch * TKC, (ch + 1) * TKC)
                P.step("vector", lambda e, dt=dt, cs=cs: e.tensor_add(
                          xn1[dt][:, cs], sc["ap1"][:], pb[3][:]),
                       reads=[("ap1",), ("pbank", 3)], writes=[("xn1", dt, cs.start)],
                       name=f"ln4_{ch}_{dt}")
            ln_pass("ln4", put_ln4)
            r32 = sbuf(nc, es, "r32", [1, 128], F32)
            P.step("gpsimd", lambda e: e.memset(r32[:], float(np.sqrt(DM))),
                   writes=[("r32",)], name="c32")
            for ch in range(NCH):
                cs = slice(ch * TKC, (ch + 1) * TKC)
                for dt in range(8):
                    P.step("scalar", lambda e, dt=dt, cs=cs: e.square(
                              out=sc["sq"][:], in_=xn1[dt][:, cs]),
                           reads=[("xn1", dt, cs.start)], writes=[("sq",)],
                           name=f"qsq{ch}_{dt}")
                    P.step("tensor", lambda e, dt=dt: e.matmul(
                              pb[0][0:1, :], sc["ones_col"][:], sc["sq"][:],
                              start=(dt == 0), stop=(dt == 7)),
                           reads=[("sq",), ("ones_col",)], writes=[("pbank", 0)],
                           name=f"qs{ch}_{dt}")
                P.step("scalar", lambda e: e.activation(out=sc["wrow"][:], in_=pb[0][0:1, :],
                                                        func=AF.Ln),
                       reads=[("pbank", 0)], writes=[("wrow",)], name=f"qln{ch}")
                P.step("scalar", lambda e: e.activation(out=sc["rrow"][:], in_=sc["wrow"][:],
                                                        func=AF.Exp, scale=-0.5),
                       reads=[("wrow",)], writes=[("rrow",)], name=f"qr{ch}")
                P.step("tensor", lambda e: e.matmul(pb[2][:], r32[:], sc["rrow"][:],
                                                    start=True, stop=True),
                       reads=[("rrow",), ("r32",)], writes=[("pbank", 2)], name=f"qbc{ch}")
                for dt in range(8):
                    P.step("vector", lambda e, dt=dt, cs=cs: e.scalar_tensor_tensor(
                              out=sc["sa"][:], in0=xn1[dt][:, cs],
                              scalar=sc["gam"][:, dt:dt + 1], in1=pb[2][:],
                              op0=ALU.mult, op1=ALU.mult),
                           reads=[("xn1", dt, cs.start), ("gam",), ("pbank", 2)],
                           writes=[("sa",)], name=f"qm{ch}_{dt}")
                    P.step("gpsimd", lambda e, dt=dt, ch=ch: e.dma_start(
                              out=outs["xq_T"][dt * 128:(dt + 1) * 128,
                                               ch * TKC:(ch + 1) * TKC],
                              in_=sc["sa"][:]),
                           reads=[("sa",)], writes=[("st_xq", dt, ch)], dma=True,
                           name=f"stxq{ch}_{dt}")
            for i in range(8):
                P.step("gpsimd", lambda e, i=i: e.dma_start(
                          out=outs["x3"][i * 128:(i + 1) * 128, :], in_=XR[i][:]),
                       reads=[("XR", i)], writes=[("st_x1", i)], dma=True, name=f"stx3_{i}")
            fin_reads = ([("st_x1", i) for i in range(8)] +
                         [("st_xq", dt, ch) for dt in range(8) for ch in range(NCH)])
            P.step("sync", None, reads=fin_reads, name="fin")
            return

        def put_xn2(dt, ch):
            cs = slice(ch * TKC, (ch + 1) * TKC)
            P.step("vector", lambda e, dt=dt: e.tensor_add(
                      sc["sq"][:], sc["ap1"][:], pb[3][:]),
                   reads=[("ap1",), ("pbank", 3)], writes=[("sq",)], name=f"xn2f_{ch}_{dt}")
            P.step("scalar", lambda e, dt=dt, cs=cs: e.copy(out=xn1[dt][:, cs], in_=sc["sq"][:]),
                   reads=[("sq",)], writes=[("xn1", dt, cs.start)], name=f"xn2b_{ch}_{dt}")
            P.step("gpsimd", lambda e, dt=dt, ch=ch: e.dma_start(
                      out=outs["xn2"][dt * 128:(dt + 1) * 128, ch * TKC:(ch + 1) * TKC],
                      in_=sc["sq"][:]),
                   reads=[("sq",)], writes=[("st_xn2", dt, ch)], dma=True,
                   name=f"stxn2_{ch}_{dt}")
        ln_pass("ln2", put_xn2)

        wxpb = sbuf(nc, es, "wxpb", [96, 1024], BF16)
        P.load(wxpb[:], ins["wxp"], writes=[("wxpb",)], q="gpsimd", name="ldwxp")
        xpjT = sbuf(nc, es, "xpjT", [128, 8 * 96], BF16)
        pT = psum(nc, es, "pT", [128, 128], BF16)
        for ko in range(8):
            P.step("tensor", lambda e, ko=ko: e.transpose(
                      pT[:, 0:96], wxpb[:, ko * 128:(ko + 1) * 128], idtb[0:96, 0:96]),
                   reads=[("wxpb",), ("idtb",)], writes=[("pT",)], name=f"trxp{ko}")
            P.step("scalar", lambda e, ko=ko: e.copy(out=xpjT[:, ko * 96:(ko + 1) * 96],
                                                     in_=pT[:, 0:96]),
                   reads=[("pT",)], writes=[("xpjT", ko)], name=f"evxp{ko}")
        xpT = sbuf(nc, es, "xpT", [96, 512], F32)
        xpdt = sbuf(nc, es, "xpdt", [64, 512], BF16)
        zrow = sbuf(nc, es, "zrow", [16, 512], BF16)
        P.step("gpsimd", lambda e: e.memset(zrow[:], 0.0), writes=[("zrow",)], name="z0")
        wdtb = sbuf(nc, es, "wdtb", [128, 8 * 64], BF16)
        P.load(wdtb[:].rearrange("p (a r) -> p a r", a=8),
               ins["wdt"].rearrange("(a p) r -> p a r", p=128),
               writes=[("wdtb",)], q="gpsimd", name="ldwdt")
        wdtT = sbuf(nc, es, "wdtT", [64, 1024], BF16)
        for dt in range(8):
            P.step("tensor", lambda e, dt=dt: e.transpose(
                      pT[0:64, 0:128], wdtb[:, dt * 64:(dt + 1) * 64], idtb[:]),
                   reads=[("wdtb",), ("idtb",)], writes=[("pT",)], name=f"trdt{dt}")
            P.step("scalar", lambda e, dt=dt: e.copy(out=wdtT[0:64, dt * 128:(dt + 1) * 128],
                                                     in_=pT[0:64, 0:128]),
                   reads=[("pT",)], writes=[("wdtT", dt)], name=f"evdt{dt}")
        r1 = sc["ap1"]
        r2 = sc["sq"]
        for ch in range(NCH):
            cs = slice(ch * TKC, (ch + 1) * TKC)
            def mmxp(e, cs=cs):
                for ko in range(8):
                    i = e.matmul(pb[0][0:96, :], xpjT[:, ko * 96:(ko + 1) * 96],
                                 xn1[ko][:, cs], start=(ko == 0), stop=(ko == 7))
                return i
            P.step("tensor", mmxp,
                   reads=[("xpjT", ko) for ko in range(8)] +
                         [("xn1", ko, cs.start) for ko in range(8)],
                   writes=[("pbank", 0)], name=f"mmxp{ch}")
            P.step("scalar", lambda e: e.copy(out=xpT[:], in_=pb[0][0:96, :]),
                   reads=[("pbank", 0)], writes=[("xpT",)], name=f"evxpT{ch}")
            P.step("scalar", lambda e: e.copy(out=xpdt[:], in_=xpT[0:64, :]),
                   reads=[("xpT",)], writes=[("xpdt",)], name=f"cxpdt{ch}")
            P.step("gpsimd", lambda e, cs=cs: e.dma_start(
                      out=outs["bc0"][0:16, cs], in_=xpT[64:80, :]),
                   reads=[("xpT",)], writes=[("st_b", cs.start)], dma=True, name=f"stB{ch}")
            P.step("gpsimd", lambda e, cs=cs: e.dma_start(
                      out=outs["bc0"][32:48, cs], in_=xpT[80:96, :]),
                   reads=[("xpT",)], writes=[("st_c", cs.start)], dma=True, name=f"stC{ch}")
            P.step("gpsimd", lambda e, cs=cs: e.dma_start(
                      out=outs["bc0"][16:32, cs], in_=zrow[:]),
                   reads=[("zrow",)], writes=[("st_z", cs.start)], dma=True, name=f"stZ{ch}")
            for mo in range(8):
                P.step("tensor", lambda e, mo=mo: e.matmul(
                          pb[1][:], wdtT[:, mo * 128:(mo + 1) * 128], xpdt[:],
                          start=True, stop=True),
                       reads=[("wdtT", mo), ("xpdt",)], writes=[("pbank", 1)],
                       name=f"mmdt{mo}_{ch}")
                P.step("scalar", lambda e, mo=mo: e.activation(
                          out=r1[:], in_=pb[1][:], func=AF.Relu, bias=sc["bdt"][:, mo:mo + 1]),
                       reads=[("pbank", 1), ("bdt",)], writes=[("ap1",)], name=f"d1_{mo}_{ch}")
                P.step("scalar", lambda e, mo=mo: e.activation(
                          out=r2[:], in_=pb[1][:], func=AF.Abs, bias=sc["bdt"][:, mo:mo + 1]),
                       reads=[("pbank", 1), ("bdt",)], writes=[("sq",)], name=f"d2_{mo}_{ch}")
                P.step("scalar", lambda e: e.activation(out=r2[:], in_=r2[:], func=AF.Exp, scale=-1.0),
                       reads=[("sq",)], writes=[("sq",)], name=f"d3_{mo}_{ch}")
                P.step("scalar", lambda e: e.activation(out=r2[:], in_=r2[:], func=AF.Ln, bias=1.0),
                       reads=[("sq",)], writes=[("sq",)], name=f"d4_{mo}_{ch}")
                P.step("vector", lambda e: e.tensor_add(r2[:], r2[:], r1[:]),
                       reads=[("sq",), ("ap1",)], writes=[("sq",)], name=f"d5_{mo}_{ch}")
                P.step("gpsimd", lambda e, mo=mo, ch=ch: e.dma_start(
                          out=outs["delta0"][mo * 128:(mo + 1) * 128, ch * TKC:(ch + 1) * TKC],
                          in_=r2[:]),
                       reads=[("sq",)], writes=[("st_d", mo, ch)], dma=True,
                       name=f"std{mo}_{ch}")


        for i in range(8):
            P.step("gpsimd", lambda e, i=i: e.dma_start(
                      out=outs["x1"][i * 128:(i + 1) * 128, :], in_=XR[i][:]),
                   reads=[("XR", i)], writes=[("st_x1", i)], dma=True, name=f"stx1_{i}")

        fin_reads = ([("st_x1", i) for i in range(8)] + [("st_b", k * 512) for k in range(4)] + [("st_c", k * 512) for k in range(4)] + [("st_z", k * 512) for k in range(4)] +
                     [("st_xn2", dt, ch) for dt in range(8) for ch in range(NCH)] +
                     [("st_d", mo, ch) for mo in range(8) for ch in range(NCH)])
        P.step("sync", None, reads=fin_reads, name="fin")
    return bld


def run_B(inputs, prep):
    in_specs = {
        "x_T": ((DM, TL), F32),
        "w1t": ((32, 128, 1024), BF16), "w2t": ((32, 128, 1024), BF16),
        "h1t": ((32, 128, 4096), BF16), "h2t": ((32, 128, 4096), BF16),
        "w3t": ((8, 128, 4096), BF16),
        "w1s": ((128, 32), F32), "w2s": ((128, 32), F32),
        "h1s": ((128, 32), F32), "h2s": ((128, 32), F32), "w3s": ((128, 8), F32),
        "wxp": ((96, 1024), F32), "wdt": ((1024, 64), F32), "bdt": ((128, 8), F32),
        "idty": ((128, 128), BF16),
        "ones_col": ((128, 1), F32), "ones_row": ((1, 128), F32),
    }
    out_specs = {
        "x1": ((DM, TL), F32), "xn2": ((DM, TL), F32),
        "delta0": ((DM, TL), F32), "bc0": ((48, TL), F32),
    }
    nc = mkprog(build_B("B"), in_specs, out_specs)
    xs = _toks(inputs["L0_attn_x"])
    shared = {
        "w1t": prep["w1t"], "w2t": prep["w2t"], "h1t": prep["h1t"],
        "h2t": prep["h2t"], "w3t": prep["w3t"],
        "w1s": _colscale(inputs["L0_ffn_w1_s"]), "w2s": _colscale(inputs["L0_ffn_w2_s"]),
        "h1s": _colscale(inputs["L0_ffn_h1_s"]), "h2s": _colscale(inputs["L0_ffn_h2_s"]),
        "w3s": _colscale(inputs["L0_ffn_w3_s"]),
        "wxp": inputs["L0_attn_W_xproj"].astype(np.float32),
        "wdt": inputs["L0_attn_W_dt"].astype(np.float32),
        "bdt": _colscale(inputs["L0_attn_b_dt"]),
        "idty": IDTY_BF, "ones_col": ONES_COL, "ones_row": ONES_ROW,
    }
    in_maps = [dict(shared, x_T=np.ascontiguousarray(xs[c].T)) for c in range(NCORE)]
    res = run_bass_kernel_spmd(nc, in_maps, core_ids=list(range(NCORE)))
    return [res.results[c] for c in range(NCORE)]



def run_D(inputs, prep, b_outs, y_tok):
    in_specs = {
        "x1_T": ((DM, TL), F32), "y_T": ((DM, TL), F32),
        "w1t": ((32, 128, 1024), BF16), "w2t": ((32, 128, 1024), BF16),
        "h1t": ((32, 128, 4096), BF16), "h2t": ((32, 128, 4096), BF16),
        "w3t": ((8, 128, 4096), BF16),
        "w1s": ((128, 32), F32), "w2s": ((128, 32), F32),
        "h1s": ((128, 32), F32), "h2s": ((128, 32), F32), "w3s": ((128, 8), F32),
        "gam": ((128, 8), F32),
        "idty": ((128, 128), BF16),
        "ones_col": ((128, 1), F32), "ones_row": ((1, 128), F32),
    }
    out_specs = {"x3": ((DM, TL), F32), "xq_T": ((DM, TL), BF16)}
    nc = mkprog(build_B("D"), in_specs, out_specs)
    shared = {
        "w1t": prep["w1t"], "w2t": prep["w2t"], "h1t": prep["h1t"],
        "h2t": prep["h2t"], "w3t": prep["w3t"],
        "w1s": _colscale(inputs["L0_ffn_w1_s"]), "w2s": _colscale(inputs["L0_ffn_w2_s"]),
        "h1s": _colscale(inputs["L0_ffn_h1_s"]), "h2s": _colscale(inputs["L0_ffn_h2_s"]),
        "w3s": _colscale(inputs["L0_ffn_w3_s"]),
        "gam": _colscale(inputs["L1_ffn_norm_gamma"]),
        "idty": IDTY_BF, "ones_col": ONES_COL, "ones_row": ONES_ROW,
    }
    in_maps = [dict(shared, x1_T=b_outs[c]["x1"], y_T=y_tok[c]) for c in range(NCORE)]
    res = run_bass_kernel_spmd(nc, in_maps, core_ids=list(range(NCORE)))
    return [res.results[c] for c in range(NCORE)]


def build_C():
    TC = 1024
    NOC = (NB * SEQ) // TC

    def bld(nc, P, ins, outs, es):
        sel = sbuf(nc, es, "s_sel", [48, 16 * 128], F32)
        P.load(sel[:], ins["sel16"], writes=[("sel",)])
        idtb = sbuf(nc, es, "s_idtb", [128, 128], BF16)
        P.load(idtb[:], ins["idty"], writes=[("idtb",)])
        alog = sbuf(nc, es, "s_alog", [128, 16], F32)
        P.load(alog[:], ins["alog"], writes=[("alog",)])
        dsl = sbuf(nc, es, "s_dsl", [128, 1], F32)
        P.load(dsl[:], ins["dsl"], writes=[("dsl",)])
        A = sbuf(nc, es, "A", [128, 16], F32)
        P.step("scalar", lambda e: e.activation(out=A[:], in_=alog[:], func=AF.Exp),
               reads=[("alog",)], writes=[("A",)], name="expA")
        P.step("vector", lambda e: e.tensor_scalar_mul(A[:], A[:], -1.0),
               reads=[("A",)], writes=[("A",)], name="negA")
        carry = sbuf(nc, es, "carry", [128, 16], F32)
        zcol = sbuf(nc, es, "zcol", [128, 1], F32)
        P.step("gpsimd", lambda e: e.memset(zcol[:], 0.0), writes=[("zcol",)], name="z")
        uc = [sbuf(nc, es, f"uc{i}", [128, TC], F32) for i in range(2)]
        dc = [sbuf(nc, es, f"dc{i}", [128, TC], F32) for i in range(2)]
        bcc = [sbuf(nc, es, f"bcc{i}", [48, TC], F32) for i in range(2)]
        wcb = sbuf(nc, es, "wcb", [128, TC], BF16)
        Bbs = [sbuf(nc, es, f"Bbs{i}", [128, TC], BF16) for i in range(3)]
        Cbs = [sbuf(nc, es, f"Cbs{i}", [128, TC], BF16) for i in range(3)]
        dac = [sbuf(nc, es, f"dac{i}", [128, TC], BF16) for i in range(4)]
        bbc = [sbuf(nc, es, f"bbc{i}", [128, TC], BF16) for i in range(4)]
        hcb = [sbuf(nc, es, f"hcb{i}", [128, TC], BF16) for i in range(4)]
        ymb = [sbuf(nc, es, f"ymb{i}", [128, TC], BF16) for i in range(4)]
        yac = sbuf(nc, es, "yac", [128, TC], F32)
        ymf = sbuf(nc, es, "ymf", [128, TC], F32)
        pbb = psum(nc, es, "pbb", [128, TC], F32)
        pbc = psum(nc, es, "pbc", [128, TC], F32)
        pyy = psum(nc, es, "pyy", [128, TC], F32)

        for oc in range(NOC):
            s = oc % 2
            cs = slice(oc * TC, (oc + 1) * TC)
            P.load(uc[s][:], ins["u0"][:, cs], writes=[("uc", s)], name=f"ldu{oc}")
            P.load(dc[s][:], ins["d0"][:, cs], writes=[("dc", s)], name=f"ldd{oc}")
            P.load(bcc[s][:], ins["bc"][:, cs], writes=[("bcc", s)], name=f"ldbc{oc}")
            P.step("vector", lambda e, s=s: e.tensor_mul(wcb[:], dc[s][:], uc[s][:]),
                   reads=[("dc", s), ("uc", s)], writes=[("wcb",)], name=f"w{oc}")

            def emit_bcast(n):
                p = n % 3
                def mmb(e, s=s, n=n):
                    e.matmul(pbb[:, 0:512], sel[0:16, n * 128:(n + 1) * 128],
                             bcc[s][0:16, 0:512], start=True, stop=True)
                    return e.matmul(pbb[:, 512:1024], sel[0:16, n * 128:(n + 1) * 128],
                                    bcc[s][0:16, 512:1024], start=True, stop=True)
                P.step("tensor", mmb, reads=[("sel",), ("bcc", s)], writes=[("psB",)],
                       name=f"bb{oc}_{n}")
                P.step("scalar", lambda e, p=p: e.copy(out=Bbs[p][:], in_=pbb[:]),
                       reads=[("psB",)], writes=[("Bbs", p)], name=f"be{oc}_{n}")
                def mmc(e, s=s, n=n):
                    e.matmul(pbc[:, 0:512], sel[32:48, n * 128:(n + 1) * 128],
                             bcc[s][32:48, 0:512], start=True, stop=True)
                    return e.matmul(pbc[:, 512:1024], sel[32:48, n * 128:(n + 1) * 128],
                                    bcc[s][32:48, 512:1024], start=True, stop=True)
                P.step("tensor", mmc, reads=[("sel",), ("bcc", s)], writes=[("psC",)],
                       name=f"cb{oc}_{n}")
                P.step("scalar", lambda e, p=p: e.copy(out=Cbs[p][:], in_=pbc[:]),
                       reads=[("psC",)], writes=[("Cbs", p)], name=f"ce{oc}_{n}")
                q = n % 4
                P.step("scalar", lambda e, s=s, n=n, q=q: e.activation(
                          out=dac[q][:], in_=dc[s][:], func=AF.Exp, scale=A[:, n:n + 1]),
                       reads=[("dc", s), ("A",)], writes=[("dac", q)], name=f"da{oc}_{n}")

            emit_bcast(0)
            for n in range(16):
                p = n % 3
                q = n % 4
                if n + 1 < 16:
                    emit_bcast(n + 1)
                P.step("vector", lambda e, p=p, q=q: e.tensor_mul(
                          bbc[q][:], wcb[:], Bbs[p][:]),
                       reads=[("wcb",), ("Bbs", p)], writes=[("bbc", q)], name=f"bB{oc}_{n}")
                if oc % (NOC // 2) == 0:
                    init, ikey = zcol[:], ("zcol",)
                else:
                    init, ikey = carry[:, n:n + 1], ("carry", n)
                P.step("vector", lambda e, init=init, q=q: e.tensor_tensor_scan(
                          out=hcb[q][:], data0=dac[q][:], data1=bbc[q][:], initial=init,
                          op0=ALU.mult, op1=ALU.add),
                       reads=[("dac", q), ("bbc", q), ikey], writes=[("hcb", q)],
                       name=f"h{oc}_{n}")
                P.step("vector", lambda e, n=n, q=q: e.tensor_copy(
                          out=carry[:, n:n + 1], in_=hcb[q][:, TC - 1:TC]),
                       reads=[("hcb", q)], writes=[("carry", n)], name=f"cy{oc}_{n}")
                P.step("vector", lambda e, p=p, q=q: e.tensor_mul(
                          ymb[q][:], hcb[q][:], Cbs[p][:]),
                       reads=[("hcb", q), ("Cbs", p)], writes=[("ymb", q)],
                       name=f"ym{oc}_{n}")
                def mmy(e, q=q, n=n):
                    e.matmul(pyy[:, 0:512], idtb[:], ymb[q][:, 0:512],
                             start=(n == 0), stop=(n == 15))
                    return e.matmul(pyy[:, 512:1024], idtb[:], ymb[q][:, 512:1024],
                                    start=(n == 0), stop=(n == 15))
                P.step("tensor", mmy, reads=[("ymb", q), ("idtb",)],
                       writes=[("psY",)] if n in (0, 15) else [], name=f"ya{oc}_{n}")
            P.step("scalar", lambda e: e.copy(out=yac[:], in_=pyy[:]),
                   reads=[("psY",)], writes=[("yac",)], name=f"ye{oc}")
            P.step("vector", lambda e, s=s: e.tensor_scalar_mul(ymf[:], uc[s][:], dsl[:, 0:1]),
                   reads=[("uc", s), ("dsl",)], writes=[("ymf",)], name=f"ud{oc}")
            P.step("vector", lambda e: e.tensor_add(yac[:], yac[:], ymf[:]),
                   reads=[("ymf",), ("yac",)], writes=[("yac",)], name=f"yd{oc}")
            P.step("gpsimd", lambda e, cs=cs: e.dma_start(out=outs["y0"][:, cs], in_=yac[:]),
                   reads=[("yac",)], writes=[("st", oc % 4)], dma=True, name=f"st{oc}")
        P.step("sync", None, reads=[("st", i) for i in range(4)], name="fin")
    return bld


SEL16 = np.zeros((48, 16 * 128), np.float32)
for _n in range(16):
    SEL16[_n, _n * 128:(_n + 1) * 128] = 1.0
    SEL16[32 + _n, _n * 128:(_n + 1) * 128] = 1.0


def run_C(b_outs, inputs):
    TT_ = NB * SEQ
    in_specs = {
        "u0": ((128, TT_), F32), "d0": ((128, TT_), F32), "bc": ((48, TT_), F32),
        "alog": ((128, 16), F32), "dsl": ((128, 1), F32),
        "sel16": ((48, 16 * 128), F32), "idty": ((128, 128), BF16),
    }
    out_specs = {"y0": ((128, TT_), F32)}
    nc = mkprog(build_C(), in_specs, out_specs)
    ufull = np.concatenate([o["xn2"][:, :SL] for o in b_outs] +
                           [o["xn2"][:, SL:] for o in b_outs], axis=1)
    dfull = np.concatenate([o["delta0"][:, :SL] for o in b_outs] +
                           [o["delta0"][:, SL:] for o in b_outs], axis=1)
    bcfull = np.concatenate([o["bc0"][:, :SL] for o in b_outs] +
                            [o["bc0"][:, SL:] for o in b_outs], axis=1)
    alog = inputs["L0_attn_A_log"].astype(np.float32)
    dvec = inputs["L0_attn_D"].astype(np.float32)
    in_maps = []
    for c in range(NCORE):
        rs = slice(c * 128, (c + 1) * 128)
        in_maps.append({
            "u0": np.ascontiguousarray(ufull[rs]),
            "d0": np.ascontiguousarray(dfull[rs]),
            "bc": bcfull, "alog": alog[rs], "dsl": dvec[rs, None],
            "sel16": SEL16, "idty": IDTY_BF,
        })
    res = run_bass_kernel_spmd(nc, in_maps, core_ids=list(range(NCORE)))
    yfull = np.concatenate([res.results[c]["y0"] for c in range(NCORE)], axis=0)
    y_tok = []
    for c in range(NCORE):
        y_tok.append(np.ascontiguousarray(np.concatenate(
            [yfull[:, c * SL:(c + 1) * SL],
             yfull[:, SEQ + c * SL:SEQ + (c + 1) * SL]], axis=1)))
    return y_tok



def build_E():
    TC = 512
    NCHT = (NB * SEQ) // TC
    CPB = NCHT // NB

    def bld(nc, P, ins, outs, es):
        idtb = sbuf(nc, es, "s_idtb", [128, 128], BF16)
        idtf = sbuf(nc, es, "s_idtf", [128, 128], F32)
        P.load(idtb[:], ins["idty"], writes=[("idtb",)])
        P.load(idtf[:], ins["idtf"], writes=[("idtf",)])
        sgam = sbuf(nc, es, "s_sgam", [128, 8], F32)
        P.load(sgam[:], ins["sgam"], writes=[("sgam",)])
        b1s = sbuf(nc, es, "s_b1s", [128, 32], F32)
        P.load(b1s[:], ins["b1s"], writes=[("b1s",)])
        b2s = sbuf(nc, es, "s_b2s", [128, 8], F32)
        P.load(b2s[:], ins["b2s"], writes=[("b2s",)])
        pba = [psum(nc, es, f"pba{i}", [128, 512], F32) for i in range(6)]
        pbt = [psum(nc, es, f"pbt{i}", [128, 512], BF16) for i in range(2)]
        lg = nc.dram_tensor("lg", [512, NB * SEQ], F32, kind="Internal").ap()

        snT = [sbuf(nc, es, f"snT{i}", [128, 512], BF16) for i in range(8)]
        se = sbuf(nc, es, "se", [128, 1024], F32)
        seb = sbuf(nc, es, "seb", [128, 1024], BF16)
        srow = sbuf(nc, es, "srow", [128, 1], F32)
        ssq = sbuf(nc, es, "ssq", [128, 1024], F32)
        for st in range(4):
            P.load(se[:], ins["semb"].rearrange("(a p) d -> a p d", p=128)[st],
                   writes=[("se",)], name=f"ldse{st}")
            P.step("scalar", lambda e: e.square(out=ssq[:], in_=se[:]),
                   reads=[("se",)], writes=[("ssq",)], name=f"ssq{st}")
            P.step("vector", lambda e: e.reduce_sum(srow[:], ssq[:], axis=mybir.AxisListType.X),
                   reads=[("ssq",)], writes=[("srow",)], name=f"snrm{st}")
            P.step("scalar", lambda e: e.activation(out=srow[:], in_=srow[:], func=AF.Ln),
                   reads=[("srow",)], writes=[("srow",)], name=f"snl{st}")
            P.step("scalar", lambda e: e.activation(out=srow[:], in_=srow[:], func=AF.Exp, scale=-0.5),
                   reads=[("srow",)], writes=[("srow",)], name=f"sne{st}")
            P.step("vector", lambda e: e.tensor_scalar_mul(srow[:], srow[:], float(np.sqrt(DM))),
                   reads=[("srow",)], writes=[("srow",)], name=f"sn32{st}")
            P.step("vector", lambda e: e.tensor_scalar_mul(seb[:], se[:], srow[:, 0:1]),
                   reads=[("se",), ("srow",)], writes=[("seb",)], name=f"snn{st}")
            for dt in range(8):
                P.step("tensor", lambda e, dt=dt: e.transpose(
                          pbt[0][:, 0:128], seb[:, dt * 128:(dt + 1) * 128], idtb[:]),
                       reads=[("seb",), ("idtb",)], writes=[("pbt", 0)], name=f"snt{st}_{dt}")
                P.step("scalar", lambda e, dt=dt, st=st: e.activation(
                          out=snT[dt][:, st * 128:(st + 1) * 128], in_=pbt[0][:, 0:128],
                          func=AF.Copy, scale=sgam[:, dt:dt + 1]),
                       reads=[("pbt", 0), ("sgam",)], writes=[("snT", dt, st)],
                       name=f"sne{st}_{dt}")

        xqt = [[sbuf(nc, es, f"xqt{r}_{i}", [128, 512], BF16) for i in range(8)]
               for r in range(2)]
        lgb = [sbuf(nc, es, f"lgb{r}", [128, 512], F32) for r in range(4)]
        rmax = sbuf(nc, es, "rmax", [128, 8], F32)
        tcol = [sbuf(nc, es, f"tcol{r}", [128, 1], F32) for r in range(4)]

        def load_xqt(ch):
            r = ch % 2
            cs = slice(ch * TC, (ch + 1) * TC)
            for ko in range(8):
                P.load(xqt[r][ko][:], ins["xq_T"].rearrange("(a p) t -> a p t", p=128)[ko, :, cs],
                       writes=[("xqt", r, ko)], name=f"lqt{ch}_{ko}")
        load_xqt(0)
        for ch in range(NCHT):
            b = ch // CPB
            r = ch % 2
            cs = slice(ch * TC, (ch + 1) * TC)
            if ch + 1 < NCHT:
                load_xqt(ch + 1)
            for et in range(4):
                def mml(e, et=et, r=r):
                    for ko in range(8):
                        i = e.matmul(pba[et % 2][:], snT[ko][:, et * 128:(et + 1) * 128],
                                     xqt[r][ko][:], start=(ko == 0), stop=(ko == 7))
                    return i
                P.step("tensor", mml,
                       reads=[("snT", ko, st) for ko in range(8) for st in range(4)] +
                             [("xqt", r, ko) for ko in range(8)],
                       writes=[("pbank", et % 2)], name=f"lmm{ch}_{et}")
                P.step("scalar", lambda e, et=et: e.copy(out=lgb[et][:], in_=pba[et % 2][:]),
                       reads=[("pbank", et % 2)], writes=[("lgb", et)], name=f"lev{ch}_{et}")
                P.step("vector", lambda e, et=et: e.reduce_max(
                          tcol[et][:], lgb[et][:], axis=mybir.AxisListType.X),
                       reads=[("lgb", et)], writes=[("tcol", et)], name=f"lmx{ch}_{et}")
                col = 2 * et + b
                if ch % CPB == 0:
                    P.step("vector", lambda e, col=col, et=et: e.tensor_copy(
                              out=rmax[:, col:col + 1], in_=tcol[et][:]),
                           reads=[("tcol", et)], writes=[("rmax", col)], name=f"lrm{ch}_{et}")
                else:
                    P.step("vector", lambda e, col=col, et=et: e.tensor_max(
                              rmax[:, col:col + 1], rmax[:, col:col + 1], tcol[et][:]),
                           reads=[("tcol", et), ("rmax", col)], writes=[("rmax", col)],
                           name=f"lrm{ch}_{et}")
                P.step("gpsimd", lambda e, et=et, cs=cs: e.dma_start(
                          out=lg[et * 128:(et + 1) * 128, cs], in_=lgb[et][:]),
                       reads=[("lgb", et)], writes=[("st_lg", et)], dma=True,
                       name=f"lst{ch}_{et}")

        nrmax = sbuf(nc, es, "nrmax", [128, 8], F32)
        rsum = sbuf(nc, es, "rsum", [128, 8], F32)
        lgc = [sbuf(nc, es, f"lgc{i}", [128, 512], F32) for i in range(4)]
        ec = [sbuf(nc, es, f"ec{r}", [128, 512], BF16) for r in range(4)]
        dT = [sbuf(nc, es, f"dT{i}", [128, 512], BF16) for i in range(4)]
        xqn = [sbuf(nc, es, f"xqn{i}", [128, 1024], BF16) for i in range(4)]
        slots = [sbuf(nc, es, f"slots{i}", [128, 512], F32) for i in range(8)]
        sT = [sbuf(nc, es, f"sT{i}", [128, 512], BF16) for i in range(8)]
        hT = [sbuf(nc, es, f"hT{i}", [128, 512], BF16) for i in range(32)]
        eon = [sbuf(nc, es, f"eon{i}", [128, 1024], BF16) for i in range(4)]
        wf = [sbuf(nc, es, f"wf{i}", [128, 4096], F32) for i in range(2)]
        wb = [sbuf(nc, es, f"wb{i}", [128, 4096], BF16) for i in range(2)]
        rrow = sbuf(nc, es, "rrow", [1, 512], F32)
        onesr = sbuf(nc, es, "onesr", [1, 128], F32)
        P.load(onesr[:], ins["ones_row"], writes=[("ones_row",)])
        lgn = [sbuf(nc, es, f"lgn{i}", [128, 512], F32) for i in range(4)]
        cnT = [sbuf(nc, es, f"cnT{i}", [128, 512], BF16) for i in range(4)]
        cn = [sbuf(nc, es, f"cn{r}", [128, 512], BF16) for r in range(4)]
        ncb = sbuf(nc, es, "ncb", [128, 512], BF16)
        d2sb = sbuf(nc, es, "d2sb", [128, 128], F32)
        mx2sb = sbuf(nc, es, "mx2sb", [128, 128], F32)

        for b in range(NB):
            P.step("vector", lambda e: e.tensor_scalar_mul(nrmax[:], rmax[:], -1.0),
                   reads=[("rmax", c) for c in range(8)], writes=[("nrmax",)],
                   name=f"nrm{b}")
            for ci in range(CPB):
                ch = b * CPB + ci
                cs = slice(ch * TC, (ch + 1) * TC)
                for ts in range(4):
                    P.load(xqn[ts][:], ins["xq_nat"].rearrange("(c p) d -> c p d", p=128)[4 * ch + ts],
                           writes=[("xqn", ts)], name=f"dqn{ch}_{ts}")
                for et in range(4):
                    P.load(lgc[et][:], lg[et * 128:(et + 1) * 128, cs],
                           reads=[("st_lg", et)], writes=[("lgc", et)], name=f"dlg{ch}_{et}")
                    col = 2 * et + b
                    P.step("scalar", lambda e, et=et, col=col: e.activation(
                              out=ec[et][:], in_=lgc[et][:], func=AF.Exp,
                              bias=nrmax[:, col:col + 1]),
                           reads=[("lgc", et), ("nrmax",)], writes=[("ec", et)],
                           name=f"dex{ch}_{et}")
                    P.step("vector", lambda e, et=et: e.reduce_sum(
                              tcol[et][:], ec[et][:], axis=mybir.AxisListType.X),
                           reads=[("ec", et)], writes=[("tcol", et)], name=f"dsm{ch}_{et}")
                    if ci == 0:
                        P.step("vector", lambda e, col=col, et=et: e.tensor_copy(
                                  out=rsum[:, col:col + 1], in_=tcol[et][:]),
                               reads=[("tcol", et)], writes=[("rsum", col)], name=f"drs{ch}_{et}")
                    else:
                        P.step("vector", lambda e, col=col, et=et: e.tensor_add(
                                  rsum[:, col:col + 1], rsum[:, col:col + 1], tcol[et][:]),
                               reads=[("tcol", et), ("rsum", col)], writes=[("rsum", col)],
                               name=f"drs{ch}_{et}")
                for ts in range(4):
                    def trg(e, ts=ts):
                        for et in range(4):
                            i = e.transpose(pbt[ts % 2][:, et * 128:(et + 1) * 128],
                                            ec[et][:, ts * 128:(ts + 1) * 128], idtb[:])
                        return i
                    P.step("tensor", trg,
                           reads=[("ec", et) for et in range(4)] + [("idtb",)],
                           writes=[("pbt", ts % 2)], name=f"dtr{ch}_{ts}")
                    P.step("scalar", lambda e, ts=ts: e.copy(
                              out=dT[ts][:], in_=pbt[ts % 2][:]),
                           reads=[("pbt", ts % 2)],
                           writes=[("dT", ts, et) for et in range(4)],
                           name=f"dev{ch}_{ts}")
                for dt in range(8):
                    bank = 2 + (dt % 2)
                    def mms(e, dt=dt, bank=bank):
                        for ts in range(4):
                            i = e.matmul(pba[bank][:], xqn[ts][:, dt * 128:(dt + 1) * 128],
                                         dT[ts][:], start=(ts == 0), stop=(ts == 3))
                        return i
                    P.step("tensor", mms,
                           reads=[("xqn", ts) for ts in range(4)] +
                                 [("dT", ts, et) for ts in range(4) for et in range(4)],
                           writes=[("pbank", bank)], name=f"dss{ch}_{dt}")
                    if ci == 0:
                        P.step("scalar", lambda e, dt=dt, bank=bank: e.copy(
                                  out=slots[dt][:], in_=pba[bank][:]),
                               reads=[("pbank", bank)], writes=[("slots", dt)],
                               name=f"dsa{ch}_{dt}")
                    else:
                        P.step("vector", lambda e, dt=dt, bank=bank: e.tensor_add(
                                  slots[dt][:], slots[dt][:], pba[bank][:]),
                               reads=[("pbank", bank), ("slots", dt)], writes=[("slots", dt)],
                               name=f"dsa{ch}_{dt}")
            P.step("vector", lambda e: e.reciprocal(nrmax[:], rsum[:]),
                   reads=[("rsum", c) for c in range(8)], writes=[("nrmax",)],
                   name=f"rcp{b}")
            for et in range(4):
                col = 2 * et + b
                P.step("tensor", lambda e, et=et, col=col: e.transpose(
                          pba[4][0:1, 0:128], nrmax[:, col:col + 1], idtf[:]),
                       reads=[("nrmax",), ("idtf",)], writes=[("pbank", 4)],
                       name=f"rtr{b}_{et}")
                P.step("scalar", lambda e, et=et: e.copy(
                          out=rrow[:, et * 128:(et + 1) * 128], in_=pba[4][0:1, 0:128]),
                       reads=[("pbank", 4)], writes=[("rrow", et)], name=f"rev{b}_{et}")
            P.step("tensor", lambda e: e.matmul(pba[5][:], onesr[:], rrow[:],
                                                start=True, stop=True),
                   reads=[("rrow", et) for et in range(4)] + [("ones_row",)],
                   writes=[("pbank", 5)], name=f"rbc{b}")
            for dt in range(8):
                P.step("vector", lambda e, dt=dt: e.tensor_mul(slots[dt][:], slots[dt][:], pba[5][:]),
                       reads=[("pbank", 5), ("slots", dt)], writes=[("slots", dt)],
                       name=f"rnm{b}_{dt}")
                P.step("scalar", lambda e, dt=dt: e.copy(out=sT[dt][:], in_=slots[dt][:]),
                       reads=[("slots", dt)], writes=[("sT", dt)], name=f"rcast{b}_{dt}")
            for hm in range(32):
                s = hm % 2
                P.load(wf[s][:, 0:1024], ins["w1e"][hm], writes=[("wf", s)], name=f"f1l{b}_{hm}")
                P.step("scalar", lambda e, s=s: e.copy(out=wb[s][:, 0:1024], in_=wf[s][:, 0:1024]),
                       reads=[("wf", s)], writes=[("wb", s)], name=f"f1c{b}_{hm}")
                def mmh(e, s=s, hm=hm):
                    for dt in range(8):
                        i = e.matmul(pba[4][:], wb[s][:, dt * 128:(dt + 1) * 128],
                                     sT[dt][:], start=(dt == 0), stop=(dt == 7))
                    return i
                P.step("tensor", mmh,
                       reads=[("wb", s)] + [("sT", dt) for dt in range(8)],
                       writes=[("pbank", 4)], name=f"f1m{b}_{hm}")
                P.step("scalar", lambda e, hm=hm: e.activation(
                          out=hT[hm][:], in_=pba[4][:], func=AF.Gelu,
                          bias=b1s[:, hm:hm + 1]),
                       reads=[("pbank", 4), ("b1s",)], writes=[("hT", hm)],
                       name=f"f1e{b}_{hm}")
            for dm in range(8):
                s = dm % 2
                P.load(wf[s][:], ins["w2e"][dm], writes=[("wf", s)], name=f"f2l{b}_{dm}")
                P.step("scalar", lambda e, s=s: e.copy(out=wb[s][:], in_=wf[s][:]),
                       reads=[("wf", s)], writes=[("wb", s)], name=f"f2c{b}_{dm}")
                def mme(e, s=s):
                    for hm in range(32):
                        i = e.matmul(pba[5][:], wb[s][:, hm * 128:(hm + 1) * 128],
                                     hT[hm][:], start=(hm == 0), stop=(hm == 31))
                    return i
                P.step("tensor", mme,
                       reads=[("wb", s)] + [("hT", hm) for hm in range(32)],
                       writes=[("pbank", 5)], name=f"f2m{b}_{dm}")
                P.step("vector", lambda e, dm=dm: e.tensor_scalar_add(
                          ncb[:], pba[5][:], b2s[:, dm:dm + 1]),
                       reads=[("pbank", 5), ("b2s",)], writes=[("ncb",)], name=f"f2e{b}_{dm}")
                for ts in range(4):
                    P.step("tensor", lambda e, ts=ts: e.transpose(
                              pbt[ts % 2][:, 0:128], ncb[:, ts * 128:(ts + 1) * 128], idtb[:]),
                           reads=[("ncb",), ("idtb",)], writes=[("pbt", ts % 2)],
                           name=f"f2t{b}_{dm}_{ts}")
                    P.step("scalar", lambda e, ts=ts, dm=dm: e.copy(
                              out=eon[ts][:, dm * 128:(dm + 1) * 128], in_=pbt[ts % 2][:, 0:128]),
                           reads=[("pbt", ts % 2)], writes=[("eon", ts, dm)],
                           name=f"f2v{b}_{dm}_{ts}")
            for ci in range(CPB):
                ch = b * CPB + ci
                cs = slice(ch * TC, (ch + 1) * TC)
                for et in range(4):
                    P.load(lgc[et][:], lg[et * 128:(et + 1) * 128, cs],
                           reads=[("st_lg", et)], writes=[("lgc", et)], name=f"clg{ch}_{et}")
                for ts in range(4):
                    bank = 4 + ts % 2
                    def trl(e, ts=ts, bank=bank):
                        for et in range(4):
                            i = e.transpose(pba[bank][:, et * 128:(et + 1) * 128],
                                            lgc[et][:, ts * 128:(ts + 1) * 128], idtf[:])
                        return i
                    P.step("tensor", trl,
                           reads=[("lgc", et) for et in range(4)] + [("idtf",)],
                           writes=[("pbank", bank)], name=f"ctr{ch}_{ts}")
                    P.step("scalar", lambda e, ts=ts, bank=bank: e.copy(
                              out=lgn[ts][:], in_=pba[bank][:]),
                           reads=[("pbank", bank)],
                           writes=[("lgn", ts, et) for et in range(4)],
                           name=f"cev{ch}_{ts}")
                for ts in range(4):
                    tg = 4 * ch + ts
                    cr = ts
                    P.step("vector", lambda e, ts=ts: e.reduce_max(
                              tcol[ts][:], lgn[ts][:], axis=mybir.AxisListType.X),
                           reads=[("lgn", ts, et) for et in range(4)], writes=[("tcol", ts)],
                           name=f"cmx{ch}_{ts}")
                    P.step("vector", lambda e, tg=tg, ts=ts: e.tensor_copy(
                              out=mx2sb[:, tg % 128:tg % 128 + 1], in_=tcol[ts][:]),
                           reads=[("tcol", ts)], writes=[("mx2", tg)], name=f"cmc{ch}_{ts}")
                    P.step("vector", lambda e, ts=ts: e.tensor_scalar_mul(
                              tcol[ts][:], tcol[ts][:], -1.0),
                           reads=[("tcol", ts)], writes=[("tcol", ts)], name=f"cmn{ch}_{ts}")
                    P.step("scalar", lambda e, ts=ts, cr=cr: e.activation(
                              out=cn[cr][:], in_=lgn[ts][:], func=AF.Exp, bias=tcol[ts][:, 0:1]),
                           reads=[("lgn", ts, et) for et in range(4)] + [("tcol", ts)],
                           writes=[("cn", cr)], name=f"cex{ch}_{ts}")
                    P.step("vector", lambda e, ts=ts, cr=cr: e.reduce_sum(
                              tcol[ts][:], cn[cr][:], axis=mybir.AxisListType.X),
                           reads=[("cn", cr)], writes=[("tcol", ts)], name=f"csm{ch}_{ts}")
                    P.step("vector", lambda e, tg=tg, ts=ts: e.tensor_copy(
                              out=d2sb[:, tg % 128:tg % 128 + 1], in_=tcol[ts][:]),
                           reads=[("tcol", ts)], writes=[("d2", tg)], name=f"cdc{ch}_{ts}")
                for et in range(4):
                    def trc(e, et=et):
                        for ts in range(4):
                            i = e.transpose(pbt[et % 2][:, ts * 128:(ts + 1) * 128],
                                            cn[ts][:, et * 128:(et + 1) * 128], idtb[:])
                        return i
                    P.step("tensor", trc,
                           reads=[("cn", ts) for ts in range(4)] + [("idtb",)],
                           writes=[("pbt", et % 2)], name=f"cct{ch}_{et}")
                    P.step("scalar", lambda e, et=et: e.copy(out=cnT[et][:], in_=pbt[et % 2][:]),
                           reads=[("pbt", et % 2)],
                           writes=[("cnT", et, ts) for ts in range(4)],
                           name=f"ccv{ch}_{et}")
                for dm in range(8):
                    bank = 2 + (dm % 2)
                    def mmn(e, dm=dm, bank=bank):
                        for et in range(4):
                            i = e.matmul(pba[bank][:], eon[et][:, dm * 128:(dm + 1) * 128],
                                         cnT[et][:], start=(et == 0), stop=(et == 3))
                        return i
                    P.step("tensor", mmn,
                           reads=[("cnT", et, ts) for et in range(4) for ts in range(4)] +
                                 [("eon", et, dmx) for et in range(4) for dmx in range(8)],
                           writes=[("pbank", bank)], name=f"cnm{ch}_{dm}")
                    P.step("scalar", lambda e, bank=bank: e.copy(out=ncb[:], in_=pba[bank][:]),
                           reads=[("pbank", bank)], writes=[("ncb",)], name=f"cne{ch}_{dm}")
                    P.step("gpsimd", lambda e, dm=dm, cs=cs: e.dma_start(
                              out=outs["nT"][dm * 128:(dm + 1) * 128, cs], in_=ncb[:]),
                           reads=[("ncb",)], writes=[("st_n", dm)], dma=True,
                           name=f"cns{ch}_{dm}")
            P.step("gpsimd", lambda e, b=b: e.dma_start(
                      out=outs["d2"][:, b * 64:(b + 1) * 64], in_=d2sb[:, b * 64:(b + 1) * 64]),
                   reads=[("d2", tg) for tg in range(b * 64, (b + 1) * 64)],
                   writes=[("st_d2", b)], dma=True, name=f"std2{b}")
            P.step("gpsimd", lambda e, b=b: e.dma_start(
                      out=outs["mx2"][:, b * 64:(b + 1) * 64], in_=mx2sb[:, b * 64:(b + 1) * 64]),
                   reads=[("mx2", tg) for tg in range(b * 64, (b + 1) * 64)],
                   writes=[("st_m2", b)], dma=True, name=f"stm2{b}")
        P.step("sync", None,
               reads=[("st_n", dm) for dm in range(8)] +
                     [("st_d2", b) for b in range(NB)] + [("st_m2", b) for b in range(NB)],
               name="fin")
    return bld


def run_E(inputs, xq_T_full, layer_prefix="L1_ffn"):
    TT_ = NB * SEQ
    in_specs = {
        "xq_T": ((DM, TT_), BF16), "xq_nat": ((TT_, DM), BF16),
        "semb": ((512, DM), F32), "sgam": ((128, 8), F32),
        "w1e": ((32, 128, 1024), F32), "w2e": ((8, 128, 4096), F32),
        "b1s": ((128, 32), F32), "b2s": ((128, 8), F32),
        "idty": ((128, 128), BF16), "idtf": ((128, 128), F32),
        "ones_row": ((1, 128), F32),
    }
    out_specs = {"nT": ((DM, TT_), BF16), "d2": ((128, 128), F32),
                 "mx2": ((128, 128), F32)}
    nc = mkprog(build_E(), in_specs, out_specs)
    xq_nat = np.ascontiguousarray(xq_T_full.T)
    w1 = inputs[layer_prefix + "_w1"].astype(np.float32)
    w2 = inputs[layer_prefix + "_w2"].astype(np.float32)
    b1 = inputs[layer_prefix + "_b1"].astype(np.float32)
    b2 = inputs[layer_prefix + "_b2"].astype(np.float32)
    semb = inputs[layer_prefix + "_slot_embeds"].astype(np.float32)
    sgam = _colscale(inputs[layer_prefix + "_slot_norm_gamma"])
    in_maps = []
    for c in range(NCORE):
        in_maps.append({
            "xq_T": xq_T_full, "xq_nat": xq_nat,
            "semb": np.ascontiguousarray(semb[c]), "sgam": sgam,
            "w1e": np.ascontiguousarray(
                w1[c].reshape(8, 128, 32, 128).transpose(2, 1, 0, 3).reshape(32, 128, 1024)),
            "w2e": np.ascontiguousarray(
                w2[c].reshape(32, 128, 8, 128).transpose(2, 1, 0, 3).reshape(8, 128, 4096)),
            "b1s": _colscale(b1[c]), "b2s": _colscale(b2[c]),
            "idty": IDTY_BF, "idtf": IDTY, "ones_row": ONES_ROW,
        })
    res = run_bass_kernel_spmd(nc, in_maps, core_ids=list(range(NCORE)))
    return [res.results[c] for c in range(NCORE)]


def assemble_xq_full(d_outs, key="xq_T"):
    return np.ascontiguousarray(np.concatenate(
        [o[key][:, :SL] for o in d_outs] + [o[key][:, SL:] for o in d_outs], axis=1))



TP = 2176
NTS = TP // 128


def _own_segments(p0, p1):
    segs = []
    for lo, hi, off in ((3, 1027, -3), (1030, 2054, -6)):
        a, b = max(p0, lo), min(p1, hi)
        if a < b:
            segs.append((a, b - a, a + off))
    return segs


def build_FG():
    def bld(nc, P, ins, outs, es):
        idtb = sbuf(nc, es, "s_idtb", [128, 128], BF16)
        idtf = sbuf(nc, es, "s_idtf", [128, 128], F32)
        P.load(idtb[:], ins["idty"], writes=[("idtb",)])
        P.load(idtf[:], ins["idtf"], writes=[("idtf",)])
        cvw = sbuf(nc, es, "s_cvw", [128, 16 * 4], F32)
        P.load(cvw[:].rearrange("p (a k) -> p a k", a=16),
               ins["cvw"].rearrange("(a p) o k -> p a (o k)", p=128), writes=[("cvw",)])
        cvb = sbuf(nc, es, "s_cvb", [128, 16], F32)
        P.load(cvb[:], ins["cvb"], writes=[("cvb",)])
        bdt1 = sbuf(nc, es, "s_bdt1", [128, 16], F32)
        P.load(bdt1[:], ins["bdt1"], writes=[("bdt1",)])
        pba = [psum(nc, es, f"pba{i}", [128, 512], F32) for i in range(6)]
        pbt = [psum(nc, es, f"pbt{i}", [128, 512], BF16) for i in range(2)]

        m2s = sbuf(nc, es, "s_m2s", [128, NTS * 8], F32)
        P.load(m2s[:], ins["m2s"], writes=[("m2s",)])
        d2s = sbuf(nc, es, "s_d2s", [128, NTS * 8], F32)
        P.load(d2s[:], ins["d2s"], writes=[("d2s",)])
        ceps = sbuf(nc, es, "s_ceps", [128, 1], F32)
        P.step("gpsimd", lambda e: e.memset(ceps[:], EPS_LN), writes=[("ceps",)], name="ceps")
        ceps2 = sbuf(nc, es, "s_ceps2", [128, 1], F32)
        P.step("gpsimd", lambda e: e.memset(ceps2[:], float(np.finfo(np.float32).eps)),
               writes=[("ceps2",)], name="ceps2")
        nw = sbuf(nc, es, "s_nw", [128, 8], F32)
        P.load(nw[:], ins["nw"], writes=[("nw",)])
        sd8 = [sbuf(nc, es, f"sd8_{i}", [128, 8], F32) for i in range(2)]
        m2c = [sbuf(nc, es, f"m2c{i}", [128, 1], F32) for i in range(2)]
        x3n = [sbuf(nc, es, f"s_x3n{i}", [128, 1024], F32) for i in range(2)]
        ntile = [sbuf(nc, es, f"ntile{i}", [128, 1024], BF16) for i in range(3)]
        acc = [sbuf(nc, es, f"acc{i}", [128, 1024], F32) for i in range(2)]
        srow8 = [sbuf(nc, es, f"srow8_{i}", [128, 8], F32) for i in range(2)]
        mcol = [sbuf(nc, es, f"mcol{i}", [128, 1], F32) for i in range(2)]
        dcol = [sbuf(nc, es, f"dcol{i}", [128, 1], F32) for i in range(2)]
        sqn = [sbuf(nc, es, f"sqn{i}", [128, 1024], F32) for i in range(1)]
        xn5n = [sbuf(nc, es, f"xn5n{i}", [128, 1024], BF16) for i in range(2)]
        xn5T = [sbuf(nc, es, f"xn5T{i}", [128, TP], BF16) for i in range(8)]
        for ts in range(NTS):
            p = ts % 2
            P.load(x3n[p][:], ins["x3n"].rearrange("(c p) d -> c p d", p=128)[ts],
                   writes=[("x3n", p)], name=f"ldx3{ts}")
            P.step("vector", lambda e, ts=ts, p=p: e.reduce_max(
                      mcol[p][:], m2s[:, ts * 8:(ts + 1) * 8], axis=mybir.AxisListType.X),
                   reads=[("m2s",)], writes=[("mcol", p)], name=f"fM{ts}")
            P.step("vector", lambda e, p=p: e.tensor_scalar_mul(mcol[p][:], mcol[p][:], -1.0),
                   reads=[("mcol", p)], writes=[("mcol", p)], name=f"fMn{ts}")
            P.step("scalar", lambda e, ts=ts, p=p: e.activation(
                      out=srow8[p][:], in_=m2s[:, ts * 8:(ts + 1) * 8], func=AF.Exp,
                      bias=mcol[p][:, 0:1]),
                   reads=[("m2s",), ("mcol", p)], writes=[("srow8", p)], name=f"fS{ts}")
            P.step("vector", lambda e, ts=ts, p=p: e.tensor_mul(
                      sd8[p][:], srow8[p][:], d2s[:, ts * 8:(ts + 1) * 8]),
                   reads=[("srow8", p), ("d2s",)], writes=[("sd8", p)], name=f"fSD{ts}")
            P.step("vector", lambda e, p=p: e.reduce_sum(
                      dcol[p][:], sd8[p][:], axis=mybir.AxisListType.X),
                   reads=[("sd8", p)], writes=[("dcol", p)], name=f"fden{ts}")
            P.step("vector", lambda e, p=p: e.reciprocal(dcol[p][:], dcol[p][:]),
                   reads=[("dcol", p)], writes=[("dcol", p)], name=f"frc{ts}")
            for c in range(NCORE):
                nr = (ts * 8 + c) % 3
                P.load(ntile[nr][:], ins["nTs"].rearrange("e (c p) d -> e c p d", p=128)[c, ts],
                       writes=[("ntile", nr)], name=f"ldn{ts}_{c}")
                if c == 0:
                    P.step("vector", lambda e, c=c, p=p, nr=nr: e.tensor_scalar_mul(
                              acc[p][:], ntile[nr][:], srow8[p][:, c:c + 1]),
                           reads=[("ntile", nr), ("srow8", p)], writes=[("acc", p)],
                           name=f"fa{ts}_{c}")
                else:
                    P.step("vector", lambda e, c=c, p=p, nr=nr: e.scalar_tensor_tensor(
                              out=acc[p][:], in0=ntile[nr][:], scalar=srow8[p][:, c:c + 1],
                              in1=acc[p][:], op0=ALU.mult, op1=ALU.add),
                           reads=[("ntile", nr), ("srow8", p), ("acc", p)], writes=[("acc", p)],
                           name=f"fa{ts}_{c}")
            P.step("vector", lambda e, p=p: e.tensor_scalar(
                      out=acc[p][:], in0=acc[p][:], scalar1=dcol[p][:, 0:1], scalar2=0.5,
                      op0=ALU.mult, op1=ALU.mult),
                   reads=[("acc", p), ("dcol", p)], writes=[("acc", p)], name=f"fmx{ts}")
            P.step("vector", lambda e, p=p: e.tensor_add(x3n[p][:], x3n[p][:], acc[p][:]),
                   reads=[("acc", p), ("x3n", p)], writes=[("x3n", p)], name=f"fx4{ts}")
            for si, (ps, ln_, os_) in enumerate(_own_segments(ts * 128, ts * 128 + 128)):
                P.step("gpsimd", lambda e, ps=ps, ln_=ln_, os_=os_, ts=ts, p=p: e.dma_start(
                          out=outs["x4"][os_:os_ + ln_, :],
                          in_=x3n[p][ps - ts * 128:ps - ts * 128 + ln_, :]),
                       reads=[("x3n", p)], writes=[("st_x4", ts, si)], dma=True,
                       name=f"sx4{ts}_{si}")
            P.step("vector", lambda e, p=p: e.reduce_sum(
                      mcol[p][:], x3n[p][:], axis=mybir.AxisListType.X),
                   reads=[("x3n", p)], writes=[("mcol", p)], name=f"l5s{ts}")
            P.step("vector", lambda e, p=p: e.tensor_scalar_mul(mcol[p][:], mcol[p][:], 1.0 / 1024),
                   reads=[("mcol", p)], writes=[("mcol", p)], name=f"l5m{ts}")
            P.step("scalar", lambda e, p=p: e.square(out=sqn[0][:], in_=x3n[p][:]),
                   reads=[("x3n", p)], writes=[("sqn", 0)], name=f"l5q{ts}")
            P.step("vector", lambda e, p=p: e.reduce_sum(
                      dcol[p][:], sqn[0][:], axis=mybir.AxisListType.X),
                   reads=[("sqn", 0)], writes=[("dcol", p)], name=f"l5e{ts}")
            P.step("vector", lambda e, p=p: e.tensor_scalar_mul(dcol[p][:], dcol[p][:], 1.0 / 1024),
                   reads=[("dcol", p)], writes=[("dcol", p)], name=f"l5e2{ts}")
            P.step("vector", lambda e, p=p: e.scalar_tensor_tensor(
                      out=m2c[p][:], in0=mcol[p][:], scalar=-1.0, in1=mcol[p][:],
                      op0=ALU.mult, op1=ALU.mult),
                   reads=[("mcol", p)], writes=[("m2c", p)], name=f"l5m2{ts}")
            P.step("vector", lambda e, p=p: e.tensor_add(dcol[p][:], dcol[p][:], m2c[p][:]),
                   reads=[("m2c", p), ("dcol", p)], writes=[("dcol", p)], name=f"l5v{ts}")
            P.step("scalar", lambda e, p=p: e.activation(out=dcol[p][:], in_=dcol[p][:],
                                                         func=AF.Ln, bias=ceps[:, 0:1]),
                   reads=[("dcol", p), ("ceps",)], writes=[("dcol", p)], name=f"l5l{ts}")
            P.step("scalar", lambda e, p=p: e.activation(out=dcol[p][:], in_=dcol[p][:],
                                                         func=AF.Exp, scale=-0.5),
                   reads=[("dcol", p)], writes=[("dcol", p)], name=f"l5r{ts}")
            P.step("vector", lambda e, p=p: e.tensor_scalar_mul(mcol[p][:], mcol[p][:], -1.0),
                   reads=[("mcol", p)], writes=[("mcol", p)], name=f"l5nm{ts}")
            P.step("vector", lambda e, p=p: e.tensor_scalar(
                      out=sqn[0][:], in0=x3n[p][:], scalar1=mcol[p][:, 0:1],
                      scalar2=dcol[p][:, 0:1], op0=ALU.add, op1=ALU.mult),
                   reads=[("x3n", p), ("mcol", p), ("dcol", p)], writes=[("sqn", 0)],
                   name=f"l5a{ts}")
            P.step("vector", lambda e, p=p: e.tensor_copy(out=xn5n[p][:], in_=sqn[0][:]),
                   reads=[("sqn", 0)], writes=[("xn5n", p)], name=f"l5c{ts}")
            for si, (ps, ln_, os_) in enumerate(_own_segments(ts * 128, ts * 128 + 128)):
                P.step("gpsimd", lambda e, ps=ps, ln_=ln_, os_=os_, ts=ts, p=p: e.dma_start(
                          out=outs["ln5"][os_:os_ + ln_, :],
                          in_=sqn[0][ps - ts * 128:ps - ts * 128 + ln_, :]),
                       reads=[("sqn", 0)], writes=[("st_l5", ts, si)], dma=True,
                       name=f"sl5{ts}_{si}")
            P.step("scalar", lambda e, p=p: e.square(out=sqn[0][:], in_=xn5n[p][:]),
                   reads=[("xn5n", p)], writes=[("sqn", 0)], name=f"r5q{ts}")
            P.step("vector", lambda e, p=p: e.reduce_sum(
                      dcol[p][:], sqn[0][:], axis=mybir.AxisListType.X),
                   reads=[("sqn", 0)], writes=[("dcol", p)], name=f"r5s{ts}")
            P.step("vector", lambda e, p=p: e.tensor_scalar_mul(dcol[p][:], dcol[p][:], 1.0 / 1024),
                   reads=[("dcol", p)], writes=[("dcol", p)], name=f"r5m{ts}")
            P.step("scalar", lambda e, p=p: e.activation(out=dcol[p][:], in_=dcol[p][:],
                                                         func=AF.Ln, bias=ceps2[:, 0:1]),
                   reads=[("dcol", p), ("ceps2",)], writes=[("dcol", p)], name=f"r5l{ts}")
            P.step("scalar", lambda e, p=p: e.activation(out=dcol[p][:], in_=dcol[p][:],
                                                         func=AF.Exp, scale=-0.5),
                   reads=[("dcol", p)], writes=[("dcol", p)], name=f"r5r{ts}")
            P.step("vector", lambda e, p=p: e.tensor_scalar_mul(xn5n[p][:], xn5n[p][:],
                                                                dcol[p][:, 0:1]),
                   reads=[("dcol", p), ("xn5n", p)], writes=[("xn5n", p)], name=f"r5a{ts}")
            for dt in range(8):
                P.step("tensor", lambda e, dt=dt, p=p: e.transpose(
                          pbt[dt % 2][:, 0:128], xn5n[p][:, dt * 128:(dt + 1) * 128], idtb[:]),
                       reads=[("xn5n", p), ("idtb",)], writes=[("pbt", dt % 2)],
                       name=f"l5t{ts}_{dt}")
                P.step("scalar", lambda e, dt=dt, ts=ts: e.activation(
                          out=xn5T[dt][:, ts * 128:(ts + 1) * 128], in_=pbt[dt % 2][:, 0:128],
                          func=AF.Copy, scale=nw[:, dt:dt + 1]),
                       reads=[("pbt", dt % 2), ("nw",)], writes=[("xn5T", dt, ts)],
                       name=f"l5tv{ts}_{dt}")

        xmr = [sbuf(nc, es, f"xmr{i}", [128, TP], BF16) for i in range(2)]
        wsl = [sbuf(nc, es, f"wsl{i}", [128, 1024], BF16) for i in range(2)]
        zb = sbuf(nc, es, "zb", [128, 512], BF16)
        NCH5 = [(0, 512), (512, 512), (1024, 512), (1536, 512), (2048, 128)]
        ubf = [sbuf(nc, es, f"ubf{i}", [128, TL], BF16) for i in range(16)]
        uf = sbuf(nc, es, "uf", [128, TL], F32)
        cacc = sbuf(nc, es, "cacc", [128, TL], F32)
        for mo in range(32):
            s = mo % 2
            P.load(wsl[s][:], ins["wint"][mo], writes=[("wsl", s)], name=f"wi{mo}")
            for ci, (c0, cw) in enumerate(NCH5):
                bank = ci % 2
                def mmw(e, s=s, bank=bank, c0=c0, cw=cw):
                    for ko in range(8):
                        i = e.matmul(pba[bank][:, 0:cw], wsl[s][:, ko * 128:(ko + 1) * 128],
                                     xn5T[ko][:, c0:c0 + cw], start=(ko == 0), stop=(ko == 7))
                    return i
                P.step("tensor", mmw,
                       reads=[("wsl", s)] + [("xn5T", ko, ts) for ko in range(8)
                                             for ts in range(c0 // 128, (c0 + cw) // 128)],
                       writes=[("pbank", bank)], name=f"wm{mo}_{ci}")
                if mo < 16:
                    P.step("scalar", lambda e, mo=mo, bank=bank, c0=c0, cw=cw: e.copy(
                              out=xmr[mo % 2][:, c0:c0 + cw], in_=pba[bank][:, 0:cw]),
                           reads=[("pbank", bank)], writes=[("xmr", mo % 2, c0)],
                           name=f"we{mo}_{ci}")
                else:
                    P.step("scalar", lambda e, bank=bank, cw=cw: e.copy(
                              out=zb[:, 0:cw], in_=pba[bank][:, 0:cw]),
                           reads=[("pbank", bank)], writes=[("zb",)], name=f"ze{mo}_{ci}")
                    for si, (ps, ln_, os_) in enumerate(_own_segments(c0, c0 + cw)):
                        P.step("gpsimd", lambda e, mo=mo, c0=c0, ps=ps, ln_=ln_, os_=os_: e.dma_start(
                                  out=outs["z1"][(mo - 16) * 128:(mo - 15) * 128, os_:os_ + ln_],
                                  in_=zb[:, ps - c0:ps - c0 + ln_]),
                               reads=[("zb",)], writes=[("st_z", mo, c0, si)], dma=True,
                               name=f"zs{mo}_{ci}_{si}")

            if mo < 16:
                kt = mo
                sl = mo % 2
                rk = [("xmr", sl, c0) for c0, _ in NCH5]
                for sb, (pb_, ob_) in enumerate(((0, 0), (1027, 1024))):
                    P.step("vector", lambda e, sl=sl, kt=kt, pb_=pb_, ob_=ob_: e.tensor_scalar_mul(
                              cacc[:, ob_:ob_ + SL], xmr[sl][:, pb_:pb_ + SL],
                              cvw[:, kt * 4:kt * 4 + 1]),
                           reads=rk + [("cvw",)], writes=[("cacc",)], name=f"cv0_{kt}_{sb}")
                    for k in range(1, 4):
                        P.step("vector", lambda e, sl=sl, kt=kt, k=k, pb_=pb_, ob_=ob_: e.scalar_tensor_tensor(
                                  out=cacc[:, ob_:ob_ + SL], in0=xmr[sl][:, pb_ + k:pb_ + k + SL],
                                  scalar=cvw[:, kt * 4 + k:kt * 4 + k + 1], in1=cacc[:, ob_:ob_ + SL],
                                  op0=ALU.mult, op1=ALU.add),
                               reads=[("cacc",), ("cvw",)] + rk,
                               writes=[("cacc",)], name=f"cv{k}_{kt}_{sb}")
                P.step("scalar", lambda e, kt=kt: e.activation(
                          out=uf[:], in_=cacc[:], func=AF.Silu, bias=cvb[:, kt:kt + 1]),
                       reads=[("cacc",), ("cvb",)], writes=[("uf",)], name=f"cs{kt}")
                P.step("scalar", lambda e, kt=kt: e.copy(out=ubf[kt][:], in_=uf[:]),
                       reads=[("uf",)], writes=[("ubf", kt)], name=f"cb{kt}")
                P.step("gpsimd", lambda e, kt=kt: e.dma_start(
                          out=outs["u1"][kt * 128:(kt + 1) * 128, :], in_=uf[:]),
                       reads=[("uf",)], writes=[("st_u", kt)], dma=True, name=f"cu{kt}")

        wxr = sbuf(nc, es, "wxr", [128, 2048], F32)
        wxr2 = sbuf(nc, es, "wxr2", [64, 2048], F32)
        wxb = sbuf(nc, es, "wxb", [128, 2048], BF16)
        wxb2 = sbuf(nc, es, "wxb2", [64, 2048], BF16)
        P.load(wxr[:], ins["wx"][0:128, :], writes=[("wxr",)], name="lwx")
        P.load(wxr2[:], ins["wx"][128:192, :], writes=[("wxr2",)], name="lwx2")
        P.step("scalar", lambda e: e.copy(out=wxb[:], in_=wxr[:]),
               reads=[("wxr",)], writes=[("wxb",)], name="cwx")
        P.step("scalar", lambda e: e.copy(out=wxb2[:], in_=wxr2[:]),
               reads=[("wxr2",)], writes=[("wxb2",)], name="cwx2")
        wxT = [sbuf(nc, es, f"wxT{i}", [128, 192], BF16) for i in range(16)]
        for kt in range(16):
            P.step("tensor", lambda e, kt=kt: e.transpose(
                      pbt[0][:, 0:128], wxb[:, kt * 128:(kt + 1) * 128], idtb[:]),
                   reads=[("wxb",), ("idtb",)], writes=[("pbt", 0)], name=f"tx{kt}")
            P.step("scalar", lambda e, kt=kt: e.copy(out=wxT[kt][:, 0:128], in_=pbt[0][:, 0:128]),
                   reads=[("pbt", 0)], writes=[("wxT", kt)], name=f"tv{kt}")
            P.step("tensor", lambda e, kt=kt: e.transpose(
                      pbt[1][:, 0:64], wxb2[:, kt * 128:(kt + 1) * 128], idtb[0:64, 0:64]),
                   reads=[("wxb2",), ("idtb",)], writes=[("pbt", 1)], name=f"tx2{kt}")
            P.step("scalar", lambda e, kt=kt: e.copy(out=wxT[kt][:, 128:192],
                                                     in_=pbt[1][:, 0:64]),
                   reads=[("pbt", 1)], writes=[("wxT", kt)], name=f"tv2{kt}")
        wdtb = sbuf(nc, es, "s_wdtb", [128, 16 * 64], BF16)
        P.load(wdtb[:].rearrange("p (a r) -> p a r", a=16),
               ins["wdt1"].rearrange("(a p) r -> p a r", p=128),
               writes=[("wdtb",)], q="gpsimd", name="ldwdt1")
        wdtT = sbuf(nc, es, "s_wdtT", [64, 2048], BF16)
        for kt in range(16):
            P.step("tensor", lambda e, kt=kt: e.transpose(
                      pbt[0][0:64, 0:128], wdtb[:, kt * 64:(kt + 1) * 64], idtb[:]),
                   reads=[("wdtb",), ("idtb",)], writes=[("pbt", 0)], name=f"td{kt}")
            P.step("scalar", lambda e, kt=kt: e.copy(
                      out=wdtT[0:64, kt * 128:(kt + 1) * 128], in_=pbt[0][0:64, 0:128]),
                   reads=[("pbt", 0)], writes=[("wdtT", kt)], name=f"tdv{kt}")

        xdt = sbuf(nc, es, "xdt", [64, TL], BF16)
        xdf = sbuf(nc, es, "xdf", [128, 512], F32)
        for mt, (mb, mw) in enumerate([(0, 128), (128, 64)]):
            for ci in range(4):
                c0 = ci * 512
                bank = 2 + ci % 2
                def mmx(e, mb=mb, mw=mw, c0=c0, bank=bank):
                    for kt in range(16):
                        i = e.matmul(pba[bank][0:mw, :], wxT[kt][:, mb:mb + mw],
                                     ubf[kt][:, c0:c0 + 512], start=(kt == 0), stop=(kt == 15))
                    return i
                P.step("tensor", mmx,
                       reads=[("wxT", kt) for kt in range(16)] +
                             [("ubf", kt) for kt in range(16)],
                       writes=[("pbank", bank)], name=f"xd{mt}_{ci}")
                P.step("scalar", lambda e, mw=mw, bank=bank: e.copy(
                          out=xdf[0:mw, :], in_=pba[bank][0:mw, :]),
                       reads=[("pbank", bank)], writes=[("xdf",)], name=f"xde{mt}_{ci}")
                if mt == 0:
                    P.step("scalar", lambda e, c0=c0: e.copy(
                              out=xdt[:, c0:c0 + 512], in_=xdf[0:64, :]),
                           reads=[("xdf",)], writes=[("xdt", c0)], name=f"xdc{ci}")
                    P.step("gpsimd", lambda e, c0=c0: e.dma_start(
                              out=outs["bc1"][0:64, c0:c0 + 512], in_=xdf[64:128, :]),
                           reads=[("xdf",)], writes=[("st_bcb", ci)], dma=True,
                           name=f"xsb{ci}")
                else:
                    P.step("gpsimd", lambda e, c0=c0: e.dma_start(
                              out=outs["bc1"][64:128, c0:c0 + 512], in_=xdf[0:64, :]),
                           reads=[("xdf",)], writes=[("st_bcc", ci)], dma=True,
                           name=f"xsc{ci}")

        r1 = sbuf(nc, es, "r1f", [128, 512], F32)
        r2 = sbuf(nc, es, "r2f", [128, 512], F32)
        for mo in range(16):
            for ci in range(4):
                c0 = ci * 512
                P.step("tensor", lambda e, mo=mo, c0=c0: e.matmul(
                          pba[4][:], wdtT[:, mo * 128:(mo + 1) * 128], xdt[:, c0:c0 + 512],
                          start=True, stop=True),
                       reads=[("wdtT", kt) for kt in range(16)] +
                             [("xdt", k * 512) for k in range(4)],
                       writes=[("pbank", 4)], name=f"dm{mo}_{ci}")
                P.step("scalar", lambda e, mo=mo: e.activation(
                          out=r1[:], in_=pba[4][:], func=AF.Relu, bias=bdt1[:, mo:mo + 1]),
                       reads=[("pbank", 4), ("bdt1",)], writes=[("r1f",)], name=f"dd1{mo}_{ci}")
                P.step("scalar", lambda e, mo=mo: e.activation(
                          out=r2[:], in_=pba[4][:], func=AF.Abs, bias=bdt1[:, mo:mo + 1]),
                       reads=[("pbank", 4), ("bdt1",)], writes=[("r2f",)], name=f"dd2{mo}_{ci}")
                P.step("scalar", lambda e: e.activation(out=r2[:], in_=r2[:], func=AF.Exp, scale=-1.0),
                       reads=[("r2f",)], writes=[("r2f",)], name=f"dd3{mo}_{ci}")
                P.step("scalar", lambda e: e.activation(out=r2[:], in_=r2[:], func=AF.Ln, bias=1.0),
                       reads=[("r2f",)], writes=[("r2f",)], name=f"dd4{mo}_{ci}")
                P.step("vector", lambda e: e.tensor_add(r2[:], r2[:], r1[:]),
                       reads=[("r2f",), ("r1f",)], writes=[("r2f",)], name=f"dd5{mo}_{ci}")
                P.step("gpsimd", lambda e, mo=mo, c0=c0: e.dma_start(
                          out=outs["delta1"][mo * 128:(mo + 1) * 128, c0:c0 + 512],
                          in_=r2[:]),
                       reads=[("r2f",)], writes=[("st_dl", mo, c0)], dma=True,
                       name=f"ds{mo}_{ci}")

        fin = ([("st_x4", ts, si) for ts in range(NTS)
                for si in range(len(_own_segments(ts * 128, ts * 128 + 128)))] +
               [("st_l5", ts, si) for ts in range(NTS)
                for si in range(len(_own_segments(ts * 128, ts * 128 + 128)))] +
               [("st_z", mo, c0, si) for mo in range(16, 32) for c0, cw in NCH5
                for si in range(len(_own_segments(c0, c0 + cw)))] +
               [("st_u", kt) for kt in range(16)] +
               [("st_bcb", ci) for ci in range(4)] + [("st_bcc", ci) for ci in range(4)] +
               [("st_dl", mo, k * 512) for mo in range(16) for k in range(4)])
        P.step("sync", None, reads=fin, name="fin")
    return bld


def _padded_rows(arr_nat, c, fill_zero=False, const=None):
    D = arr_nat.shape[1]
    b0 = arr_nat[c * SL:(c + 1) * SL]
    b1 = arr_nat[SEQ + c * SL:SEQ + (c + 1) * SL]
    if c == 0:
        h0 = np.zeros((3, D), arr_nat.dtype) if const is None else np.full((3, D), const, arr_nat.dtype)
        h1 = h0
    else:
        h0 = arr_nat[c * SL - 3:c * SL]
        h1 = arr_nat[SEQ + c * SL - 3:SEQ + c * SL]
    pad = np.repeat(b1[-1:], 122, axis=0)
    return np.ascontiguousarray(np.concatenate([h0, b0, h1, b1, pad], 0))


def run_FG(inputs, prep, d_outs, e_outs):
    in_specs = {
        "x3n": ((TP, DM), F32), "nTs": ((NCORE, TP, DM), BF16),
        "m2s": ((128, NTS * 8), F32), "d2s": ((128, NTS * 8), F32),
        "wint": ((32, 128, 1024), BF16),
        "cvw": ((2048, 1, 4), F32), "cvb": ((128, 16), F32),
        "wx": ((192, 2048), F32), "wdt1": ((2048, 64), F32), "bdt1": ((128, 16), F32),
        "nw": ((128, 8), F32),
        "idty": ((128, 128), BF16), "idtf": ((128, 128), F32),
    }
    out_specs = {
        "x4": ((TL, DM), F32), "u1": ((2048, TL), F32), "delta1": ((2048, TL), F32),
        "bc1": ((128, TL), F32), "z1": ((2048, TL), BF16), "ln5": ((TL, DM), F32),
    }
    nc = mkprog(build_FG(), in_specs, out_specs)
    x3_T = np.concatenate([o["x3"][:, :SL] for o in d_outs] +
                          [o["x3"][:, SL:] for o in d_outs], axis=1)
    x3_nat = np.ascontiguousarray(x3_T.T)
    nT_nat = [np.ascontiguousarray(o["nT"].T) for o in e_outs]
    m2_flat = np.stack([o["mx2"].T.reshape(-1) for o in e_outs])
    d2_flat = np.stack([o["d2"].T.reshape(-1) for o in e_outs])
    shared = {
        "wint": prep["wint"],
        "cvw": inputs["L1_attn_conv_w"].astype(np.float32),
        "cvb": _colscale(inputs["L1_attn_conv_b"]),
        "wx": inputs["L1_attn_W_x"].astype(np.float32),
        "wdt1": inputs["L1_attn_W_dt"].astype(np.float32),
        "bdt1": _colscale(inputs["L1_attn_b_dt"]),
        "nw": _colscale(inputs["L1_attn_norm_w"]),
        "idty": IDTY_BF, "idtf": IDTY,
    }
    in_maps = []
    for c in range(NCORE):
        nTs = np.stack([_padded_rows(nT_nat[k], c) for k in range(NCORE)])
        m2p = np.stack([_padded_rows(m2_flat[k][:, None], c)[:, 0] for k in range(NCORE)])
        d2p = np.stack([_padded_rows(d2_flat[k][:, None], c, const=1.0)[:, 0]
                        for k in range(NCORE)])
        m2s = np.ascontiguousarray(m2p.reshape(8, NTS, 128).transpose(2, 1, 0).reshape(128, NTS * 8))
        d2s = np.ascontiguousarray(d2p.reshape(8, NTS, 128).transpose(2, 1, 0).reshape(128, NTS * 8))
        in_maps.append(dict(shared, x3n=_padded_rows(x3_nat, c), nTs=nTs,
                            m2s=m2s.astype(np.float32), d2s=d2s.astype(np.float32)))
    res = run_bass_kernel_spmd(nc, in_maps, core_ids=list(range(NCORE)))
    return [res.results[c] for c in range(NCORE)]



def build_H():
    TC = 1024
    NOC = (NB * SEQ) // TC

    def bld(nc, P, ins, outs, es):
        sel = sbuf(nc, es, "s_sel64", [128, 64 * 128], F32)
        P.load(sel[:], ins["sel64"], writes=[("sel",)])
        idtb = sbuf(nc, es, "s_idtb", [128, 128], BF16)
        P.load(idtb[:], ins["idty"], writes=[("idtb",)])
        alog = sbuf(nc, es, "s_alog1", [128, 2 * 64], F32)
        P.load(alog[:].rearrange("p (a n) -> p a n", a=2),
               ins["alog1"].rearrange("(a p) n -> p a n", p=128),
               writes=[("alog",)], name="ldal")
        dpsl = sbuf(nc, es, "s_dpsl", [128, 2], F32)
        P.load(dpsl[:], ins["dpsl"], writes=[("dpsl",)])
        A = sbuf(nc, es, "s_A1", [128, 2 * 64], F32)
        P.step("scalar", lambda e: e.activation(out=A[:], in_=alog[:], func=AF.Exp),
               reads=[("alog",)], writes=[("A",)], name="expA")
        P.step("vector", lambda e: e.tensor_scalar_mul(A[:], A[:], -1.0),
               reads=[("A",)], writes=[("A",)], name="negA")
        carry = [sbuf(nc, es, f"carry{i}", [128, 64], F32) for i in range(2)]
        zcol = sbuf(nc, es, "zcol", [128, 1], F32)
        P.step("gpsimd", lambda e: e.memset(zcol[:], 0.0), writes=[("zcol",)], name="z")
        uc = [[sbuf(nc, es, f"uc{i}_{j}", [128, TC], F32) for i in range(2)] for j in range(2)]
        dc = [[sbuf(nc, es, f"dc{i}_{j}", [128, TC], F32) for i in range(2)] for j in range(2)]
        bcc = [sbuf(nc, es, f"bcc{i}", [128, TC], F32) for i in range(2)]
        wcb = [sbuf(nc, es, f"wcb{j}", [128, TC], BF16) for j in range(2)]
        Bbs = [sbuf(nc, es, f"Bbs{i}", [128, TC], BF16) for i in range(3)]
        Cbs = [sbuf(nc, es, f"Cbs{i}", [128, TC], BF16) for i in range(3)]
        dac = [sbuf(nc, es, f"dac{i}", [128, TC], BF16) for i in range(4)]
        bbc = [sbuf(nc, es, f"bbc{i}", [128, TC], BF16) for i in range(4)]
        hcb = [sbuf(nc, es, f"hcb{i}", [128, TC], BF16) for i in range(4)]
        ymb = [sbuf(nc, es, f"ymb{i}", [128, TC], BF16) for i in range(4)]
        yac = sbuf(nc, es, "yac", [128, TC], F32)
        ymf = sbuf(nc, es, "ymf", [128, TC], F32)
        pbb = psum(nc, es, "pbb", [128, TC], F32)
        pbc = psum(nc, es, "pbc", [128, TC], F32)
        pyy = [psum(nc, es, f"pyy{j}", [128, TC], F32) for j in range(2)]

        for oc in range(NOC):
            s = oc % 2
            cs = slice(oc * TC, (oc + 1) * TC)
            for j in range(2):
                P.load(uc[s][j][:], ins["u1f"][j * 128:(j + 1) * 128, cs],
                       writes=[("uc", s, j)], name=f"ldu{oc}_{j}")
                P.load(dc[s][j][:], ins["d1f"][j * 128:(j + 1) * 128, cs],
                       writes=[("dc", s, j)], name=f"ldd{oc}_{j}")
                P.step("vector", lambda e, s=s, j=j: e.tensor_mul(
                          wcb[j][:], dc[s][j][:], uc[s][j][:]),
                       reads=[("dc", s, j), ("uc", s, j)], writes=[("wcb", j)],
                       name=f"w{oc}_{j}")
            P.load(bcc[s][:], ins["bc1f"][:, cs], writes=[("bcc", s)], name=f"ldbc{oc}")

            def emit_bcast(n):
                p = n % 3
                def mmb(e, s=s, n=n):
                    e.matmul(pbb[:, 0:512], sel[0:64, n * 128:(n + 1) * 128],
                             bcc[s][0:64, 0:512], start=True, stop=True)
                    return e.matmul(pbb[:, 512:1024], sel[0:64, n * 128:(n + 1) * 128],
                                    bcc[s][0:64, 512:1024], start=True, stop=True)
                P.step("tensor", mmb, reads=[("sel",), ("bcc", s)], writes=[("psB",)],
                       name=f"bb{oc}_{n}")
                P.step("scalar", lambda e, p=p: e.copy(out=Bbs[p][:], in_=pbb[:]),
                       reads=[("psB",)], writes=[("Bbs", p)], name=f"be{oc}_{n}")
                def mmc(e, s=s, n=n):
                    e.matmul(pbc[:, 0:512], sel[64:128, n * 128:(n + 1) * 128],
                             bcc[s][64:128, 0:512], start=True, stop=True)
                    return e.matmul(pbc[:, 512:1024], sel[64:128, n * 128:(n + 1) * 128],
                                    bcc[s][64:128, 512:1024], start=True, stop=True)
                P.step("tensor", mmc, reads=[("sel",), ("bcc", s)], writes=[("psC",)],
                       name=f"cb{oc}_{n}")
                P.step("scalar", lambda e, p=p: e.copy(out=Cbs[p][:], in_=pbc[:]),
                       reads=[("psC",)], writes=[("Cbs", p)], name=f"ce{oc}_{n}")
                for j in range(2):
                    q = (2 * n + j) % 4
                    P.step("scalar", lambda e, s=s, n=n, j=j, q=q: e.activation(
                              out=dac[q][:], in_=dc[s][j][:], func=AF.Exp,
                              scale=A[:, j * 64 + n:j * 64 + n + 1]),
                           reads=[("dc", s, j), ("A",)], writes=[("dac", q)],
                           name=f"da{oc}_{n}_{j}")

            emit_bcast(0)
            for n in range(64):
                p = n % 3
                if n + 1 < 64:
                    emit_bcast(n + 1)
                for j in range(2):
                    q = (2 * n + j) % 4
                    P.step("vector", lambda e, j=j, p=p, q=q: e.tensor_mul(
                              bbc[q][:], wcb[j][:], Bbs[p][:]),
                           reads=[("wcb", j), ("Bbs", p)], writes=[("bbc", q)],
                           name=f"bB{oc}_{n}_{j}")
                    if oc % (NOC // 2) == 0:
                        init, ikey = zcol[:], ("zcol",)
                    else:
                        init, ikey = carry[j][:, n:n + 1], ("carry", j, n)
                    P.step("vector", lambda e, init=init, q=q: e.tensor_tensor_scan(
                              out=hcb[q][:], data0=dac[q][:], data1=bbc[q][:], initial=init,
                              op0=ALU.mult, op1=ALU.add),
                           reads=[("dac", q), ("bbc", q), ikey], writes=[("hcb", q)],
                           name=f"h{oc}_{n}_{j}")
                    P.step("vector", lambda e, j=j, n=n, q=q: e.tensor_copy(
                              out=carry[j][:, n:n + 1], in_=hcb[q][:, TC - 1:TC]),
                           reads=[("hcb", q)], writes=[("carry", j, n)], name=f"cy{oc}_{n}_{j}")
                    P.step("vector", lambda e, p=p, q=q: e.tensor_mul(
                              ymb[q][:], hcb[q][:], Cbs[p][:]),
                           reads=[("hcb", q), ("Cbs", p)], writes=[("ymb", q)],
                           name=f"ym{oc}_{n}_{j}")
                    def mmy(e, j=j, q=q, n=n):
                        e.matmul(pyy[j][:, 0:512], idtb[:], ymb[q][:, 0:512],
                                 start=(n == 0), stop=(n == 63))
                        return e.matmul(pyy[j][:, 512:1024], idtb[:], ymb[q][:, 512:1024],
                                        start=(n == 0), stop=(n == 63))
                    P.step("tensor", mmy, reads=[("ymb", q), ("idtb",)],
                           writes=[("psY", j)] if n in (0, 63) else [],
                           name=f"ya{oc}_{n}_{j}")
            for j in range(2):
                P.step("scalar", lambda e, j=j: e.copy(out=yac[:], in_=pyy[j][:]),
                       reads=[("psY", j)], writes=[("yac",)], name=f"ye{oc}_{j}")
                P.step("vector", lambda e, s=s, j=j: e.tensor_scalar_mul(
                          ymf[:], uc[s][j][:], dpsl[:, j:j + 1]),
                       reads=[("uc", s, j), ("dpsl",)], writes=[("ymf",)], name=f"ud{oc}_{j}")
                P.step("vector", lambda e, j=j: e.tensor_add(yac[:], yac[:], ymf[:]),
                       reads=[("ymf",), ("yac",)], writes=[("yac",)], name=f"yd{oc}_{j}")
                P.step("gpsimd", lambda e, j=j, cs=cs: e.dma_start(
                          out=outs["y1"][j * 128:(j + 1) * 128, cs], in_=yac[:]),
                       reads=[("yac",)], writes=[("st", j, oc % 2)], dma=True,
                       name=f"st{oc}_{j}")
        P.step("sync", None, reads=[("st", j, k) for j in range(2) for k in range(2)],
               name="fin")
    return bld


SEL64 = np.zeros((128, 64 * 128), np.float32)
for _n in range(64):
    SEL64[_n, _n * 128:(_n + 1) * 128] = 1.0
    SEL64[64 + _n, _n * 128:(_n + 1) * 128] = 1.0


def run_H(fg_outs, inputs):
    TT_ = NB * SEQ
    in_specs = {
        "u1f": ((256, TT_), F32), "d1f": ((256, TT_), F32), "bc1f": ((128, TT_), F32),
        "alog1": ((256, 64), F32), "dpsl": ((128, 2), F32),
        "sel64": ((128, 64 * 128), F32), "idty": ((128, 128), BF16),
    }
    out_specs = {"y1": ((256, TT_), F32)}
    nc = mkprog(build_H(), in_specs, out_specs)
    ufull = np.concatenate([o["u1"][:, :SL] for o in fg_outs] +
                           [o["u1"][:, SL:] for o in fg_outs], axis=1)
    dfull = np.concatenate([o["delta1"][:, :SL] for o in fg_outs] +
                           [o["delta1"][:, SL:] for o in fg_outs], axis=1)
    bcfull = np.concatenate([o["bc1"][:, :SL] for o in fg_outs] +
                            [o["bc1"][:, SL:] for o in fg_outs], axis=1)
    alog = inputs["L1_attn_A_log"].astype(np.float32)
    dp = inputs["L1_attn_Dp"].astype(np.float32)
    in_maps = []
    for c in range(NCORE):
        rs = slice(c * 256, (c + 1) * 256)
        dpc = dp[rs].reshape(2, 128).T
        in_maps.append({
            "u1f": np.ascontiguousarray(ufull[rs]),
            "d1f": np.ascontiguousarray(dfull[rs]),
            "bc1f": bcfull, "alog1": np.ascontiguousarray(alog[rs]),
            "dpsl": np.ascontiguousarray(dpc), "sel64": SEL64, "idty": IDTY_BF,
        })
    res = run_bass_kernel_spmd(nc, in_maps, core_ids=list(range(NCORE)))
    yfull = np.concatenate([res.results[c]["y1"] for c in range(NCORE)], axis=0)
    y_tok = []
    for c in range(NCORE):
        y_tok.append(np.ascontiguousarray(np.concatenate(
            [yfull[:, c * SL:(c + 1) * SL],
             yfull[:, SEQ + c * SL:SEQ + (c + 1) * SL]], axis=1)))
    return y_tok



def build_I():
    TKC = 512
    NCH = TL // TKC

    def bld(nc, P, ins, outs, es):
        sc = {}
        sc["ones_col"] = sbuf(nc, es, "s_ones_col", [128, 1], F32)
        sc["ones_row"] = sbuf(nc, es, "s_ones_row", [1, 128], F32)
        P.load(sc["ones_col"][:], ins["ones_col"], writes=[("ones_col",)])
        P.load(sc["ones_row"][:], ins["ones_row"], writes=[("ones_row",)])
        gam = sbuf(nc, es, "s_gam", [128, 8], F32)
        P.load(gam[:], ins["gam"], writes=[("gam",)])
        ceps = sbuf(nc, es, "s_ceps", [128, 1], F32)
        P.step("gpsimd", lambda e: e.memset(ceps[:], EPS_LN), writes=[("ceps",)], name="ceps")
        r32 = sbuf(nc, es, "s_r32", [1, 128], F32)
        P.step("gpsimd", lambda e: e.memset(r32[:], float(np.sqrt(DM))),
               writes=[("r32",)], name="c32")
        pb = [psum(nc, es, f"pb{i}", [128, 512], F32) for i in range(8)]
        XR = [sbuf(nc, es, f"XR{i}", [128, TL], F32) for i in range(8)]
        for i in range(8):
            P.load(XR[i][:], ins["x4_T"].rearrange("(a p) t -> a p t", p=128)[i],
                   writes=[("XR", i)], name=f"ldx{i}")
        g = [sbuf(nc, es, f"g{i}", [128, TL], BF16) for i in range(16)]
        zt = [sbuf(nc, es, f"zt{i}", [128, TL], BF16) for i in range(1)]
        yt = [sbuf(nc, es, f"yt{i}", [128, TL], F32) for i in range(1)]
        for i in range(8):
            P.load(yt[0][:], ins["ln5_T"].rearrange("(a p) t -> a p t", p=128)[i],
                   writes=[("yt", 0)], name=f"ldl5{i}")
            P.step("vector", lambda e, i=i: e.tensor_add(XR[i][:], XR[i][:], yt[0][:]),
                   reads=[("yt", 0), ("XR", i)], writes=[("XR", i)], name=f"al5{i}")
        szb = sbuf(nc, es, "szb", [128, TL], BF16)
        for kt in range(16):
            s = 0
            P.load(zt[s][:], ins["z_T"][kt * 128:(kt + 1) * 128, :],
                   writes=[("zt", s)], name=f"ldz{kt}")
            P.load(yt[s][:], ins["y_T"][kt * 128:(kt + 1) * 128, :],
                   writes=[("yt", s)], name=f"ldy{kt}")
            P.step("scalar", lambda e, s=s: e.activation(out=szb[:], in_=zt[s][:], func=AF.Silu),
                   reads=[("zt", s)], writes=[("szb",)], name=f"sz{kt}")
            P.step("vector", lambda e, s=s, kt=kt: e.tensor_mul(g[kt][:], yt[s][:], szb[:]),
                   reads=[("yt", s), ("szb",)], writes=[("g", kt)], name=f"g{kt}")
        wsl = [sbuf(nc, es, f"wsl{i}", [128, 2048], BF16) for i in range(2)]
        tmpo = sbuf(nc, es, "s_tmpo", [128, 512], F32)
        for mo in range(8):
            s = mo % 2
            P.load(wsl[s][:], ins["woutt"][mo], writes=[("wsl", s)], name=f"lw{mo}")
            for ci in range(NCH):
                c0 = ci * TKC
                bank = ci % 2
                def mmo(e, s=s, bank=bank, c0=c0):
                    for kt in range(16):
                        i = e.matmul(pb[bank][:], wsl[s][:, kt * 128:(kt + 1) * 128],
                                     g[kt][:, c0:c0 + TKC], start=(kt == 0), stop=(kt == 15))
                    return i
                P.step("tensor", mmo,
                       reads=[("wsl", s)] + [("g", kt) for kt in range(16)],
                       writes=[("pbank", bank)], name=f"om{mo}_{ci}")
                P.step("scalar", lambda e, bank=bank: e.copy(out=tmpo[:], in_=pb[bank][:]),
                       reads=[("pbank", bank)], writes=[("tmpo",)], name=f"oe{mo}_{ci}")
                P.step("vector", lambda e, mo=mo, c0=c0: e.tensor_add(
                          XR[mo][:, c0:c0 + TKC], XR[mo][:, c0:c0 + TKC], tmpo[:]),
                       reads=[("tmpo",), ("XR", mo)], writes=[("XR", mo)],
                       name=f"ox{mo}_{ci}")
        for i in range(8):
            P.step("gpsimd", lambda e, i=i: e.dma_start(
                      out=outs["x5"][i * 128:(i + 1) * 128, :], in_=XR[i][:]),
                   reads=[("XR", i)], writes=[("st_x5", i)], dma=True, name=f"sx5{i}")
        xn = [sbuf(nc, es, f"xn{i}", [128, TL], BF16) for i in range(8)]
        sq = sbuf(nc, es, "s_sq", [128, 512], F32)
        ap1 = sbuf(nc, es, "s_ap1", [128, 512], F32)
        sa = sbuf(nc, es, "s_sa", [128, 512], BF16)
        for nm in ("mrow", "wrow", "rrow", "mbrow"):
            sc[nm] = sbuf(nc, es, "s_" + nm, [1, 512], F32)
        for ch in range(NCH):
            cs = slice(ch * TKC, (ch + 1) * TKC)
            for dt in range(8):
                P.step("scalar", lambda e, dt=dt, cs=cs: e.square(out=sq[:], in_=XR[dt][:, cs]),
                       reads=[("XR", dt)], writes=[("sq",)], name=f"l6q{ch}_{dt}")
                P.step("tensor", lambda e, dt=dt, cs=cs: e.matmul(
                          pb[0][0:1, :], sc["ones_col"][:], XR[dt][:, cs],
                          start=(dt == 0), stop=(dt == 7)),
                       reads=[("XR", dt), ("ones_col",)], writes=[("pbank", 0)],
                       name=f"l6x{ch}_{dt}")
                P.step("tensor", lambda e, dt=dt: e.matmul(
                          pb[1][0:1, :], sc["ones_col"][:], sq[:],
                          start=(dt == 0), stop=(dt == 7)),
                       reads=[("sq",), ("ones_col",)], writes=[("pbank", 1)],
                       name=f"l6s{ch}_{dt}")
            P.step("vector", lambda e: e.tensor_scalar_mul(sc["mrow"][:], pb[0][0:1, :], 1.0 / 1024),
                   reads=[("pbank", 0)], writes=[("mrow",)], name=f"l6m{ch}")
            P.step("vector", lambda e: e.tensor_scalar_mul(sc["wrow"][:], pb[1][0:1, :], 1.0 / 1024),
                   reads=[("pbank", 1)], writes=[("wrow",)], name=f"l6e{ch}")
            P.step("vector", lambda e: e.tensor_mul(sc["mbrow"][:], sc["mrow"][:], sc["mrow"][:]),
                   reads=[("mrow",)], writes=[("mbrow",)], name=f"l6m2{ch}")
            P.step("vector", lambda e: e.tensor_sub(sc["wrow"][:], sc["wrow"][:], sc["mbrow"][:]),
                   reads=[("mbrow",), ("wrow",)], writes=[("wrow",)], name=f"l6v{ch}")
            P.step("scalar", lambda e: e.activation(out=sc["wrow"][:], in_=sc["wrow"][:],
                                                    func=AF.Ln, bias=ceps[0:1, 0:1]),
                   reads=[("wrow",), ("ceps",)], writes=[("wrow",)], name=f"l6l{ch}")
            P.step("scalar", lambda e: e.activation(out=sc["rrow"][:], in_=sc["wrow"][:],
                                                    func=AF.Exp, scale=-0.5),
                   reads=[("wrow",)], writes=[("rrow",)], name=f"l6r{ch}")
            P.step("vector", lambda e: e.scalar_tensor_tensor(
                      out=sc["mbrow"][:], in0=sc["mrow"][:], scalar=-1.0, in1=sc["rrow"][:],
                      op0=ALU.mult, op1=ALU.mult),
                   reads=[("mrow",), ("rrow",)], writes=[("mbrow",)], name=f"l6mb{ch}")
            P.step("tensor", lambda e: e.matmul(pb[2][:], sc["ones_row"][:], sc["rrow"][:],
                                                start=True, stop=True),
                   reads=[("rrow",), ("ones_row",)], writes=[("pbank", 2)], name=f"l6br{ch}")
            P.step("tensor", lambda e: e.matmul(pb[3][:], sc["ones_row"][:], sc["mbrow"][:],
                                                start=True, stop=True),
                   reads=[("mbrow",), ("ones_row",)], writes=[("pbank", 3)], name=f"l6bm{ch}")
            for dt in range(8):
                P.step("vector", lambda e, dt=dt, cs=cs: e.tensor_mul(
                          ap1[:], XR[dt][:, cs], pb[2][:]),
                       reads=[("XR", dt), ("pbank", 2)], writes=[("ap1",)],
                       name=f"l6a1_{ch}_{dt}")
                P.step("vector", lambda e, dt=dt, cs=cs: e.tensor_add(
                          xn[dt][:, cs], ap1[:], pb[3][:]),
                       reads=[("ap1",), ("pbank", 3)], writes=[("xn", dt, cs.start)],
                       name=f"l6a2_{ch}_{dt}")
        for ch in range(NCH):
            cs = slice(ch * TKC, (ch + 1) * TKC)
            for dt in range(8):
                P.step("scalar", lambda e, dt=dt, cs=cs: e.square(out=sq[:], in_=xn[dt][:, cs]),
                       reads=[("xn", dt, cs.start)], writes=[("sq",)], name=f"q6s{ch}_{dt}")
                P.step("tensor", lambda e, dt=dt: e.matmul(
                          pb[0][0:1, :], sc["ones_col"][:], sq[:],
                          start=(dt == 0), stop=(dt == 7)),
                       reads=[("sq",), ("ones_col",)], writes=[("pbank", 0)],
                       name=f"q6m{ch}_{dt}")
            P.step("scalar", lambda e: e.activation(out=sc["wrow"][:], in_=pb[0][0:1, :],
                                                    func=AF.Ln),
                   reads=[("pbank", 0)], writes=[("wrow",)], name=f"q6l{ch}")
            P.step("scalar", lambda e: e.activation(out=sc["rrow"][:], in_=sc["wrow"][:],
                                                    func=AF.Exp, scale=-0.5),
                   reads=[("wrow",)], writes=[("rrow",)], name=f"q6r{ch}")
            P.step("tensor", lambda e: e.matmul(pb[2][:], r32[:], sc["rrow"][:],
                                                start=True, stop=True),
                   reads=[("rrow",), ("r32",)], writes=[("pbank", 2)], name=f"q6b{ch}")
            for dt in range(8):
                P.step("vector", lambda e, dt=dt, cs=cs: e.scalar_tensor_tensor(
                          out=sa[:], in0=xn[dt][:, cs], scalar=gam[:, dt:dt + 1],
                          in1=pb[2][:], op0=ALU.mult, op1=ALU.mult),
                       reads=[("xn", dt, cs.start), ("gam",), ("pbank", 2)],
                       writes=[("sa",)], name=f"q6x{ch}_{dt}")
                P.step("gpsimd", lambda e, dt=dt, ch=ch: e.dma_start(
                          out=outs["xq2_T"][dt * 128:(dt + 1) * 128, ch * TKC:(ch + 1) * TKC],
                          in_=sa[:]),
                       reads=[("sa",)], writes=[("st_q", dt, ch)], dma=True,
                       name=f"sq6{ch}_{dt}")
        P.step("sync", None,
               reads=[("st_x5", i) for i in range(8)] +
                     [("st_q", dt, ch) for dt in range(8) for ch in range(NCH)],
               name="fin")
    return bld


def run_I(inputs, prep, fg_outs, y1_tok):
    in_specs = {
        "x4_T": ((DM, TL), F32), "ln5_T": ((DM, TL), F32),
        "y_T": ((2048, TL), F32), "z_T": ((2048, TL), BF16),
        "woutt": ((8, 128, 2048), BF16), "gam": ((128, 8), F32),
        "ones_col": ((128, 1), F32), "ones_row": ((1, 128), F32),
    }
    out_specs = {"x5": ((DM, TL), F32), "xq2_T": ((DM, TL), BF16)}
    nc = mkprog(build_I(), in_specs, out_specs)
    shared = {"woutt": prep["woutt"], "gam": _colscale(inputs["L1_ffn_norm_gamma"]),
              "ones_col": ONES_COL, "ones_row": ONES_ROW}
    in_maps = [dict(shared,
                    x4_T=np.ascontiguousarray(fg_outs[c]["x4"].T),
                    ln5_T=np.ascontiguousarray(fg_outs[c]["ln5"].T),
                    y_T=y1_tok[c], z_T=fg_outs[c]["z1"]) for c in range(NCORE)]
    res = run_bass_kernel_spmd(nc, in_maps, core_ids=list(range(NCORE)))
    return [res.results[c] for c in range(NCORE)]


def build_K():
    def bld(nc, P, ins, outs, es):
        m2s = sbuf(nc, es, "s_m2s", [128, 16 * 8], F32)
        P.load(m2s[:], ins["m2s"], writes=[("m2s",)])
        d2s = sbuf(nc, es, "s_d2s", [128, 16 * 8], F32)
        P.load(d2s[:], ins["d2s"], writes=[("d2s",)])
        x5n = [sbuf(nc, es, f"s_x5n{i}", [128, 1024], F32) for i in range(2)]
        ntile = [sbuf(nc, es, f"s_ntile{i}", [128, 1024], BF16) for i in range(4)]
        acc = [sbuf(nc, es, f"s_acc{i}", [128, 1024], F32) for i in range(2)]
        srow8 = [sbuf(nc, es, f"s_srow8{i}", [128, 8], F32) for i in range(2)]
        sd8 = [sbuf(nc, es, f"s_sd8{i}", [128, 8], F32) for i in range(2)]
        mcol = [sbuf(nc, es, f"s_mcol{i}", [128, 1], F32) for i in range(2)]
        dcol = [sbuf(nc, es, f"s_dcol{i}", [128, 1], F32) for i in range(2)]
        for ts in range(16):
            p = ts % 2
            P.load(x5n[p][:], ins["x5n"].rearrange("(c p) d -> c p d", p=128)[ts],
                   writes=[("x5n", p)], name=f"ldx{ts}")
            P.step("vector", lambda e, ts=ts, p=p: e.reduce_max(
                      mcol[p][:], m2s[:, ts * 8:(ts + 1) * 8], axis=mybir.AxisListType.X),
                   reads=[("m2s",)], writes=[("mcol", p)], name=f"kM{ts}")
            P.step("vector", lambda e, p=p: e.tensor_scalar_mul(mcol[p][:], mcol[p][:], -1.0),
                   reads=[("mcol", p)], writes=[("mcol", p)], name=f"kMn{ts}")
            P.step("scalar", lambda e, ts=ts, p=p: e.activation(
                      out=srow8[p][:], in_=m2s[:, ts * 8:(ts + 1) * 8], func=AF.Exp,
                      bias=mcol[p][:, 0:1]),
                   reads=[("m2s",), ("mcol", p)], writes=[("srow8", p)], name=f"kS{ts}")
            P.step("vector", lambda e, ts=ts, p=p: e.tensor_mul(
                      sd8[p][:], srow8[p][:], d2s[:, ts * 8:(ts + 1) * 8]),
                   reads=[("srow8", p), ("d2s",)], writes=[("sd8", p)], name=f"kSD{ts}")
            P.step("vector", lambda e, p=p: e.reduce_sum(
                      dcol[p][:], sd8[p][:], axis=mybir.AxisListType.X),
                   reads=[("sd8", p)], writes=[("dcol", p)], name=f"kden{ts}")
            P.step("vector", lambda e, p=p: e.reciprocal(dcol[p][:], dcol[p][:]),
                   reads=[("dcol", p)], writes=[("dcol", p)], name=f"krc{ts}")
            for c in range(NCORE):
                nr = (ts * 8 + c) % 4
                P.load(ntile[nr][:], ins["nTs"].rearrange("e (c p) d -> e c p d", p=128)[c, ts],
                       writes=[("ntile", nr)], name=f"kn{ts}_{c}")
                if c == 0:
                    P.step("vector", lambda e, c=c, p=p, nr=nr: e.tensor_scalar_mul(
                              acc[p][:], ntile[nr][:], srow8[p][:, c:c + 1]),
                           reads=[("ntile", nr), ("srow8", p)], writes=[("acc", p)],
                           name=f"ka{ts}_{c}")
                else:
                    P.step("vector", lambda e, c=c, p=p, nr=nr: e.scalar_tensor_tensor(
                              out=acc[p][:], in0=ntile[nr][:], scalar=srow8[p][:, c:c + 1],
                              in1=acc[p][:], op0=ALU.mult, op1=ALU.add),
                           reads=[("ntile", nr), ("srow8", p), ("acc", p)], writes=[("acc", p)],
                           name=f"ka{ts}_{c}")
            P.step("vector", lambda e, p=p: e.tensor_scalar(
                      out=acc[p][:], in0=acc[p][:], scalar1=dcol[p][:, 0:1], scalar2=0.5,
                      op0=ALU.mult, op1=ALU.mult),
                   reads=[("acc", p), ("dcol", p)], writes=[("acc", p)], name=f"kmx{ts}")
            P.step("vector", lambda e, p=p: e.tensor_add(x5n[p][:], x5n[p][:], acc[p][:]),
                   reads=[("acc", p), ("x5n", p)], writes=[("x5n", p)], name=f"kx6{ts}")
            P.step("gpsimd", lambda e, ts=ts, p=p: e.dma_start(
                      out=outs["x6"].rearrange("(c p) d -> c p d", p=128)[ts], in_=x5n[p][:]),
                   reads=[("x5n", p)], writes=[("st", ts % 4)], dma=True, name=f"ks{ts}")
        P.step("sync", None, reads=[("st", k) for k in range(4)], name="fin")
    return bld


def run_K(i_outs, j_outs):
    in_specs = {
        "x5n": ((TL, DM), F32), "nTs": ((NCORE, TL, DM), BF16),
        "m2s": ((128, 16 * 8), F32), "d2s": ((128, 16 * 8), F32),
    }
    out_specs = {"x6": ((TL, DM), F32)}
    nc = mkprog(build_K(), in_specs, out_specs)
    nT_nat = [np.ascontiguousarray(o["nT"].T) for o in j_outs]
    m2_flat = np.stack([o["mx2"].T.reshape(-1) for o in j_outs])
    d2_flat = np.stack([o["d2"].T.reshape(-1) for o in j_outs])
    in_maps = []
    for c in range(NCORE):
        tok_idx = np.r_[c * SL:(c + 1) * SL, SEQ + c * SL:SEQ + (c + 1) * SL]
        nTs = np.stack([nT_nat[k][tok_idx] for k in range(NCORE)])
        m2p = m2_flat[:, tok_idx]
        d2p = d2_flat[:, tok_idx]
        m2s = np.ascontiguousarray(m2p.reshape(8, 16, 128).transpose(2, 1, 0).reshape(128, 128))
        d2s = np.ascontiguousarray(d2p.reshape(8, 16, 128).transpose(2, 1, 0).reshape(128, 128))
        in_maps.append({"x5n": np.ascontiguousarray(i_outs[c]["x5"].T), "nTs": nTs,
                        "m2s": m2s.astype(np.float32), "d2s": d2s.astype(np.float32)})
    res = run_bass_kernel_spmd(nc, in_maps, core_ids=list(range(NCORE)))
    return [res.results[c]["x6"] for c in range(NCORE)]


def kernel(**inputs):
    inputs = {k: np.asarray(v) for k, v in inputs.items()}
    prep = run_A(inputs)
    b_outs = run_B(inputs, prep)
    y0_tok = run_C(b_outs, inputs)
    d_outs = run_D(inputs, prep, b_outs, y0_tok)
    xq_full = assemble_xq_full(d_outs)
    e_outs = run_E(inputs, xq_full, "L1_ffn")
    fg_outs = run_FG(inputs, prep, d_outs, e_outs)
    y1_tok = run_H(fg_outs, inputs)
    i_outs = run_I(inputs, prep, fg_outs, y1_tok)
    xq2_full = assemble_xq_full(i_outs, key="xq2_T")
    j_outs = run_E(inputs, xq2_full, "L1_ffn")
    x6 = run_K(i_outs, j_outs)
    return _untoks(x6)
